# version 01
# baseline (speedup 1.0000x reference)
import contextlib
import numpy as np
import ml_dtypes
import concourse.bass as bass
import concourse.mybir as mybir
from concourse.bass_utils import run_bass_kernel_spmd

F32 = mybir.dt.float32
BF16 = mybir.dt.bfloat16
AF = mybir.ActivationFunctionType
ALU = mybir.AluOpType
AX = mybir.AxisListType

D = 1024
CD = D // 128
LN_EPS = 1e-5



class P:

    def __init__(self, nc, stack):
        self.nc = nc
        self.stack = stack
        self.block = None
        self._sems = {}
        self.cnt = {}
        self.uid = 0

    def fresh(self, stack):
        q = P.__new__(P)
        q.__dict__.update(self.__dict__)
        q.stack = stack
        return q

    def sem(self, name):
        if name not in self._sems:
            self._sems[name] = self.stack.enter_context(self.nc.semaphore(name))
            self.cnt[name] = 0
        return self._sems[name]

    def inc(self, inst, name, n=1):
        inst.then_inc(self.sem(name), n)
        self.cnt[name] += n
        return self.cnt[name]

    def val(self, name):
        self.sem(name)
        return self.cnt[name]

    def wait(self, eng, name, v=None):
        eng.wait_ge(self.sem(name), self.cnt[name] if v is None else v)

    def fence(self, tag="ph"):
        name = "fence"
        self.sem(name)
        target = self.cnt[name] + 3

        @self.block.tensor
        def _(te):
            self.inc(te.drain(), name, 1)
            te.wait_ge(self._sems[name], target)

        @self.block.vector
        def _(ve):
            self.inc(ve.drain(), name, 1)
            ve.wait_ge(self._sems[name], target)

        @self.block.scalar
        def _(se):
            self.inc(se.drain(), name, 1)
            se.wait_ge(self._sems[name], target)
        assert self.cnt[name] == target


class Bank:

    def __init__(self, p, tensor, name):
        self.p = p
        self.t = tensor
        self.name = "bk_" + name
        self.uses = 0
        p.sem(self.name)

    def wait_free(self, eng):
        if self.uses > 0:
            eng.wait_ge(self.p.sem(self.name), self.uses)
        self.uses += 1

    def release(self, inst):
        self.p.inc(inst, self.name, 1)

    def wait_released(self, eng):
        eng.wait_ge(self.p.sem(self.name), self.uses)


def sbuf(p, name, shape, dtype):
    p.uid += 1
    return p.stack.enter_context(p.nc.sbuf_tensor(f"{name}_{p.uid}", shape, dtype))


def psum_bank(p, name, shape, dtype=F32):
    p.uid += 1
    t = p.stack.enter_context(p.nc.psum_tensor(f"{name}_{p.uid}", shape, dtype))
    return Bank(p, t, f"{name}_{p.uid}")



class LN:

    def __init__(self, p, tag, eps_ap):
        self.p = p
        self.eps = eps_ap
        self.stats = sbuf(p, f"lnst{tag}", [128, 12], F32)
        self.mv = sbuf(p, f"lnmv{tag}", [128, 2], F32)
        self.lnv = sbuf(p, f"lnlv{tag}", [128, 1], F32)
        self.rstd = sbuf(p, f"lnrs{tag}", [128, 1], F32)
        self.s_v2s = f"lnv2s{tag}"
        self.s_s2v = f"lns2v{tag}"
        self.rdy = f"lnrdy{tag}"
        p.sem(self.rdy)

    def emit(self, x_view, out_view):
        p = self.p

        @p.block.vector
        def _(ve):
            if p.val(self.s_s2v) > 0:
                p.wait(ve, self.s_s2v)
            ve.bn_stats(self.stats[:, 0:6], x_view[:, 0:512])
            ve.bn_stats(self.stats[:, 6:12], x_view[:, 512:1024])
            ve.drain()
            p.inc(ve.bn_aggr(self.mv[:], self.stats[:]), self.s_v2s, 1)

        @p.block.scalar
        def _(se):
            p.wait(se, self.s_v2s)
            se.activation(self.lnv[:], self.mv[:, 1:2], AF.Ln, bias=self.eps)
            se.drain()
            p.inc(se.activation(self.rstd[:], self.lnv[:], AF.Exp, scale=-0.5),
                  self.s_s2v, 1)

        @p.block.vector
        def _(ve):
            p.wait(ve, self.s_s2v)
            p.inc(ve.tensor_scalar(out_view, x_view, self.mv[:, 0:1], self.rstd[:, 0:1],
                                   ALU.subtract, ALU.mult), self.rdy, 1)



def emit_ffn0(p, x_sb, wr_d, wk_d, wv_d, mur_d, muk_d, consts, n_tok, tag):
    NTILE = n_tok // 512
    ident = consts["ident"]

    with contextlib.ExitStack() as st:
        pp = p.fresh(st)

        wr = sbuf(pp, f"fwr{tag}", [128, CD * 1024], BF16)
        wk = sbuf(pp, f"fwk{tag}", [128, CD * 1024], BF16)
        wv = sbuf(pp, f"fwv{tag}", [128, CD * 1024], BF16)
        mur = sbuf(pp, f"fmur{tag}", [128, CD], F32)
        muk = sbuf(pp, f"fmuk{tag}", [128, CD], F32)
        xh = sbuf(pp, f"fxh{tag}", [128, CD * 512], BF16)
        xs = sbuf(pp, f"fxs{tag}", [128, CD * 512], BF16)
        carry = sbuf(pp, f"fcar{tag}", [128, CD], BF16)
        diff = sbuf(pp, f"fdif{tag}", [128, CD * 512], BF16)
        xr = sbuf(pp, f"fxr{tag}", [128, CD * 512], BF16)
        xk = sbuf(pp, f"fxk{tag}", [128, CD * 512], BF16)
        kact = sbuf(pp, f"fka{tag}", [128, CD * 512], BF16)
        krelu = [sbuf(pp, f"fkr{i}{tag}", [128, 512], F32) for i in range(2)]
        r_sb = sbuf(pp, f"fr{tag}", [128, 4 * 1024], F32)
        t1 = sbuf(pp, f"ft1{tag}", [128, 512], F32)
        xhat = [sbuf(pp, f"fxt{i}{tag}", [128, 1024], BF16) for i in range(2)]

        tr_banks = [psum_bank(pp, f"ftr{i}{tag}", [128, 128], BF16) for i in range(2)]
        g_banks = [psum_bank(pp, f"fg{i}{tag}", [128, 512], F32) for i in range(4)]

        ln = LN(pp, f"f{tag}", consts["eps"])
        s_tr = f"ftr{tag}"

        @pp.block.gpsimd
        def _(g):
            for w_sb, w_d in ((wr, wr_d), (wk, wk_d), (wv, wv_d)):
                o3 = w_sb[:].rearrange("p (c d) -> p c d", c=CD)
                i3 = w_d.rearrange("(c p) d -> p c d", p=128)
                pp.inc(g.dma_start(out=o3, in_=i3), "dpool", 16)
            pp.inc(g.dma_start(out=mur[:], in_=mur_d.rearrange("(c p) -> p c", p=128),
                               allow_slow_non_contiguous=True), "dpool", 16)
            pp.inc(g.dma_start(out=muk[:], in_=muk_d.rearrange("(c p) -> p c", p=128),
                               allow_slow_non_contiguous=True), "dpool", 16)
        w_ready = pp.val("dpool")

        for tile in range(NTILE):
            for j in range(4):
                tc = tile * 4 + j
                xht = xhat[j % 2]
                if j >= 2:

                    @pp.block.vector
                    def _(ve):
                        pp.wait(ve, s_tr, pp.cnt.get(s_tr, 0) - CD)
                ln.emit(x_sb[:, tc * 1024:(tc + 1) * 1024], xht[:])
                ln_n = pp.val(ln.rdy)
                for c in range(CD):
                    bank = tr_banks[c % 2]

                    @pp.block.tensor
                    def _(te):
                        pp.wait(te, ln.rdy, ln_n)
                        bank.wait_free(te)
                        pp.inc(te.transpose(bank.t[:, :], xht[:, c * 128:(c + 1) * 128],
                                            ident[:]), s_tr, 1)

                    @pp.block.scalar
                    def _(se):
                        pp.wait(se, s_tr)
                        col = j * 128
                        se.copy(xh[:, c * 512 + col: c * 512 + col + 128], bank.t[:, :])
                        se.drain()
                        ncols = 127 if j == 3 else 128
                        ev = se.copy(xs[:, c * 512 + col + 1: c * 512 + col + 1 + ncols],
                                     bank.t[:, 0:ncols])
                        bank.release(ev)

            @pp.block.scalar
            def _(se):
                if tile == 0:
                    se.memzero(carry[:])
                    se.drain()
                for c in range(CD):
                    se.copy(xs[:, c * 512: c * 512 + 1], carry[:, c:c + 1])
                se.drain()
                for c in range(CD):
                    se.copy(carry[:, c:c + 1], xh[:, c * 512 + 511: c * 512 + 512])
            pp.fence()

            @pp.block.vector
            def _(ve):
                if tile == 0:
                    pp.wait(ve, "dpool", w_ready)
                ve.tensor_sub(diff[:], xh[:], xs[:])
                ve.drain()
                for c in range(CD):
                    ve.scalar_tensor_tensor(xr[:, c * 512:(c + 1) * 512],
                                            diff[:, c * 512:(c + 1) * 512],
                                            mur[:, c:c + 1],
                                            xs[:, c * 512:(c + 1) * 512],
                                            ALU.mult, ALU.add)
                    ve.scalar_tensor_tensor(xk[:, c * 512:(c + 1) * 512],
                                            diff[:, c * 512:(c + 1) * 512],
                                            muk[:, c:c + 1],
                                            xs[:, c * 512:(c + 1) * 512],
                                            ALU.mult, ALU.add)
            pp.fence()

            for j in range(4):
                for dh in range(2):
                    bank = g_banks[(j * 2 + dh) % 4]

                    @pp.block.tensor
                    def _(te):
                        if tile == 0 and j == 0 and dh == 0:
                            pp.wait(te, "dpool", w_ready)
                        bank.wait_free(te)
                        for c in range(CD):
                            mm = te.matmul(bank.t[:, :],
                                           xr[:, c * 512 + j * 128: c * 512 + j * 128 + 128],
                                           wr[:, c * 1024 + dh * 512: c * 1024 + dh * 512 + 512],
                                           start=(c == 0), stop=(c == CD - 1))
                        pp.inc(mm, f"frmm{tag}", 1)

                    @pp.block.scalar
                    def _(se):
                        pp.wait(se, f"frmm{tag}")
                        ev = se.activation(r_sb[:, j * 1024 + dh * 512: j * 1024 + dh * 512 + 512],
                                           bank.t[:, :], AF.Sigmoid)
                        bank.release(ev)

            for dk in range(CD):
                bank = g_banks[dk % 4]
                kr = krelu[dk % 2]

                @pp.block.tensor
                def _(te):
                    bank.wait_free(te)
                    for c in range(CD):
                        mm = te.matmul(bank.t[:, :],
                                       wk[:, c * 1024 + dk * 128: c * 1024 + dk * 128 + 128],
                                       xk[:, c * 512:(c + 1) * 512],
                                       start=(c == 0), stop=(c == CD - 1))
                    pp.inc(mm, f"fkmm{tag}", 1)

                @pp.block.vector
                def _(ve):
                    pp.wait(ve, f"fkmm{tag}")
                    if pp.val(f"fkev{tag}") >= 2:
                        pp.wait(ve, f"fkev{tag}", pp.cnt[f"fkev{tag}"] - 1)
                    mx = ve.tensor_scalar(kr[:, :], bank.t[:, :], 0.0, None, ALU.max)
                    bank.release(mx)

                @pp.block.scalar
                def _(se):
                    bank.wait_released(se)
                    pp.inc(se.activation(kact[:, dk * 512:(dk + 1) * 512], kr[:, :],
                                         AF.Square), f"fkev{tag}", 1)

            pp.fence()

            for j in range(4):
                for dh in range(2):
                    bank = g_banks[(j * 2 + dh) % 4]

                    @pp.block.tensor
                    def _(te):
                        bank.wait_free(te)
                        for dk in range(CD):
                            mm = te.matmul(bank.t[:, :],
                                           kact[:, dk * 512 + j * 128: dk * 512 + j * 128 + 128],
                                           wv[:, dk * 1024 + dh * 512: dk * 1024 + dh * 512 + 512],
                                           start=(dk == 0), stop=(dk == CD - 1))
                        pp.inc(mm, f"fomm{tag}", 1)

                    @pp.block.vector
                    def _(ve):
                        pp.wait(ve, f"fomm{tag}")
                        tc = tile * 4 + j
                        xv = x_sb[:, tc * 1024 + dh * 512: tc * 1024 + dh * 512 + 512]
                        stt = ve.scalar_tensor_tensor(t1[:, :], bank.t[:, :], 0.5,
                                                      r_sb[:, j * 1024 + dh * 512: j * 1024 + dh * 512 + 512],
                                                      ALU.mult, ALU.mult)
                        bank.release(stt)
                        ve.drain()
                        ve.tensor_add(xv, xv, t1[:, :])
                        ve.drain()

            pp.fence()

        p.cnt.update(pp.cnt)





def emit_xhat_fm(pp, ln, x_sb, tile, xhat_fm, xhat_bufs, tr_banks, ident, s_tr):
    for j in range(4):
        tc = tile * 4 + j
        xht = xhat_bufs[j % 2]
        if j >= 2:
            @pp.block.vector
            def _(ve):
                pp.wait(ve, s_tr, pp.val(s_tr) - CD)
        ln.emit(x_sb[:, tc * 1024:(tc + 1) * 1024], xht[:])
        ln_n = pp.val(ln.rdy)
        for c in range(CD):
            bank = tr_banks[c % len(tr_banks)]

            @pp.block.tensor
            def _(te):
                pp.wait(te, ln.rdy, ln_n)
                bank.wait_free(te)
                pp.inc(te.transpose(bank.t[:, :], xht[:, c * 128:(c + 1) * 128],
                                    ident[:]), s_tr, 1)

            @pp.block.scalar
            def _(se):
                pp.wait(se, s_tr)
                ev = se.copy(xhat_fm[:, c * 512 + j * 128: c * 512 + j * 128 + 128],
                             bank.t[:, :])
                bank.release(ev)

    @pp.block.scalar
    def _(se):
        pp.inc(se.drain(), "xfdone", 1)
    return pp.val("xfdone")



def emit_moe(p, x_sb, wg_d, w1_d, b1r_d, w2_d, b2_d, consts, n_tok, tag,
             cast_w=True):
    NTILE = n_tok // 512
    ident = consts["ident"]

    with contextlib.ExitStack() as st:
        pp = p.fresh(st)

        wg = sbuf(pp, f"mwg{tag}", [128, CD * 8], F32)
        b1r = sbuf(pp, f"mb1{tag}", [128, 256], F32)
        b2 = sbuf(pp, f"mb2{tag}", [8, 1024], F32)
        xhat_fm = sbuf(pp, f"mxf{tag}", [128, CD * 512], BF16)
        xhat = [sbuf(pp, f"mxt{i}{tag}", [128, 1024], BF16) for i in range(2)]
        xhf32 = sbuf(pp, f"mxg{tag}", [128, 1024], F32)
        xf32c = sbuf(pp, f"mxgc{tag}", [128, CD * 128], F32)
        w1 = sbuf(pp, f"mw1{tag}", [128, CD * 4096], BF16)
        w2 = [sbuf(pp, f"mw2{i}{tag}", [128, 4 * 512], BF16) for i in range(2)]
        h_sb = sbuf(pp, f"mh{tag}", [128, 32 * 512], BF16)
        e8 = sbuf(pp, f"me8{tag}", [128, 8], F32)
        ge8 = sbuf(pp, f"mge{tag}", [128, 8], F32)
        tmp8 = sbuf(pp, f"mt8{tag}", [128, 8], F32)
        msk2 = sbuf(pp, f"mm2{tag}", [128, 8], F32)
        m1n = sbuf(pp, f"mm1n{tag}", [128, 1], F32)
        m2n = sbuf(pp, f"mm2n{tag}", [128, 1], F32)
        lsum = sbuf(pp, f"mls{tag}", [128, 1], F32)
        lrc = sbuf(pp, f"mlr{tag}", [128, 1], F32)
        maskp = sbuf(pp, f"mmp{tag}", [128, 4 * 8], F32)
        mtT = sbuf(pp, f"mmt{tag}", [8, 4 * 128], F32)

        tr_banks = [psum_bank(pp, f"mtr{i}{tag}", [128, 128], BF16) for i in range(1)]
        h_banks = [psum_bank(pp, f"mhb{i}{tag}", [128, 512], F32) for i in range(2)]
        y_banks = [psum_bank(pp, f"myb{i}{tag}", [128, 512], F32) for i in range(4)]
        sm_bank = psum_bank(pp, f"msm{tag}", [128, 512], F32)

        ln = LN(pp, f"m{tag}", consts["eps"])
        s_tr = f"mtr{tag}"
        dqeng = "gpsimd" if cast_w else "sync"

        @pp.block.gpsimd
        def _(g):
            pp.inc(g.dma_start(out=wg[:].rearrange("p (c e) -> p c e", c=CD),
                               in_=wg_d.rearrange("(c p) e -> p c e", p=128)), f"dwg{tag}", 16)
        @pp.block.sync
        def _(sy):
            pp.inc(sy.dma_start(out=b1r[:], in_=b1r_d), "dsync2", 16)
            pp.inc(sy.dma_start(out=b2[:], in_=b2_d), "dsync2", 16)
        smalls_g = pp.val(f"dwg{tag}")
        smalls_s = pp.val("dsync2")

        def load_w1(e):
            eng = getattr(pp.block, dqeng)

            @eng
            def _(g):
                if pp.val(f"mw1f{tag}") > 0:
                    pp.wait(g, f"mw1f{tag}")
                o3 = w1[:].rearrange("p (c h) -> p c h", c=CD)
                i3 = w1_d[e].rearrange("(c p) h -> p c h", p=128)
                pp.inc(g.dma_start(out=o3, in_=i3), f"dw1{tag}", 16)
            return pp.val(f"dw1{tag}")

        def load_w2(e, dh, hq, buf):
            eng = getattr(pp.block, dqeng)

            @eng
            def _(g):
                if pp.val(f"mw2f{buf}{tag}") > 0:
                    pp.wait(g, f"mw2f{buf}{tag}")
                o3 = w2[buf][:].rearrange("p (q d) -> p q d", q=4)
                i3 = w2_d[e][hq * 512:(hq + 1) * 512,
                             dh * 512:(dh + 1) * 512].rearrange("(q p) d -> p q d", p=128)
                pp.inc(g.dma_start(out=o3, in_=i3), f"dw2{buf}{tag}", 16)
            return pp.val(f"dw2{buf}{tag}")

        w1_rdy = load_w1(0)

        for tile in range(NTILE):
            for j in range(4):
                tc = tile * 4 + j
                xht = xhat[j % 2]
                if j >= 2:
                    @pp.block.vector
                    def _(ve):
                        pp.wait(ve, s_tr, pp.val(s_tr) - 2 * CD)
                ln.emit(x_sb[:, tc * 1024:(tc + 1) * 1024], xht[:])
                ln_n = pp.val(ln.rdy)
                @pp.block.vector
                def _(ve):
                    if pp.val(s_tr) > 0:
                        pp.wait(ve, s_tr)
                    pp.inc(ve.tensor_scalar(xhf32[:], x_sb[:, tc * 1024:(tc + 1) * 1024],
                                            ln.mv[:, 0:1], ln.rstd[:, 0:1],
                                            ALU.subtract, ALU.mult), f"mf32{tag}", 1)
                f32_n = pp.val(f"mf32{tag}")
                for c in range(CD):
                    bank = tr_banks[0]

                    @pp.block.tensor
                    def _(te):
                        pp.wait(te, ln.rdy, ln_n)
                        bank.wait_free(te)
                        pp.inc(te.transpose(bank.t[:, :], xht[:, c * 128:(c + 1) * 128],
                                            ident[:]), s_tr, 1)

                    @pp.block.scalar
                    def _(se):
                        pp.wait(se, s_tr)
                        ev = se.copy(xhat_fm[:, c * 512 + j * 128: c * 512 + j * 128 + 128],
                                     bank.t[:, :])
                        bank.release(ev)
                for c in range(CD):
                    @pp.block.tensor
                    def _(te):
                        pp.wait(te, f"mf32{tag}", f32_n)
                        sm_bank.wait_free(te)
                        pp.inc(te.transpose(sm_bank.t[0:128, 0:128],
                                            xhf32[:, c * 128:(c + 1) * 128],
                                            consts["identf"][:]), s_tr, 1)

                    @pp.block.scalar
                    def _(se):
                        pp.wait(se, s_tr)
                        ev = se.copy(xf32c[:, c * 128:(c + 1) * 128], sm_bank.t[0:128, 0:128])
                        sm_bank.release(ev)

                @pp.block.tensor
                def _(te):
                    if tile == 0 and j == 0:
                        pp.wait(te, f"dwg{tag}", smalls_g)
                        pp.wait(te, "dsync2", smalls_s)
                    sm_bank.wait_free(te)
                    for c in range(CD):
                        mm = te.matmul(sm_bank.t[:, 0:8],
                                       xf32c[:, c * 128:(c + 1) * 128],
                                       wg[:, c * 8:(c + 1) * 8],
                                       start=(c == 0), stop=(c == CD - 1))
                    pp.inc(mm, f"mgmm{tag}", 1)

                @pp.block.vector
                def _(ve):
                    pp.wait(ve, f"mgmm{tag}")
                    gps = sm_bank.t[:, 0:8]
                    ve.reduce_max(m1n[:], gps, axis=AX.X, negate=True)
                    ve.drain()
                    ve.tensor_scalar(ge8[:], gps, m1n[:, 0:1], 0.0, ALU.add, ALU.is_ge)
                    ve.drain()
                    ve.tensor_scalar(tmp8[:], ge8[:], -1e30, None, ALU.mult)
                    ve.drain()
                    ve.tensor_add(msk2[:], gps, tmp8[:])
                    ve.drain()
                    ve.reduce_max(m2n[:], msk2[:], axis=AX.X, negate=True)
                    ve.drain()
                    pp.inc(ve.tensor_scalar(ge8[:], gps, m2n[:, 0:1], 0.0, ALU.add, ALU.is_ge),
                           f"mdv{tag}", 1)

                @pp.block.scalar
                def _(se):
                    pp.wait(se, f"mdv{tag}")
                    pp.inc(se.activation(e8[:], sm_bank.t[:, 0:8], AF.Exp, bias=m1n[:, 0:1]),
                           f"mex{tag}", 1)

                @pp.block.vector
                def _(ve):
                    pp.wait(ve, f"mex{tag}")
                    em = ve.tensor_mul(e8[:], e8[:], ge8[:])
                    sm_bank.release(em)
                    ve.drain()
                    ve.reduce_sum(lsum[:], e8[:], axis=AX.X)
                    ve.drain()
                    ve.reciprocal(lrc[:], lsum[:])
                    ve.drain()
                    pp.inc(ve.tensor_scalar(maskp[:, j * 8:(j + 1) * 8], e8[:],
                                            lrc[:, 0:1], 0.0625, ALU.mult, ALU.mult),
                           f"mmp{tag}", 1)

            @pp.block.scalar
            def _(se):
                pp.inc(se.drain(), "xfdone", 1)
            xf_n = pp.val("xfdone")

            pp.fence()
            mp_n = pp.val(f"mmp{tag}")
            for j in range(4):
                @pp.block.tensor
                def _(te):
                    pp.wait(te, f"mmp{tag}", mp_n)
                    sm_bank.wait_free(te)
                    pp.inc(te.transpose(sm_bank.t[0:8, 0:128],
                                        maskp[:, j * 8:(j + 1) * 8], consts["identf"][:]),
                           f"mtp{tag}", 1)

                @pp.block.scalar
                def _(se):
                    pp.wait(se, f"mtp{tag}")
                    ev = se.copy(mtT[0:8, j * 128:(j + 1) * 128], sm_bank.t[0:8, 0:128])
                    sm_bank.release(ev)
            for j in range(4):
                for dh in range(2):
                    @pp.block.tensor
                    def _(te):
                        sm_bank.wait_free(te)
                        pp.inc(te.matmul(sm_bank.t[:, 0:512],
                                         mtT[0:8, j * 128:(j + 1) * 128],
                                         b2[0:8, dh * 512:(dh + 1) * 512],
                                         start=True, stop=True), f"mb2m{tag}", 1)

                    @pp.block.vector
                    def _(ve):
                        pp.wait(ve, f"mb2m{tag}")
                        tc = tile * 4 + j
                        xv = x_sb[:, tc * 1024 + dh * 512: tc * 1024 + dh * 512 + 512]
                        ad = ve.tensor_add(xv, xv, sm_bank.t[:, 0:512])
                        sm_bank.release(ad)
                        ve.drain()

            for e in range(8):
                for hc in range(32):
                    bank = h_banks[hc % 2]

                    @pp.block.tensor
                    def _(te):
                        if hc == 0:
                            pp.wait(te, f"dw1{tag}", w1_rdy)
                            pp.wait(te, "xfdone", xf_n)
                        bank.wait_free(te)
                        for c in range(CD):
                            mm = te.matmul(bank.t[:, :],
                                           w1[:, c * 4096 + hc * 128: c * 4096 + hc * 128 + 128],
                                           xhat_fm[:, c * 512:(c + 1) * 512],
                                           start=(c == 0), stop=(c == CD - 1))
                        pp.inc(mm, f"mh1{tag}", 1)

                    @pp.block.scalar
                    def _(se):
                        pp.wait(se, f"mh1{tag}")
                        ev = se.activation(h_sb[:, hc * 512:(hc + 1) * 512], bank.t[:, :],
                                           AF.Relu, bias=b1r[:, e * 32 + hc: e * 32 + hc + 1])
                        bank.release(ev)

                h_rel = [(b, b.uses) for b in h_banks]

                @pp.block.tensor
                def _(te):
                    pp.inc(te.drain(), f"mw1f{tag}", 1)
                if not (e == 7 and tile == NTILE - 1):
                    w1_rdy = load_w1((e + 1) % 8)

                seq = [(dh, hq) for dh in (0, 1) for hq in range(8)]
                rdy = {}
                rdy[(0, 0)] = load_w2(e, 0, 0, 0)
                rdy[(0, 1)] = load_w2(e, 0, 1, 1)
                for idx, (dh, hq) in enumerate(seq):
                    buf = idx % 2

                    @pp.block.tensor
                    def _(te):
                        pp.wait(te, f"dw2{idx % 2}{tag}", rdy[(dh, hq)])
                        if hq == 0:
                            for j in range(4):
                                y_banks[j].wait_free(te)
                            if idx == 0:
                                for b, u in h_rel:
                                    te.wait_ge(pp.sem(b.name), u)
                        for hc4 in range(4):
                            hc = hq * 4 + hc4
                            for j in range(4):
                                mm = te.matmul(y_banks[j].t[:, :],
                                               h_sb[:, hc * 512 + j * 128: hc * 512 + j * 128 + 128],
                                               w2[buf][:, hc4 * 512:(hc4 + 1) * 512],
                                               start=(hq == 0 and hc4 == 0),
                                               stop=(hq == 7 and hc4 == 3))
                        pp.inc(mm, f"mw2f{buf}{tag}", 1)
                        if hq == 7:
                            pp.inc(te.drain(), f"my2{tag}", 1)

                    if idx + 2 < 16:
                        nd, nh = seq[idx + 2]
                        rdy[(nd, nh)] = load_w2(e, nd, nh, buf)

                    if hq == 7:
                        y_done = pp.val(f"my2{tag}")

                        @pp.block.vector
                        def _(ve):
                            pp.wait(ve, f"my2{tag}", y_done)
                            for j in range(4):
                                tc = tile * 4 + j
                                xv = x_sb[:, tc * 1024 + dh * 512: tc * 1024 + dh * 512 + 512]
                                stt = ve.scalar_tensor_tensor(xv, y_banks[j].t[:, :],
                                                              maskp[:, j * 8 + e: j * 8 + e + 1],
                                                              xv, ALU.mult, ALU.add)
                                y_banks[j].release(stt)
                                ve.drain()

            pp.fence()

        p.cnt.update(pp.cnt)





def emit_attn0(p, x_sb, wattn_d, wproj_d, consts, n_tok, tag):
    NTILE = n_tok // 512
    NT = n_tok // 128
    ident = consts["ident"]
    identf = consts["identf"]

    with contextlib.ExitStack() as st:
        pp = p.fresh(st)

        K_sb = sbuf(pp, f"aK{tag}", [128, CD * n_tok], BF16)
        V_sb = sbuf(pp, f"aV{tag}", [128, NT * 1040], BF16)
        knm = sbuf(pp, f"akm{tag}", [1, 16], F32)
        knt = sbuf(pp, f"akt{tag}", [1, 1], F32)
        qn2m = sbuf(pp, f"aqm{tag}", [1, 16], F32)
        mhn = sbuf(pp, f"amh{tag}", [1, 16], F32)
        prod = sbuf(pp, f"apr{tag}", [1, 16], F32)
        xhat_fm = sbuf(pp, f"axf{tag}", [128, CD * 512], BF16)
        xhat = [sbuf(pp, f"axt{i}{tag}", [128, 1024], BF16) for i in range(2)]
        sq = sbuf(pp, f"asq{tag}", [128, 512], BF16)
        mhb = sbuf(pp, f"amb{tag}", [128, 1], F32)
        onesrow = consts["onesrow"]
        onespair = consts["onespair"]
        masksT = consts["masksT"]

        tr_banks = [psum_bank(pp, f"atr{tag}", [128, 128], BF16)]
        s_banks = [psum_bank(pp, f"as{i}{tag}", [128, 512], F32) for i in range(2)]
        av_bank = psum_bank(pp, f"aav{tag}", [128, 512], F32)
        sm_bank = psum_bank(pp, f"asm{tag}", [128, 512], F32)

        ln = LN(pp, f"a{tag}", consts["eps"])
        s_tr = f"atr{tag}"

        with contextlib.ExitStack() as st1:
            p1 = pp.fresh(st1)
            wak = sbuf(p1, f"awk{tag}", [128, CD * 1024], BF16)
            wav = sbuf(p1, f"awv{tag}", [128, CD * 1024], BF16)

            @p1.block.gpsimd
            def _(g):
                for w_sb, lo in ((wak, 0), (wav, 2048)):
                    o3 = w_sb[:].rearrange("p (c d) -> p c d", c=CD)
                    i3 = wattn_d[:, lo:lo + 1024].rearrange("(c p) d -> p c d", p=128)
                    p1.inc(g.dma_start(out=o3, in_=i3), f"adw1{tag}", 16)
            w1_rdy = p1.val(f"adw1{tag}")

            @p1.block.vector
            def _(ve):
                ve.memset(V_sb[:], 1.0)
                ve.drain()
                ve.memset(knm[:], 0.0)
                p1.inc(ve.drain(), f"avini{tag}", 1)
            vini = p1.val(f"avini{tag}")

            for tile in range(NTILE):
                xf_n = emit_xhat_fm(p1, ln, x_sb, tile, xhat_fm, xhat, tr_banks, ident, s_tr)

                for ch in range(CD):
                    bank = s_banks[ch % 2]

                    @p1.block.tensor
                    def _(te):
                        if tile == 0 and ch == 0:
                            p1.wait(te, f"adw1{tag}", w1_rdy)
                        p1.wait(te, "xfdone", xf_n)
                        bank.wait_free(te)
                        for c in range(CD):
                            mm = te.matmul(bank.t[:, :],
                                           wak[:, c * 1024 + ch * 128: c * 1024 + ch * 128 + 128],
                                           xhat_fm[:, c * 512:(c + 1) * 512],
                                           start=(c == 0), stop=(c == CD - 1))
                        p1.inc(mm, f"akmm{tag}", 1)

                    @p1.block.scalar
                    def _(se):
                        p1.wait(se, f"akmm{tag}")
                        se.copy(K_sb[:, ch * n_tok + tile * 512: ch * n_tok + tile * 512 + 512],
                                bank.t[:, :])
                        se.drain()
                        ev = se.activation(sq[:, :], bank.t[:, :], AF.Square)
                        bank.release(ev)
                    k_u = (bank, bank.uses)

                    for par in range(2):
                        @p1.block.tensor
                        def _(te):
                            te.wait_ge(p1.sem(k_u[0].name), k_u[1])
                            sm_bank.wait_free(te)
                            p1.inc(te.matmul(sm_bank.t[0:1, 0:512],
                                             onespair[:, par:par + 1], sq[:, :],
                                             start=True, stop=True), f"aknm{tag}", 1)

                        @p1.block.vector
                        def _(ve):
                            p1.wait(ve, f"aknm{tag}")
                            h = 2 * ch + par
                            mx = ve.reduce_max(knt[0:1, 0:1], sm_bank.t[0:1, 0:512], axis=AX.X)
                            sm_bank.release(mx)
                            ve.drain()
                            ve.tensor_max(knm[0:1, h:h + 1], knm[0:1, h:h + 1], knt[0:1, 0:1])
                            ve.drain()

                for j in range(4):
                    tc = tile * 4 + j
                    for vh in range(2):
                        bank = s_banks[vh % 2]

                        @p1.block.tensor
                        def _(te):
                            bank.wait_free(te)
                            for c in range(CD):
                                mm = te.matmul(bank.t[:, :],
                                               xhat_fm[:, c * 512 + j * 128: c * 512 + j * 128 + 128],
                                               wav[:, c * 1024 + vh * 512: c * 1024 + vh * 512 + 512],
                                               start=(c == 0), stop=(c == CD - 1))
                            p1.inc(mm, f"avmm{tag}", 1)

                        @p1.block.scalar
                        def _(se):
                            p1.wait(se, f"avmm{tag}")
                            if tile == 0 and j == 0 and vh == 0:
                                p1.wait(se, f"avini{tag}", vini)
                            ov = V_sb[:, tc * 1040 + vh * 520: tc * 1040 + vh * 520 + 520]
                            ov = ov.rearrange("p (h i) -> p h i", h=8)[:, :, 0:64]
                            iv = bank.t[:, :].rearrange("p (h i) -> p h i", h=8)
                            ev = se.copy(ov, iv)
                            bank.release(ev)
                pp.fence()

            pp.fence()
            pp.cnt.update(p1.cnt)

        with contextlib.ExitStack() as st2:
            p2 = pp.fresh(st2)
            waq = sbuf(p2, f"awq{tag}", [128, CD * 1024], BF16)
            wap = sbuf(p2, f"awp{tag}", [128, CD * 512], BF16)
            Qh = sbuf(p2, f"aQ{tag}", [128, 16 * 512], BF16)
            STx = [sbuf(p2, f"aST{i}{tag}", [128, 512], BF16) for i in range(2)]
            y_sb = sbuf(p2, f"ay{tag}", [128, CD * 512], BF16)
            rl = sbuf(p2, f"arl{tag}", [1, 512], F32)
            lbc = sbuf(p2, f"albc{tag}", [64, 512], F32)

            @p2.block.gpsimd
            def _(g):
                p2.wait(g, f"adw1{tag}")
                p2.wait(g, "fence")
                o3 = waq[:].rearrange("p (c d) -> p c d", c=CD)
                i3 = wattn_d[:, 1024:2048].rearrange("(c p) d -> p c d", p=128)
                p2.inc(g.dma_start(out=o3, in_=i3), f"adw2{tag}", 16)
            w2_rdy = p2.val(f"adw2{tag}")

            def load_wap(dh):
                @p2.block.gpsimd
                def _(g):
                    if p2.val(f"awapf{tag}") > 0:
                        p2.wait(g, f"awapf{tag}")
                    o3 = wap[:].rearrange("p (c d) -> p c d", c=CD)
                    i3 = wproj_d[:, dh * 512:(dh + 1) * 512].rearrange("(c p) d -> p c d", p=128)
                    p2.inc(g.dma_start(out=o3, in_=i3), f"adwp{tag}", 16)
                return p2.val(f"adwp{tag}")

            for qt in range(NTILE):
                xf_n = emit_xhat_fm(p2, ln, x_sb, qt, xhat_fm, xhat, tr_banks, ident, s_tr)

                @p2.block.vector
                def _(ve):
                    ve.memset(Qh[:], 0.0)
                    p2.inc(ve.drain(), f"aqz{tag}", 1)
                qz = p2.val(f"aqz{tag}")

                for qh in range(CD):
                    bank = s_banks[qh % 2]

                    @p2.block.tensor
                    def _(te):
                        if qt == 0 and qh == 0:
                            p2.wait(te, f"adw2{tag}", w2_rdy)
                        p2.wait(te, "xfdone", xf_n)
                        bank.wait_free(te)
                        for c in range(CD):
                            mm = te.matmul(bank.t[:, :],
                                           waq[:, c * 1024 + qh * 128: c * 1024 + qh * 128 + 128],
                                           xhat_fm[:, c * 512:(c + 1) * 512],
                                           start=(c == 0), stop=(c == CD - 1))
                        p2.inc(mm, f"aqmm{tag}", 1)

                    @p2.block.scalar
                    def _(se):
                        p2.wait(se, f"aqmm{tag}")
                        if qh == 0:
                            p2.wait(se, f"aqz{tag}", qz)
                        se.activation(Qh[0:64, (2 * qh) * 512:(2 * qh) * 512 + 512],
                                      bank.t[0:64, :], AF.Copy, scale=0.125)
                        se.drain()
                        se.activation(Qh[64:128, (2 * qh + 1) * 512:(2 * qh + 1) * 512 + 512],
                                      bank.t[64:128, :], AF.Copy, scale=0.125)
                        se.drain()
                        ev = se.activation(sq[:, :], bank.t[:, :], AF.Square)
                        bank.release(ev)
                    q_u = (bank, bank.uses)

                    for par in range(2):
                        @p2.block.tensor
                        def _(te):
                            te.wait_ge(p2.sem(q_u[0].name), q_u[1])
                            sm_bank.wait_free(te)
                            p2.inc(te.matmul(sm_bank.t[0:1, 0:512],
                                             onespair[:, par:par + 1], sq[:, :],
                                             start=True, stop=True), f"aqnm{tag}", 1)

                        @p2.block.vector
                        def _(ve):
                            p2.wait(ve, f"aqnm{tag}")
                            mx = ve.reduce_max(qn2m[0:1, 2 * qh + par:2 * qh + par + 1],
                                               sm_bank.t[0:1, 0:512], axis=AX.X)
                            sm_bank.release(mx)

                @p2.block.vector
                def _(ve):
                    ve.drain()
                    p2.inc(ve.tensor_mul(prod[0:1, :], qn2m[0:1, :], knm[0:1, :]),
                           f"apro{tag}", 1)

                @p2.block.scalar
                def _(se):
                    p2.wait(se, f"apro{tag}")
                    se.activation(prod[0:1, :], prod[0:1, :], AF.Ln)
                    se.drain()
                    p2.inc(se.activation(mhn[0:1, :], prod[0:1, :], AF.Exp, scale=0.5),
                           f"amh1{tag}", 1)

                @p2.block.vector
                def _(ve):
                    p2.wait(ve, f"amh1{tag}")
                    p2.inc(ve.tensor_scalar(mhn[0:1, :], mhn[0:1, :], -0.125, None, ALU.mult),
                           f"amh2{tag}", 1)
                mh_n = p2.val(f"amh2{tag}")

                n_kc = 4 * (qt + 1)
                for h in range(16):
                    ch = h // 2
                    @p2.block.tensor
                    def _(te):
                        p2.wait(te, f"amh2{tag}", mh_n)
                        sm_bank.wait_free(te)
                        p2.inc(te.matmul(sm_bank.t[0:128, 0:1], onesrow, mhn[0:1, h:h + 1],
                                         start=True, stop=True), f"ambc{tag}", 1)

                    @p2.block.scalar
                    def _(se):
                        p2.wait(se, f"ambc{tag}")
                        ev = se.copy(mhb[:, 0:1], sm_bank.t[0:128, 0:1])
                        sm_bank.release(ev)
                    mb_u = sm_bank.uses

                    for kc in range(n_kc):
                        bank = s_banks[kc % 2]
                        stx = STx[kc % 2]

                        @p2.block.tensor
                        def _(te):
                            bank.wait_free(te)
                            p2.inc(te.matmul(bank.t[:, :],
                                             K_sb[:, ch * n_tok + kc * 128: ch * n_tok + kc * 128 + 128],
                                             Qh[:, h * 512:(h + 1) * 512],
                                             start=True, stop=True), f"asmm{tag}", 1)

                        @p2.block.scalar
                        def _(se):
                            p2.wait(se, f"asmm{tag}")
                            if kc == 0:
                                se.wait_ge(p2.sem(sm_bank.name), mb_u)
                            if p2.val(f"astx{tag}") >= 2:
                                p2.wait(se, f"astx{tag}", p2.cnt[f"astx{tag}"] - 1)
                            ev = se.activation(stx[:, :], bank.t[:, :], AF.Exp, bias=mhb[:, 0:1])
                            bank.release(ev)
                        s_u = (bank, bank.uses)

                        diag = kc - 4 * qt
                        if diag >= 0:
                            @p2.block.vector
                            def _(ve):
                                ve.wait_ge(p2.sem(s_u[0].name), s_u[1])
                                p2.inc(ve.tensor_mul(stx[:, :], stx[:, :], masksT[diag][:]),
                                       f"asev{tag}", 1)

                        @p2.block.tensor
                        def _(te):
                            if diag >= 0:
                                p2.wait(te, f"asev{tag}")
                            else:
                                te.wait_ge(p2.sem(s_u[0].name), s_u[1])
                            if kc == 0:
                                av_bank.wait_free(te)
                            mm = te.matmul(av_bank.t[0:65, :],
                                           V_sb[:, kc * 1040 + h * 65: kc * 1040 + h * 65 + 65],
                                           stx[:, :],
                                           start=(kc == 0), stop=(kc == n_kc - 1))
                            p2.inc(mm, f"astx{tag}", 1)
                            if kc == n_kc - 1:
                                p2.inc(te.drain(), f"aavd{tag}", 1)
                    av_n = p2.val(f"aavd{tag}")

                    @p2.block.vector
                    def _(ve):
                        p2.wait(ve, f"aavd{tag}", av_n)
                        p2.inc(ve.reciprocal(rl[0:1, :], av_bank.t[64:65, 0:512]),
                               f"arl{tag}", 1)

                    @p2.block.tensor
                    def _(te):
                        p2.wait(te, f"arl{tag}")
                        sm_bank.wait_free(te)
                        p2.inc(te.matmul(sm_bank.t[0:64, 0:512], onesrow[0:1, 0:64], rl[0:1, :],
                                         start=True, stop=True), f"albm{tag}", 1)

                    @p2.block.scalar
                    def _(se):
                        p2.wait(se, f"albm{tag}")
                        ev = se.copy(lbc[:, :], sm_bank.t[0:64, 0:512])
                        sm_bank.release(ev)
                    lb_u = sm_bank.uses

                    @p2.block.vector
                    def _(ve):
                        ve.wait_ge(p2.sem(sm_bank.name), lb_u)
                        ev = ve.tensor_mul(y_sb[(h % 2) * 64:(h % 2) * 64 + 64, ch * 512:(ch + 1) * 512],
                                           av_bank.t[0:64, :], lbc[:, :])
                        av_bank.release(ev)

                p2.fence()

                for dh in range(2):
                    wap_rdy = load_wap(dh)
                    for j in range(4):
                        tc = qt * 4 + j
                        bank = s_banks[j % 2]

                        @p2.block.tensor
                        def _(te):
                            if j == 0:
                                p2.wait(te, f"adwp{tag}", wap_rdy)
                            bank.wait_free(te)
                            for c in range(CD):
                                mm = te.matmul(bank.t[:, :],
                                               y_sb[:, c * 512 + j * 128: c * 512 + j * 128 + 128],
                                               wap[:, c * 512:(c + 1) * 512],
                                               start=(c == 0), stop=(c == CD - 1))
                            p2.inc(mm, f"apmm{tag}", 1)
                            if j == 3:
                                p2.inc(te.drain(), f"awapf{tag}", 1)

                        @p2.block.vector
                        def _(ve):
                            p2.wait(ve, f"apmm{tag}")
                            xv = x_sb[:, tc * 1024 + dh * 512: tc * 1024 + dh * 512 + 512]
                            ad = ve.tensor_add(xv, xv, bank.t[:, :])
                            bank.release(ad)
                            ve.drain()

                p2.fence()

            pp.cnt.update(p2.cnt)

        p.cnt.update(pp.cnt)





def emit_attn1(p, x_sb, wqtp_d, wktp_d, wvt_d, cos_d, sin_d, consts, n_tok, tag):
    NTILE = n_tok // 512
    NT = n_tok // 128
    ident = consts["ident"]

    with contextlib.ExitStack() as st:
        pp = p.fresh(st)

        K_sb = sbuf(pp, f"bK{tag}", [128, CD * n_tok], BF16)
        V_sb = sbuf(pp, f"bV{tag}", [128, NT * 1024], BF16)
        xhat_fm = sbuf(pp, f"bxf{tag}", [128, CD * 512], BF16)
        xhat = [sbuf(pp, f"bxt{i}{tag}", [128, 1024], BF16) for i in range(2)]
        raw = sbuf(pp, f"braw{tag}", [128, 2 * 512], F32)
        tcos = sbuf(pp, f"bco{tag}", [128, 512], F32)
        tsin = sbuf(pp, f"bsi{tag}", [128, 512], F32)
        t1 = sbuf(pp, f"bt1{tag}", [128, 512], F32)
        t2 = sbuf(pp, f"bt2{tag}", [128, 512], F32)
        negm = sbuf(pp, f"bnm{tag}", [128, 1], F32)
        lsum = sbuf(pp, f"bl{tag}", [128, 1], F32)
        rl1 = sbuf(pp, f"brl{tag}", [128, 1], F32)
        masksA = consts["masksA"]

        tr_banks = [psum_bank(pp, f"btr{tag}", [128, 128], BF16)]
        s_banks = [psum_bank(pp, f"bs{i}{tag}", [128, 512], F32) for i in range(2)]
        av_bank = psum_bank(pp, f"bav{tag}", [128, 512], F32)

        ln = LN(pp, f"b{tag}", consts["eps"])
        s_tr = f"btr{tag}"

        def load_tables(i, tile):
            @pp.block.sync
            def _(sy):
                if pp.val(f"btbf{tag}") > 0:
                    pp.wait(sy, f"btbf{tag}")
                for dst, srcd in ((tcos, cos_d), (tsin, sin_d)):
                    i3 = srcd[i * 128:(i + 1) * 128, tile * 512:(tile + 1) * 512]
                    pp.inc(sy.dma_start(out=dst[:], in_=i3), f"btb{tag}", 16)
            return pp.val(f"btb{tag}")

        def rope_pair(ppx, i, tile, w_sb, dst_sb, dst_col, tb_rdy, wname, w_rdy, xf_n):
            for half, ch in ((0, i), (1, i + 4)):
                bank = s_banks[half]

                @ppx.block.tensor
                def _(te):
                    if w_rdy is not None:
                        ppx.wait(te, wname, w_rdy)
                    ppx.wait(te, "xfdone", xf_n)
                    bank.wait_free(te)
                    for c in range(CD):
                        mm = te.matmul(bank.t[:, :],
                                       w_sb[:, c * 1024 + ch * 128: c * 1024 + ch * 128 + 128],
                                       xhat_fm[:, c * 512:(c + 1) * 512],
                                       start=(c == 0), stop=(c == CD - 1))
                    ppx.inc(mm, f"brmm{tag}", 1)

                @ppx.block.scalar
                def _(se):
                    ppx.wait(se, f"brmm{tag}")
                    if half == 0 and ppx.val(f"btbf{tag}") > 0:
                        ppx.wait(se, f"btbf{tag}")
                    ev = se.copy(raw[:, half * 512:(half + 1) * 512], bank.t[:, :])
                    bank.release(ev)
            r_us = [(s_banks[0], s_banks[0].uses), (s_banks[1], s_banks[1].uses)]

            @ppx.block.vector
            def _(ve):
                for b, u in r_us:
                    ve.wait_ge(ppx.sem(b.name), u)
                ppx.wait(ve, f"btb{tag}", tb_rdy)
                r0 = raw[:, 0:512]
                r1 = raw[:, 512:1024]
                ve.tensor_mul(t1[:], r0, tcos[:])
                ve.drain()
                ve.tensor_mul(t2[:], r1, tsin[:])
                ve.drain()
                ve.tensor_sub(dst_sb[:, i * n_tok + dst_col: i * n_tok + dst_col + 512], t1[:], t2[:])
                ve.drain()
                ve.tensor_mul(t1[:], r1, tcos[:])
                ve.drain()
                ve.tensor_mul(t2[:], r0, tsin[:])
                ve.drain()
                ve.tensor_add(dst_sb[:, (i + 4) * n_tok + dst_col: (i + 4) * n_tok + dst_col + 512],
                              t1[:], t2[:])
                ppx.inc(ve.drain(), f"btbf{tag}", 1)

        with contextlib.ExitStack() as st1:
            p1 = pp.fresh(st1)
            wk = sbuf(p1, f"bwk{tag}", [128, CD * 1024], BF16)
            wv = sbuf(p1, f"bwv{tag}", [128, CD * 1024], BF16)

            @p1.block.gpsimd
            def _(g):
                for w_sb, w_d in ((wk, wktp_d), (wv, wvt_d)):
                    o3 = w_sb[:].rearrange("p (c d) -> p c d", c=CD)
                    i3 = w_d.rearrange("(c p) d -> p c d", p=128)
                    p1.inc(g.dma_start(out=o3, in_=i3), f"bdw1{tag}", 16)
            w1_rdy = p1.val(f"bdw1{tag}")

            for tile in range(NTILE):
                xf_n = emit_xhat_fm(p1, ln, x_sb, tile, xhat_fm, xhat, tr_banks, ident, s_tr)
                for i in range(4):
                    tb = load_tables(i, tile)
                    rope_pair(p1, i, tile, wk, K_sb, tile * 512, tb,
                              f"bdw1{tag}", w1_rdy if (tile == 0 and i == 0) else None, xf_n)
                for j in range(4):
                    tc = tile * 4 + j
                    for dh in range(2):
                        bank = s_banks[dh]

                        @p1.block.tensor
                        def _(te):
                            bank.wait_free(te)
                            for c in range(CD):
                                mm = te.matmul(bank.t[:, :],
                                               xhat_fm[:, c * 512 + j * 128: c * 512 + j * 128 + 128],
                                               wv[:, c * 1024 + dh * 512: c * 1024 + dh * 512 + 512],
                                               start=(c == 0), stop=(c == CD - 1))
                            p1.inc(mm, f"bvmm{tag}", 1)

                        @p1.block.scalar
                        def _(se):
                            p1.wait(se, f"bvmm{tag}")
                            ev = se.copy(V_sb[:, tc * 1024 + dh * 512: tc * 1024 + dh * 512 + 512],
                                         bank.t[:, :])
                            bank.release(ev)
                pp.fence()
            pp.cnt.update(p1.cnt)

        with contextlib.ExitStack() as st2:
            p2 = pp.fresh(st2)
            wq = sbuf(p2, f"bwq{tag}", [128, CD * 1024], BF16)
            Qrot = sbuf(p2, f"bQ{tag}", [128, CD * 512], BF16)
            Sf = sbuf(p2, f"bS{tag}", [128, n_tok], F32)
            STe = sbuf(p2, f"bSe{tag}", [128, n_tok], BF16)
            STt = sbuf(p2, f"bSt{tag}", [128, n_tok], BF16)

            @p2.block.gpsimd
            def _(g):
                p2.wait(g, f"bdw1{tag}")
                p2.wait(g, "fence")
                o3 = wq[:].rearrange("p (c d) -> p c d", c=CD)
                i3 = wqtp_d.rearrange("(c p) d -> p c d", p=128)
                p2.inc(g.dma_start(out=o3, in_=i3), f"bdw2{tag}", 16)
            w2_rdy = p2.val(f"bdw2{tag}")

            for qt in range(NTILE):
                xf_n = emit_xhat_fm(p2, ln, x_sb, qt, xhat_fm, xhat, tr_banks, ident, s_tr)
                for i in range(4):
                    tb = load_tables(i, qt)
                    for half, ch in ((0, i), (1, i + 4)):
                        bank = s_banks[half]

                        @p2.block.tensor
                        def _(te):
                            if qt == 0 and i == 0 and half == 0:
                                p2.wait(te, f"bdw2{tag}", w2_rdy)
                            p2.wait(te, "xfdone", xf_n)
                            bank.wait_free(te)
                            for c in range(CD):
                                mm = te.matmul(bank.t[:, :],
                                               wq[:, c * 1024 + ch * 128: c * 1024 + ch * 128 + 128],
                                               xhat_fm[:, c * 512:(c + 1) * 512],
                                               start=(c == 0), stop=(c == CD - 1))
                            p2.inc(mm, f"brmm{tag}", 1)

                        @p2.block.scalar
                        def _(se):
                            p2.wait(se, f"brmm{tag}")
                            if half == 0 and p2.val(f"btbf{tag}") > 0:
                                p2.wait(se, f"btbf{tag}")
                            ev = se.copy(raw[:, half * 512:(half + 1) * 512], bank.t[:, :])
                            bank.release(ev)
                    r_us = [(s_banks[0], s_banks[0].uses), (s_banks[1], s_banks[1].uses)]

                    @p2.block.vector
                    def _(ve):
                        for b, u in r_us:
                            ve.wait_ge(p2.sem(b.name), u)
                        p2.wait(ve, f"btb{tag}", tb)
                        r0 = raw[:, 0:512]
                        r1 = raw[:, 512:1024]
                        ve.tensor_mul(t1[:], r0, tcos[:])
                        ve.drain()
                        ve.tensor_mul(t2[:], r1, tsin[:])
                        ve.drain()
                        ve.tensor_sub(Qrot[:, i * 512:(i + 1) * 512], t1[:], t2[:])
                        ve.drain()
                        ve.tensor_mul(t1[:], r1, tcos[:])
                        ve.drain()
                        ve.tensor_mul(t2[:], r0, tsin[:])
                        ve.drain()
                        ve.tensor_add(Qrot[:, (i + 4) * 512:(i + 5) * 512], t1[:], t2[:])
                        p2.inc(ve.drain(), f"btbf{tag}", 1)

                @p2.block.vector
                def _(ve):
                    p2.inc(ve.drain(), f"bqr{tag}", 1)
                q_n = p2.val(f"bqr{tag}")

                for qq in range(4):
                    k_hi = qt * 512 + (qq + 1) * 128
                    n_k5 = (k_hi + 511) // 512
                    for kc in range(n_k5):
                        bank = s_banks[kc % 2]
                        diag = (kc == n_k5 - 1)

                        @p2.block.tensor
                        def _(te):
                            if kc == 0:
                                p2.wait(te, f"bqr{tag}", q_n)
                            bank.wait_free(te)
                            for c in range(CD):
                                mm = te.matmul(bank.t[:, :],
                                               Qrot[:, c * 512 + qq * 128: c * 512 + qq * 128 + 128],
                                               K_sb[:, c * n_tok + kc * 512: c * n_tok + kc * 512 + 512],
                                               start=(c == 0), stop=(c == CD - 1))
                            p2.inc(mm, f"bsmm{tag}", 1)

                        if diag:
                            dd = (qt * 512 + qq * 128 - kc * 512) // 128

                            @p2.block.vector
                            def _(ve):
                                p2.wait(ve, f"bsmm{tag}")
                                if p2.val(f"bex{tag}") > 0:
                                    p2.wait(ve, f"bex{tag}")
                                ad = ve.tensor_add(Sf[:, kc * 512:(kc + 1) * 512],
                                                   bank.t[:, :], masksA[dd][:])
                                bank.release(ad)
                        else:
                            @p2.block.scalar
                            def _(se):
                                p2.wait(se, f"bsmm{tag}")
                                ev = se.copy(Sf[:, kc * 512:(kc + 1) * 512], bank.t[:, :])
                                bank.release(ev)
                    s_rel = [(b, b.uses) for b in s_banks]
                    k512 = n_k5 * 512

                    @p2.block.vector
                    def _(ve):
                        for b, u in s_rel:
                            ve.wait_ge(p2.sem(b.name), u)
                        ve.reduce_max(negm[:], Sf[:, 0:k512], axis=AX.X, negate=True)
                        ve.drain()
                        p2.inc(ve.tensor_scalar(negm[:], negm[:], 0.03125, None, ALU.mult),
                               f"bnm{tag}", 1)

                    @p2.block.scalar
                    def _(se):
                        p2.wait(se, f"bnm{tag}")
                        if p2.val(f"bstr{tag}") > 0:
                            p2.wait(se, f"bstr{tag}")
                        p2.inc(se.activation(STe[:, 0:k512], Sf[:, 0:k512], AF.Exp,
                                             bias=negm[:, 0:1], scale=0.03125,
                                             accum_out=lsum[:, 0:1]), f"bex{tag}", 1)
                    e_n = p2.val(f"bex{tag}")

                    for kc in range(k512 // 128):
                        bank = tr_banks[0]

                        @p2.block.tensor
                        def _(te):
                            p2.wait(te, f"bex{tag}", e_n)
                            bank.wait_free(te)
                            p2.inc(te.transpose(bank.t[:, :], STe[:, kc * 128:(kc + 1) * 128],
                                                ident[:]), f"bstr{tag}", 1)

                        @p2.block.scalar
                        def _(se):
                            p2.wait(se, f"bstr{tag}")
                            ev = se.copy(STt[:, kc * 128:(kc + 1) * 128], bank.t[:, :])
                            bank.release(ev)
                    stt_u = (tr_banks[0], tr_banks[0].uses)

                    @p2.block.vector
                    def _(ve):
                        p2.wait(ve, f"bex{tag}", e_n)
                        p2.inc(ve.reciprocal(rl1[:], lsum[:, 0:1]), f"brc{tag}", 1)
                    rc_n = p2.val(f"brc{tag}")

                    for dh in range(2):
                        @p2.block.tensor
                        def _(te):
                            if dh == 0:
                                te.wait_ge(p2.sem(stt_u[0].name), stt_u[1])
                            av_bank.wait_free(te)
                            for kc in range(k512 // 128):
                                mm = te.matmul(av_bank.t[:, :],
                                               STt[:, kc * 128:(kc + 1) * 128],
                                               V_sb[:, kc * 1024 + dh * 512: kc * 1024 + dh * 512 + 512],
                                               start=(kc == 0), stop=(kc == k512 // 128 - 1))
                            p2.inc(mm, f"bavm{tag}", 1)

                        @p2.block.vector
                        def _(ve):
                            p2.wait(ve, f"bavm{tag}")
                            if dh == 0:
                                p2.wait(ve, f"brc{tag}", rc_n)
                            tc = qt * 4 + qq
                            xv = x_sb[:, tc * 1024 + dh * 512: tc * 1024 + dh * 512 + 512]
                            stt = ve.scalar_tensor_tensor(xv, av_bank.t[:, :], rl1[:, 0:1],
                                                          xv, ALU.mult, ALU.add)
                            av_bank.release(stt)
                            ve.drain()

                p2.fence()

            pp.cnt.update(p2.cnt)

        p.cnt.update(pp.cnt)



def _seq_io(p, block, x_sb, x_d, y_d, s, NT, load, consts=None, ident_dram=None):
    if load:
        @block.sync
        def _(sy):
            if consts is not None and s == 0:
                p.inc(sy.dma_start(out=consts["ident"][:], in_=ident_dram), "dsync", 16)
            o3 = x_sb[:].rearrange("p (c d) -> p c d", c=NT)
            i3 = x_d[s].rearrange("(c p) d -> p c d", p=128)
            p.inc(sy.dma_start(out=o3, in_=i3), "dsync", 16)
        n = p.val("dsync")
        for engname in ("tensor", "vector", "scalar"):
            @getattr(block, engname)
            def _(eng):
                p.wait(eng, "dsync", n)
    else:
        @block.sync
        def _(sy):
            sy.wait_ge(p.sem("fence"), p.cnt["fence"])
            o3 = y_d[s].rearrange("(c p) d -> p c d", p=128)
            i3 = x_sb[:].rearrange("p (c d) -> p c d", c=NT)
            p.inc(sy.dma_start(out=o3, in_=i3), "dstore", 16)
            p.wait(sy, "dstore")


def build_ffn0_only(n_tok):
    nc = bass.Bass("TRN2", target_bir_lowering=False)
    NT = n_tok // 128
    x_d = nc.dram_tensor("x", [2, n_tok, D], F32, kind="ExternalInput").ap()
    wr_d = nc.dram_tensor("wrt", [D, D], F32, kind="ExternalInput").ap()
    wk_d = nc.dram_tensor("wkt", [D, D], F32, kind="ExternalInput").ap()
    wv_d = nc.dram_tensor("wvt", [D, D], F32, kind="ExternalInput").ap()
    mur_d = nc.dram_tensor("mur", [D], F32, kind="ExternalInput").ap()
    muk_d = nc.dram_tensor("muk", [D], F32, kind="ExternalInput").ap()
    y_d = nc.dram_tensor("y", [2, n_tok, D], F32, kind="ExternalOutput").ap()

    with contextlib.ExitStack() as stack:
        p = P(nc, stack)
        ident_dram = nc.inline_tensor(np.eye(128, dtype=ml_dtypes.bfloat16), "identc").ap()
        ident = sbuf(p, "ident", [128, 128], BF16)
        eps_t = sbuf(p, "epsc", [128, 1], F32)
        x_sb = sbuf(p, "xres", [128, NT * 1024], F32)
        consts = {"ident": ident, "eps": eps_t[:, 0:1]}

        for s in range(2):
            with nc.Block() as block:
                p.block = block
                if s == 0:
                    @block.vector
                    def _(ve):
                        ve.memset(eps_t[:], LN_EPS)
                        ve.drain()
                _seq_io(p, block, x_sb, x_d, y_d, s, NT, True, consts, ident_dram)
                emit_ffn0(p, x_sb, wr_d, wk_d, wv_d, mur_d, muk_d, consts,
                          n_tok, f"s{s}")
                _seq_io(p, block, x_sb, x_d, y_d, s, NT, False)
    return nc




def build_moe_only(n_tok):
    nc = bass.Bass("TRN2", target_bir_lowering=False)
    NT = n_tok // 128
    x_d = nc.dram_tensor("x", [2, n_tok, D], F32, kind="ExternalInput").ap()
    wg_d = nc.dram_tensor("wg", [D, 8], F32, kind="ExternalInput").ap()
    w1_d = nc.dram_tensor("w1", [8, D, 4096], F32, kind="ExternalInput").ap()
    b1r_d = nc.dram_tensor("b1r", [128, 256], F32, kind="ExternalInput").ap()
    w2_d = nc.dram_tensor("w2", [8, 4096, D], F32, kind="ExternalInput").ap()
    b2_d = nc.dram_tensor("b2", [8, D], F32, kind="ExternalInput").ap()
    y_d = nc.dram_tensor("y", [2, n_tok, D], F32, kind="ExternalOutput").ap()

    with contextlib.ExitStack() as stack:
        p = P(nc, stack)
        ident_dram = nc.inline_tensor(np.eye(128, dtype=ml_dtypes.bfloat16), "identc").ap()
        identf_dram = nc.inline_tensor(np.eye(128, dtype=np.float32), "identfc").ap()
        ident = sbuf(p, "ident", [128, 128], BF16)
        identf = sbuf(p, "identf", [128, 128], F32)
        eps_t = sbuf(p, "epsc", [128, 1], F32)
        x_sb = sbuf(p, "xres", [128, NT * 1024], F32)
        consts = {"ident": ident, "identf": identf, "eps": eps_t[:, 0:1]}

        for s in range(2):
            with nc.Block() as block:
                p.block = block
                if s == 0:
                    @block.vector
                    def _(ve):
                        ve.memset(eps_t[:], LN_EPS)
                        ve.drain()
                    @block.sync
                    def _(sy):
                        p.inc(sy.dma_start(out=identf[:], in_=identf_dram), "dsync", 16)
                _seq_io(p, block, x_sb, x_d, y_d, s, NT, True, consts, ident_dram)
                emit_moe(p, x_sb, wg_d, w1_d, b1r_d, w2_d, b2_d, consts,
                         n_tok, f"s{s}", cast_w=True)
                _seq_io(p, block, x_sb, x_d, y_d, s, NT, False)
    return nc




def make_attn0_consts(nc, p):
    op = np.zeros((128, 2), dtype=ml_dtypes.bfloat16)
    op[0:64, 0] = 1.0
    op[64:128, 1] = 1.0
    op_d = nc.inline_tensor(op, "onespair_c").ap()
    onespair = sbuf(p, "onespair", [128, 2], BF16)
    masks = []
    masks_d = []
    for d in range(4):
        m = np.zeros((128, 512), dtype=ml_dtypes.bfloat16)
        for pp_ in range(128):
            j0 = d * 128 + pp_
            if j0 < 512:
                m[pp_, j0:] = 1.0
        masks_d.append(nc.inline_tensor(m, f"maskT{d}_c").ap())
        masks.append(sbuf(p, f"maskT{d}", [128, 512], BF16))
    onesrow = sbuf(p, "onesrow", [1, 128], F32)
    return op_d, onespair, masks_d, masks, onesrow


def build_attn0_only(n_tok):
    nc = bass.Bass("TRN2", target_bir_lowering=False)
    NT = n_tok // 128
    x_d = nc.dram_tensor("x", [2, n_tok, D], F32, kind="ExternalInput").ap()
    wattn_d = nc.dram_tensor("wattn", [D, 3 * D], F32, kind="ExternalInput").ap()
    wproj_d = nc.dram_tensor("wproj", [D, D], F32, kind="ExternalInput").ap()
    y_d = nc.dram_tensor("y", [2, n_tok, D], F32, kind="ExternalOutput").ap()

    with contextlib.ExitStack() as stack:
        p = P(nc, stack)
        ident_dram = nc.inline_tensor(np.eye(128, dtype=ml_dtypes.bfloat16), "identc").ap()
        identf_dram = nc.inline_tensor(np.eye(128, dtype=np.float32), "identfc").ap()
        ident = sbuf(p, "ident", [128, 128], BF16)
        identf = sbuf(p, "identf", [128, 128], F32)
        eps_t = sbuf(p, "epsc", [128, 1], F32)
        x_sb = sbuf(p, "xres", [128, NT * 1024], F32)
        op_d, onespair, masks_d, masks, onesrow = make_attn0_consts(nc, p)
        consts = {"ident": ident, "identf": identf, "eps": eps_t[:, 0:1],
                  "onespair": onespair[:], "onesrow": onesrow[:],
                  "masksT": [m[:] for m in masks]}

        for s in range(2):
            with nc.Block() as block:
                p.block = block
                if s == 0:
                    @block.vector
                    def _(ve):
                        ve.memset(eps_t[:], LN_EPS)
                        ve.drain()
                        ve.memset(onesrow[:], 1.0)
                        ve.drain()
                    @block.sync
                    def _(sy):
                        p.inc(sy.dma_start(out=identf[:], in_=identf_dram), "dsync", 16)
                        p.inc(sy.dma_start(out=onespair[:], in_=op_d), "dsync", 16)
                        for m_sb, m_d in zip(masks, masks_d):
                            p.inc(sy.dma_start(out=m_sb[:], in_=m_d), "dsync", 16)
                _seq_io(p, block, x_sb, x_d, y_d, s, NT, True, consts, ident_dram)
                emit_attn0(p, x_sb, wattn_d, wproj_d, consts, n_tok, f"s{s}")
                _seq_io(p, block, x_sb, x_d, y_d, s, NT, False)
    return nc




def build_attn1_only(n_tok):
    nc = bass.Bass("TRN2", target_bir_lowering=False)
    NT = n_tok // 128
    x_d = nc.dram_tensor("x", [2, n_tok, D], F32, kind="ExternalInput").ap()
    wqtp_d = nc.dram_tensor("wqtp", [D, D], F32, kind="ExternalInput").ap()
    wktp_d = nc.dram_tensor("wktp", [D, D], F32, kind="ExternalInput").ap()
    wvt_d = nc.dram_tensor("wvt", [D, D], F32, kind="ExternalInput").ap()
    cos_d = nc.dram_tensor("ropec", [512, n_tok], F32, kind="ExternalInput").ap()
    sin_d = nc.dram_tensor("ropes", [512, n_tok], F32, kind="ExternalInput").ap()
    y_d = nc.dram_tensor("y", [2, n_tok, D], F32, kind="ExternalOutput").ap()

    with contextlib.ExitStack() as stack:
        p = P(nc, stack)
        ident_dram = nc.inline_tensor(np.eye(128, dtype=ml_dtypes.bfloat16), "identc").ap()
        ident = sbuf(p, "ident", [128, 128], BF16)
        eps_t = sbuf(p, "epsc", [128, 1], F32)
        x_sb = sbuf(p, "xres", [128, NT * 1024], F32)
        masksA_np = []
        masksA_d = []
        masksA = []
        for dd in range(4):
            m = np.zeros((128, 512), dtype=ml_dtypes.bfloat16)
            for pp_ in range(128):
                j0 = dd * 128 + pp_
                if j0 + 1 < 512:
                    m[pp_, j0 + 1:] = -1e30
            masksA_d.append(nc.inline_tensor(m, f"maskA{dd}_c").ap())
            masksA.append(sbuf(p, f"maskA{dd}", [128, 512], BF16))
        consts = {"ident": ident, "eps": eps_t[:, 0:1],
                  "masksA": [m[:] for m in masksA]}

        for s in range(2):
            with nc.Block() as block:
                p.block = block
                if s == 0:
                    @block.vector
                    def _(ve):
                        ve.memset(eps_t[:], LN_EPS)
                        ve.drain()
                    @block.sync
                    def _(sy):
                        for m_sb, m_d in zip(masksA, masksA_d):
                            p.inc(sy.dma_start(out=m_sb[:], in_=m_d), "dsync", 16)
                _seq_io(p, block, x_sb, x_d, y_d, s, NT, True, consts, ident_dram)
                emit_attn1(p, x_sb, wqtp_d, wktp_d, wvt_d, cos_d, sin_d, consts,
                           n_tok, f"s{s}")
                _seq_io(p, block, x_sb, x_d, y_d, s, NT, False)
    return nc


def _rope_tables_host(n_tok):
    pos = np.arange(n_tok, dtype=np.float64)[:, None]
    inv = 1.0 / 10000.0 ** (np.arange(0, D, 2, dtype=np.float64) / D)
    fr = pos * inv
    cos = np.ascontiguousarray(np.cos(fr).T.astype(np.float32))
    sin = np.ascontiguousarray(np.sin(fr).T.astype(np.float32))
    return cos, sin


def kernel(**inputs):
    n_tok = 2048
    n_cores = 8
    x = np.ascontiguousarray(np.asarray(inputs["L0_attn_x"], dtype=np.float32))

    wrt = np.ascontiguousarray(np.asarray(inputs["L0_ffn_W_r"]).T)
    wkt = np.ascontiguousarray(np.asarray(inputs["L0_ffn_W_k"]).T)
    wvt = np.ascontiguousarray(np.asarray(inputs["L0_ffn_W_v"]).T)
    mur = np.ascontiguousarray(np.asarray(inputs["L0_ffn_mu_r"]))
    muk = np.ascontiguousarray(np.asarray(inputs["L0_ffn_mu_k"]))
    wattn = np.ascontiguousarray(np.asarray(inputs["L0_attn_w_attn"]))
    wproj = np.ascontiguousarray(np.asarray(inputs["L0_attn_w_proj"]))
    wg = np.ascontiguousarray(np.asarray(inputs["L1_ffn_W_gate"]))
    w1 = np.ascontiguousarray(np.asarray(inputs["L1_ffn_W1"]))
    b1 = np.asarray(inputs["L1_ffn_b1"])
    b1r = np.ascontiguousarray(b1.reshape(8, 32, 128).transpose(2, 0, 1).reshape(128, 256))
    w2 = np.ascontiguousarray(np.asarray(inputs["L1_ffn_W2"]))
    b2 = np.ascontiguousarray(np.asarray(inputs["L1_ffn_b2"]))
    perm = np.concatenate([np.arange(0, D, 2), np.arange(1, D, 2)])
    wqtp = np.ascontiguousarray(np.asarray(inputs["L1_attn_wq"]).T[:, perm])
    wktp = np.ascontiguousarray(np.asarray(inputs["L1_attn_wk"]).T[:, perm])
    wvt1 = np.ascontiguousarray(np.asarray(inputs["L1_attn_wv"]).T)
    cosf, sinf = _rope_tables_host(n_tok)

    shards = [np.ascontiguousarray(x[2 * c: 2 * c + 2]) for c in range(n_cores)]

    nc_ffn = build_ffn0_only(n_tok)
    nc_a0 = build_attn0_only(n_tok)
    nc_moe = build_moe_only(n_tok)
    nc_a1 = build_attn1_only(n_tok)

    def launch(nc, extra):
        nonlocal shards
        in_maps = [dict(extra, x=shards[c]) for c in range(n_cores)]
        res = run_bass_kernel_spmd(nc, in_maps, core_ids=list(range(n_cores)))
        shards = [np.asarray(r["y"]) for r in res.results]

    ffn_w = {"wrt": wrt, "wkt": wkt, "wvt": wvt, "mur": mur, "muk": muk}
    a0_w = {"wattn": wattn, "wproj": wproj}
    moe_w = {"wg": wg, "w1": w1, "b1r": b1r, "w2": w2, "b2": b2}
    a1_w = {"wqtp": wqtp, "wktp": wktp, "wvt": wvt1, "ropec": cosf, "ropes": sinf}

    launch(nc_ffn, ffn_w)
    launch(nc_a0, a0_w)
    launch(nc_ffn, ffn_w)
    launch(nc_moe, moe_w)
    launch(nc_a1, a1_w)
    launch(nc_moe, moe_w)

    return np.concatenate(shards, axis=0)



# version 03
# speedup vs baseline: 1.1526x; 1.1526x over previous
import contextlib
import numpy as np
import ml_dtypes
import concourse.bass as bass
import concourse.mybir as mybir
from concourse.bass_utils import run_bass_kernel_spmd

F32 = mybir.dt.float32
BF16 = mybir.dt.bfloat16
AF = mybir.ActivationFunctionType
ALU = mybir.AluOpType
AX = mybir.AxisListType

D = 1024
CD = D // 128
LN_EPS = 1e-5



class P:

    def __init__(self, nc, stack):
        self.nc = nc
        self.stack = stack
        self.root_stack = stack
        self.block = None
        self._sems = {}
        self.cnt = {}
        self.bku = {}
        self.uid = [0]

    def fresh(self, stack):
        q = P.__new__(P)
        q.__dict__.update(self.__dict__)
        q.stack = stack
        return q

    def sem(self, name):
        if name not in self._sems:
            self._sems[name] = self.root_stack.enter_context(self.nc.semaphore(name))
            self.cnt[name] = 0
        return self._sems[name]

    def inc(self, inst, name, n=1):
        inst.then_inc(self.sem(name), n)
        self.cnt[name] += n
        return self.cnt[name]

    def val(self, name):
        self.sem(name)
        return self.cnt[name]

    def wait(self, eng, name, v=None):
        eng.wait_ge(self.sem(name), self.cnt[name] if v is None else v)

    def fence(self, tag="ph"):
        name = "fence"
        self.sem(name)
        target = self.cnt[name] + 3

        @self.block.tensor
        def _(te):
            self.inc(te.drain(), name, 1)
            te.wait_ge(self._sems[name], target)

        @self.block.vector
        def _(ve):
            self.inc(ve.drain(), name, 1)
            ve.wait_ge(self._sems[name], target)

        @self.block.scalar
        def _(se):
            self.inc(se.drain(), name, 1)
            se.wait_ge(self._sems[name], target)
        assert self.cnt[name] == target


class Bank:

    def __init__(self, p, tensor, name):
        self.p = p
        self.t = tensor
        self.name = "bk_" + name
        p.sem(self.name)
        p.bku.setdefault(self.name, 0)

    @property
    def uses(self):
        return self.p.bku[self.name]

    def wait_free(self, eng):
        if self.uses > 0:
            eng.wait_ge(self.p.sem(self.name), self.uses)
        self.p.bku[self.name] += 1

    def release(self, inst):
        self.p.inc(inst, self.name, 1)

    def wait_released(self, eng):
        eng.wait_ge(self.p.sem(self.name), self.uses)


def sbuf(p, name, shape, dtype):
    p.uid[0] += 1
    return p.stack.enter_context(p.nc.sbuf_tensor(f"{name}_{p.uid[0]}", shape, dtype))


def psum_bank(p, name, shape, dtype=F32):
    p.uid[0] += 1
    t = p.stack.enter_context(p.nc.psum_tensor(f"{name}_{p.uid[0]}", shape, dtype))
    return Bank(p, t, name)



class LN:

    def __init__(self, p, tag, eps_ap):
        self.p = p
        self.eps = eps_ap
        self.stats = sbuf(p, "lnst", [128, 12], F32)
        self.mv = sbuf(p, "lnmv", [128, 2], F32)
        self.lnv = sbuf(p, "lnlv", [128, 1], F32)
        self.rstd = sbuf(p, "lnrs", [128, 1], F32)
        self.s_v2s = "lnv2s"
        self.s_s2v = "lns2v"
        self.rdy = "lnrdy"
        p.sem(self.rdy)

    def emit(self, x_view, out_view):
        p = self.p

        @p.block.vector
        def _(ve):
            if p.val(self.s_s2v) > 0:
                p.wait(ve, self.s_s2v)
            ve.bn_stats(self.stats[:, 0:6], x_view[:, 0:512])
            ve.bn_stats(self.stats[:, 6:12], x_view[:, 512:1024])
            ve.drain()
            p.inc(ve.bn_aggr(self.mv[:], self.stats[:]), self.s_v2s, 1)

        @p.block.scalar
        def _(se):
            p.wait(se, self.s_v2s)
            se.activation(self.lnv[:], self.mv[:, 1:2], AF.Ln, bias=self.eps)
            se.drain()
            p.inc(se.activation(self.rstd[:], self.lnv[:], AF.Exp, scale=-0.5),
                  self.s_s2v, 1)

        @p.block.vector
        def _(ve):
            p.wait(ve, self.s_s2v)
            p.inc(ve.tensor_scalar(out_view, x_view, self.mv[:, 0:1], self.rstd[:, 0:1],
                                   ALU.subtract, ALU.mult), self.rdy, 1)



def emit_ffn0(p, x_sb, wr_d, wk_d, wv_d, mur_d, muk_d, consts, n_tok, tag):
    NTILE = n_tok // 512
    ident = consts["ident"]

    with contextlib.ExitStack() as st:
        pp = p.fresh(st)

        wr = sbuf(pp, "fwr", [128, CD * 1024], BF16)
        wk = sbuf(pp, "fwk", [128, CD * 1024], BF16)
        wv = sbuf(pp, "fwv", [128, CD * 1024], BF16)
        mur = sbuf(pp, "fmur", [128, CD], F32)
        muk = sbuf(pp, "fmuk", [128, CD], F32)
        xh = sbuf(pp, "fxh", [128, CD * 512], BF16)
        xs = sbuf(pp, "fxs", [128, CD * 512], BF16)
        carry = sbuf(pp, "fcar", [128, CD], BF16)
        diff = sbuf(pp, "fdif", [128, CD * 512], BF16)
        xr = sbuf(pp, "fxr", [128, CD * 512], BF16)
        xk = sbuf(pp, "fxk", [128, CD * 512], BF16)
        kact = sbuf(pp, "fka", [128, CD * 512], BF16)
        krelu = [sbuf(pp, f"fkr{i}", [128, 512], F32) for i in range(2)]
        r_sb = sbuf(pp, "fr", [128, 4 * 1024], F32)
        t1 = sbuf(pp, "ft1", [128, 512], F32)
        xhat = [sbuf(pp, f"fxt{i}", [128, 1024], BF16) for i in range(2)]

        tr_banks = [psum_bank(pp, f"ftr{i}", [128, 128], BF16) for i in range(2)]
        g_banks = [psum_bank(pp, f"fg{i}", [128, 512], F32) for i in range(4)]

        ln = LN(pp, "f", consts["eps"])
        s_tr = "ftr"

        @pp.block.gpsimd
        def _(g):
            for w_sb, w_d in ((wr, wr_d), (wk, wk_d), (wv, wv_d)):
                o3 = w_sb[:].rearrange("p (c d) -> p c d", c=CD)
                i3 = w_d.rearrange("(c p) d -> p c d", p=128)
                pp.inc(g.dma_start(out=o3, in_=i3), "dpool", 16)
            pp.inc(g.dma_start(out=mur[:], in_=mur_d.rearrange("(c p) -> p c", p=128),
                               allow_slow_non_contiguous=True), "dpool", 16)
            pp.inc(g.dma_start(out=muk[:], in_=muk_d.rearrange("(c p) -> p c", p=128),
                               allow_slow_non_contiguous=True), "dpool", 16)
        w_ready = pp.val("dpool")

        for tile in range(NTILE):
            for j in range(4):
                tc = tile * 4 + j
                xht = xhat[j % 2]
                if j >= 2:

                    @pp.block.vector
                    def _(ve):
                        pp.wait(ve, s_tr, pp.cnt.get(s_tr, 0) - CD)
                ln.emit(x_sb[:, tc * 1024:(tc + 1) * 1024], xht[:])
                ln_n = pp.val(ln.rdy)
                for c in range(CD):
                    bank = tr_banks[c % 2]

                    @pp.block.tensor
                    def _(te):
                        pp.wait(te, ln.rdy, ln_n)
                        bank.wait_free(te)
                        pp.inc(te.transpose(bank.t[:, :], xht[:, c * 128:(c + 1) * 128],
                                            ident[:]), s_tr, 1)

                    @pp.block.vector
                    def _(ve):
                        pp.wait(ve, s_tr)
                        col = j * 128
                        ve.tensor_copy(xh[:, c * 512 + col: c * 512 + col + 128], bank.t[:, :])
                        ve.drain()
                        ncols = 127 if j == 3 else 128
                        ev = ve.tensor_copy(xs[:, c * 512 + col + 1: c * 512 + col + 1 + ncols],
                                            bank.t[:, 0:ncols])
                        bank.release(ev)

            @pp.block.vector
            def _(ve):
                if tile == 0:
                    ve.memset(carry[:], 0.0)
                    ve.drain()
                for c in range(CD):
                    ve.tensor_copy(xs[:, c * 512: c * 512 + 1], carry[:, c:c + 1])
                ve.drain()
                for c in range(CD):
                    ve.tensor_copy(carry[:, c:c + 1], xh[:, c * 512 + 511: c * 512 + 512])
            pp.fence()

            @pp.block.vector
            def _(ve):
                if tile == 0:
                    pp.wait(ve, "dpool", w_ready)
                ve.tensor_sub(diff[:], xh[:], xs[:])
                ve.drain()
                for c in range(CD):
                    ve.scalar_tensor_tensor(xr[:, c * 512:(c + 1) * 512],
                                            diff[:, c * 512:(c + 1) * 512],
                                            mur[:, c:c + 1],
                                            xs[:, c * 512:(c + 1) * 512],
                                            ALU.mult, ALU.add)
                    ve.scalar_tensor_tensor(xk[:, c * 512:(c + 1) * 512],
                                            diff[:, c * 512:(c + 1) * 512],
                                            muk[:, c:c + 1],
                                            xs[:, c * 512:(c + 1) * 512],
                                            ALU.mult, ALU.add)
            pp.fence()

            for j in range(4):
                for dh in range(2):
                    bank = g_banks[(j * 2 + dh) % 4]

                    @pp.block.tensor
                    def _(te):
                        if tile == 0 and j == 0 and dh == 0:
                            pp.wait(te, "dpool", w_ready)
                        bank.wait_free(te)
                        for c in range(CD):
                            mm = te.matmul(bank.t[:, :],
                                           xr[:, c * 512 + j * 128: c * 512 + j * 128 + 128],
                                           wr[:, c * 1024 + dh * 512: c * 1024 + dh * 512 + 512],
                                           start=(c == 0), stop=(c == CD - 1))
                        pp.inc(mm, "frmm", 1)

                    @pp.block.scalar
                    def _(se):
                        pp.wait(se, "frmm")
                        ev = se.activation(r_sb[:, j * 1024 + dh * 512: j * 1024 + dh * 512 + 512],
                                           bank.t[:, :], AF.Sigmoid)
                        bank.release(ev)

            for dk in range(CD):
                bank = g_banks[dk % 4]
                kr = krelu[dk % 2]

                @pp.block.tensor
                def _(te):
                    bank.wait_free(te)
                    for c in range(CD):
                        mm = te.matmul(bank.t[:, :],
                                       wk[:, c * 1024 + dk * 128: c * 1024 + dk * 128 + 128],
                                       xk[:, c * 512:(c + 1) * 512],
                                       start=(c == 0), stop=(c == CD - 1))
                    pp.inc(mm, "fkmm", 1)

                @pp.block.vector
                def _(ve):
                    pp.wait(ve, "fkmm")
                    if pp.val("fkev") >= 2:
                        pp.wait(ve, "fkev", pp.cnt["fkev"] - 1)
                    mx = ve.tensor_scalar(kr[:, :], bank.t[:, :], 0.0, None, ALU.max)
                    bank.release(mx)

                @pp.block.scalar
                def _(se):
                    bank.wait_released(se)
                    pp.inc(se.activation(kact[:, dk * 512:(dk + 1) * 512], kr[:, :],
                                         AF.Square), "fkev", 1)

            pp.fence()

            for j in range(4):
                for dh in range(2):
                    bank = g_banks[(j * 2 + dh) % 4]

                    @pp.block.tensor
                    def _(te):
                        bank.wait_free(te)
                        for dk in range(CD):
                            mm = te.matmul(bank.t[:, :],
                                           kact[:, dk * 512 + j * 128: dk * 512 + j * 128 + 128],
                                           wv[:, dk * 1024 + dh * 512: dk * 1024 + dh * 512 + 512],
                                           start=(dk == 0), stop=(dk == CD - 1))
                        pp.inc(mm, "fomm", 1)

                    @pp.block.vector
                    def _(ve):
                        pp.wait(ve, "fomm")
                        tc = tile * 4 + j
                        xv = x_sb[:, tc * 1024 + dh * 512: tc * 1024 + dh * 512 + 512]
                        stt = ve.scalar_tensor_tensor(t1[:, :], bank.t[:, :], 0.5,
                                                      r_sb[:, j * 1024 + dh * 512: j * 1024 + dh * 512 + 512],
                                                      ALU.mult, ALU.mult)
                        bank.release(stt)
                        ve.drain()
                        ve.tensor_add(xv, xv, t1[:, :])
                        ve.drain()

            pp.fence()

        p.cnt.update(pp.cnt)





def emit_xhat_fm(pp, ln, x_sb, tile, xhat_fm, xhat_bufs, tr_banks, ident, s_tr):
    for j in range(4):
        tc = tile * 4 + j
        xht = xhat_bufs[j % 2]
        if j >= 2:
            @pp.block.vector
            def _(ve):
                pp.wait(ve, s_tr, pp.val(s_tr) - CD)
        ln.emit(x_sb[:, tc * 1024:(tc + 1) * 1024], xht[:])
        ln_n = pp.val(ln.rdy)
        for c in range(CD):
            bank = tr_banks[c % len(tr_banks)]

            @pp.block.tensor
            def _(te):
                pp.wait(te, ln.rdy, ln_n)
                bank.wait_free(te)
                pp.inc(te.transpose(bank.t[:, :], xht[:, c * 128:(c + 1) * 128],
                                    ident[:]), s_tr, 1)

            @pp.block.vector
            def _(ve):
                pp.wait(ve, s_tr)
                ev = ve.tensor_copy(xhat_fm[:, c * 512 + j * 128: c * 512 + j * 128 + 128],
                                    bank.t[:, :])
                bank.release(ev)

    @pp.block.vector
    def _(ve):
        pp.inc(ve.drain(), "xfdone", 1)
    return pp.val("xfdone")



def emit_moe(p, x_sb, wg_d, w1_d, b1r_d, w2_d, b2_d, consts, n_tok, tag,
             cast_w=True):
    NTILE = n_tok // 512
    ident = consts["ident"]

    with contextlib.ExitStack() as st:
        pp = p.fresh(st)

        wg = sbuf(pp, "mwg", [128, CD * 8], F32)
        b1r = sbuf(pp, "mb1", [128, 256], F32)
        b2 = sbuf(pp, "mb2", [8, 1024], F32)
        xhat_fm = sbuf(pp, "mxf", [128, CD * 512], BF16)
        xhat = [sbuf(pp, f"mxt{i}", [128, 1024], BF16) for i in range(2)]
        xhf32 = sbuf(pp, "mxg", [128, 1024], F32)
        xf32c = sbuf(pp, "mxgc", [128, CD * 128], F32)
        w1 = sbuf(pp, "mw1", [128, CD * 4096], BF16)
        w2 = [sbuf(pp, f"mw2{i}", [128, 4 * 512], BF16) for i in range(2)]
        h_sb = sbuf(pp, "mh", [128, 32 * 512], BF16)
        e8 = sbuf(pp, "me8", [128, 8], F32)
        ge8 = sbuf(pp, "mge", [128, 8], F32)
        tmp8 = sbuf(pp, "mt8", [128, 8], F32)
        msk2 = sbuf(pp, "mm2", [128, 8], F32)
        m1n = sbuf(pp, "mm1n", [128, 1], F32)
        m2n = sbuf(pp, "mm2n", [128, 1], F32)
        lsum = sbuf(pp, "mls", [128, 1], F32)
        lrc = sbuf(pp, "mlr", [128, 1], F32)
        maskp = sbuf(pp, "mmp", [128, 4 * 8], F32)
        mtT = sbuf(pp, "mmt", [8, 4 * 128], F32)

        tr_banks = [psum_bank(pp, f"mtr{i}", [128, 128], BF16) for i in range(1)]
        h_banks = [psum_bank(pp, f"mhb{i}", [128, 512], F32) for i in range(2)]
        y_banks = [psum_bank(pp, f"myb{i}", [128, 512], F32) for i in range(4)]
        sm_bank = psum_bank(pp, "msm", [128, 512], F32)

        ln = LN(pp, "m", consts["eps"])
        s_tr = "mtr"
        dqeng = "gpsimd" if cast_w else "sync"

        @pp.block.gpsimd
        def _(g):
            pp.inc(g.dma_start(out=wg[:].rearrange("p (c e) -> p c e", c=CD),
                               in_=wg_d.rearrange("(c p) e -> p c e", p=128)), "dwg", 16)
        @pp.block.sync
        def _(sy):
            pp.inc(sy.dma_start(out=b1r[:], in_=b1r_d), "dsync2", 16)
            pp.inc(sy.dma_start(out=b2[:], in_=b2_d), "dsync2", 16)
        smalls_g = pp.val("dwg")
        smalls_s = pp.val("dsync2")

        def load_w1(e):
            eng = getattr(pp.block, dqeng)

            @eng
            def _(g):
                if pp.val("mw1f") > 0:
                    pp.wait(g, "mw1f")
                o3 = w1[:].rearrange("p (c h) -> p c h", c=CD)
                i3 = w1_d[e].rearrange("(c p) h -> p c h", p=128)
                pp.inc(g.dma_start(out=o3, in_=i3), "dw1", 16)
            return pp.val("dw1")

        def load_w2(e, dh, hq, buf):
            eng = getattr(pp.block, dqeng)

            @eng
            def _(g):
                if pp.val(f"mw2f{buf}") > 0:
                    pp.wait(g, f"mw2f{buf}")
                o3 = w2[buf][:].rearrange("p (q d) -> p q d", q=4)
                i3 = w2_d[e][hq * 512:(hq + 1) * 512,
                             dh * 512:(dh + 1) * 512].rearrange("(q p) d -> p q d", p=128)
                pp.inc(g.dma_start(out=o3, in_=i3), f"dw2{buf}", 16)
            return pp.val(f"dw2{buf}")

        w1_rdy = load_w1(0)

        for tile in range(NTILE):
            for j in range(4):
                tc = tile * 4 + j
                xht = xhat[j % 2]
                if j >= 2:
                    @pp.block.vector
                    def _(ve):
                        pp.wait(ve, s_tr, pp.val(s_tr) - 2 * CD)
                ln.emit(x_sb[:, tc * 1024:(tc + 1) * 1024], xht[:])
                ln_n = pp.val(ln.rdy)
                @pp.block.vector
                def _(ve):
                    if pp.val(s_tr) > 0:
                        pp.wait(ve, s_tr)
                    pp.inc(ve.tensor_scalar(xhf32[:], x_sb[:, tc * 1024:(tc + 1) * 1024],
                                            ln.mv[:, 0:1], ln.rstd[:, 0:1],
                                            ALU.subtract, ALU.mult), "mf32", 1)
                f32_n = pp.val("mf32")
                for c in range(CD):
                    bank = tr_banks[0]

                    @pp.block.tensor
                    def _(te):
                        pp.wait(te, ln.rdy, ln_n)
                        bank.wait_free(te)
                        pp.inc(te.transpose(bank.t[:, :], xht[:, c * 128:(c + 1) * 128],
                                            ident[:]), s_tr, 1)

                    @pp.block.vector
                    def _(ve):
                        pp.wait(ve, s_tr)
                        ev = ve.tensor_copy(xhat_fm[:, c * 512 + j * 128: c * 512 + j * 128 + 128],
                                            bank.t[:, :])
                        bank.release(ev)
                for c in range(CD):
                    @pp.block.tensor
                    def _(te):
                        pp.wait(te, "mf32", f32_n)
                        sm_bank.wait_free(te)
                        pp.inc(te.transpose(sm_bank.t[0:128, 0:128],
                                            xhf32[:, c * 128:(c + 1) * 128],
                                            consts["identf"][:]), s_tr, 1)

                    @pp.block.vector
                    def _(ve):
                        pp.wait(ve, s_tr)
                        ev = ve.tensor_copy(xf32c[:, c * 128:(c + 1) * 128], sm_bank.t[0:128, 0:128])
                        sm_bank.release(ev)

                @pp.block.tensor
                def _(te):
                    if tile == 0 and j == 0:
                        pp.wait(te, "dwg", smalls_g)
                        pp.wait(te, "dsync2", smalls_s)
                    sm_bank.wait_free(te)
                    for c in range(CD):
                        mm = te.matmul(sm_bank.t[:, 0:8],
                                       xf32c[:, c * 128:(c + 1) * 128],
                                       wg[:, c * 8:(c + 1) * 8],
                                       start=(c == 0), stop=(c == CD - 1))
                    pp.inc(mm, "mgmm", 1)

                @pp.block.vector
                def _(ve):
                    pp.wait(ve, "mgmm")
                    gps = sm_bank.t[:, 0:8]
                    ve.reduce_max(m1n[:], gps, axis=AX.X, negate=True)
                    ve.drain()
                    ve.tensor_scalar(ge8[:], gps, m1n[:, 0:1], 0.0, ALU.add, ALU.is_ge)
                    ve.drain()
                    ve.tensor_scalar(tmp8[:], ge8[:], -1e30, None, ALU.mult)
                    ve.drain()
                    ve.tensor_add(msk2[:], gps, tmp8[:])
                    ve.drain()
                    ve.reduce_max(m2n[:], msk2[:], axis=AX.X, negate=True)
                    ve.drain()
                    pp.inc(ve.tensor_scalar(ge8[:], gps, m2n[:, 0:1], 0.0, ALU.add, ALU.is_ge),
                           "mdv", 1)

                @pp.block.scalar
                def _(se):
                    pp.wait(se, "mdv")
                    pp.inc(se.activation(e8[:], sm_bank.t[:, 0:8], AF.Exp, bias=m1n[:, 0:1]),
                           "mex", 1)

                @pp.block.vector
                def _(ve):
                    pp.wait(ve, "mex")
                    em = ve.tensor_mul(e8[:], e8[:], ge8[:])
                    sm_bank.release(em)
                    ve.drain()
                    ve.reduce_sum(lsum[:], e8[:], axis=AX.X)
                    ve.drain()
                    ve.reciprocal(lrc[:], lsum[:])
                    ve.drain()
                    pp.inc(ve.tensor_scalar(maskp[:, j * 8:(j + 1) * 8], e8[:],
                                            lrc[:, 0:1], 0.0625, ALU.mult, ALU.mult),
                           "mmp", 1)

            @pp.block.scalar
            def _(se):
                pp.inc(se.drain(), "xfdone", 1)
            xf_n = pp.val("xfdone")

            pp.fence()
            mp_n = pp.val("mmp")
            for j in range(4):
                @pp.block.tensor
                def _(te):
                    pp.wait(te, "mmp", mp_n)
                    sm_bank.wait_free(te)
                    pp.inc(te.transpose(sm_bank.t[0:8, 0:128],
                                        maskp[:, j * 8:(j + 1) * 8], consts["identf"][:]),
                           "mtp", 1)

                @pp.block.vector
                def _(ve):
                    pp.wait(ve, "mtp")
                    ev = ve.tensor_copy(mtT[0:8, j * 128:(j + 1) * 128], sm_bank.t[0:8, 0:128])
                    sm_bank.release(ev)
            for j in range(4):
                for dh in range(2):
                    @pp.block.tensor
                    def _(te):
                        sm_bank.wait_free(te)
                        pp.inc(te.matmul(sm_bank.t[:, 0:512],
                                         mtT[0:8, j * 128:(j + 1) * 128],
                                         b2[0:8, dh * 512:(dh + 1) * 512],
                                         start=True, stop=True), "mb2m", 1)

                    @pp.block.vector
                    def _(ve):
                        pp.wait(ve, "mb2m")
                        tc = tile * 4 + j
                        xv = x_sb[:, tc * 1024 + dh * 512: tc * 1024 + dh * 512 + 512]
                        ad = ve.tensor_add(xv, xv, sm_bank.t[:, 0:512])
                        sm_bank.release(ad)
                        ve.drain()

            for e in range(8):
                for hc in range(32):
                    bank = h_banks[hc % 2]

                    @pp.block.tensor
                    def _(te):
                        if hc == 0:
                            pp.wait(te, "dw1", w1_rdy)
                            pp.wait(te, "xfdone", xf_n)
                        bank.wait_free(te)
                        for c in range(CD):
                            mm = te.matmul(bank.t[:, :],
                                           w1[:, c * 4096 + hc * 128: c * 4096 + hc * 128 + 128],
                                           xhat_fm[:, c * 512:(c + 1) * 512],
                                           start=(c == 0), stop=(c == CD - 1))
                        pp.inc(mm, "mh1", 1)

                    @pp.block.scalar
                    def _(se):
                        pp.wait(se, "mh1")
                        ev = se.activation(h_sb[:, hc * 512:(hc + 1) * 512], bank.t[:, :],
                                           AF.Relu, bias=b1r[:, e * 32 + hc: e * 32 + hc + 1])
                        bank.release(ev)

                h_rel = [(b, b.uses) for b in h_banks]

                @pp.block.tensor
                def _(te):
                    pp.inc(te.drain(), "mw1f", 1)
                if not (e == 7 and tile == NTILE - 1):
                    w1_rdy = load_w1((e + 1) % 8)

                seq = [(dh, hq) for dh in (0, 1) for hq in range(8)]
                rdy = {}
                rdy[(0, 0)] = load_w2(e, 0, 0, 0)
                rdy[(0, 1)] = load_w2(e, 0, 1, 1)
                for idx, (dh, hq) in enumerate(seq):
                    buf = idx % 2

                    @pp.block.tensor
                    def _(te):
                        pp.wait(te, f"dw2{idx % 2}", rdy[(dh, hq)])
                        if hq == 0:
                            for j in range(4):
                                y_banks[j].wait_free(te)
                            if idx == 0:
                                for b, u in h_rel:
                                    te.wait_ge(pp.sem(b.name), u)
                        for hc4 in range(4):
                            hc = hq * 4 + hc4
                            for j in range(4):
                                mm = te.matmul(y_banks[j].t[:, :],
                                               h_sb[:, hc * 512 + j * 128: hc * 512 + j * 128 + 128],
                                               w2[buf][:, hc4 * 512:(hc4 + 1) * 512],
                                               start=(hq == 0 and hc4 == 0),
                                               stop=(hq == 7 and hc4 == 3))
                        pp.inc(mm, f"mw2f{buf}", 1)
                        if hq == 7:
                            pp.inc(te.drain(), "my2", 1)

                    if idx + 2 < 16:
                        nd, nh = seq[idx + 2]
                        rdy[(nd, nh)] = load_w2(e, nd, nh, buf)

                    if hq == 7:
                        y_done = pp.val("my2")

                        @pp.block.vector
                        def _(ve):
                            pp.wait(ve, "my2", y_done)
                            for j in range(4):
                                tc = tile * 4 + j
                                xv = x_sb[:, tc * 1024 + dh * 512: tc * 1024 + dh * 512 + 512]
                                stt = ve.scalar_tensor_tensor(xv, y_banks[j].t[:, :],
                                                              maskp[:, j * 8 + e: j * 8 + e + 1],
                                                              xv, ALU.mult, ALU.add)
                                y_banks[j].release(stt)
                                ve.drain()

            pp.fence()

        p.cnt.update(pp.cnt)





def emit_attn0(p, x_sb, wattn_d, wproj_d, consts, n_tok, tag):
    NTILE = n_tok // 512
    NT = n_tok // 128
    ident = consts["ident"]
    identf = consts["identf"]

    with contextlib.ExitStack() as st:
        pp = p.fresh(st)

        K_sb = sbuf(pp, "aK", [128, CD * n_tok], BF16)
        V_sb = sbuf(pp, "aV", [128, NT * 1040], BF16)
        knm = sbuf(pp, "akm", [1, 16], F32)
        knt = sbuf(pp, "akt", [1, 1], F32)
        qn2m = sbuf(pp, "aqm", [1, 16], F32)
        mhn = sbuf(pp, "amh", [1, 16], F32)
        prod = sbuf(pp, "apr", [1, 16], F32)
        xhat_fm = sbuf(pp, "axf", [128, CD * 512], BF16)
        xhat = [sbuf(pp, f"axt{i}", [128, 1024], BF16) for i in range(2)]
        sq = sbuf(pp, "asq", [128, 512], BF16)
        mhb = sbuf(pp, "amb", [128, 1], F32)
        onesrow = consts["onesrow"]
        onespair = consts["onespair"]
        masksT = consts["masksT"]

        tr_banks = [psum_bank(pp, "atr", [128, 128], BF16)]
        s_banks = [psum_bank(pp, f"as{i}", [128, 512], F32) for i in range(2)]
        av_bank = psum_bank(pp, "aav", [128, 512], F32)
        sm_bank = psum_bank(pp, "asm", [128, 512], F32)

        ln = LN(pp, "a", consts["eps"])
        s_tr = "atr"

        with contextlib.ExitStack() as st1:
            p1 = pp.fresh(st1)
            wak = sbuf(p1, "awk", [128, CD * 1024], BF16)
            wav = sbuf(p1, "awv", [128, CD * 1024], BF16)

            @p1.block.gpsimd
            def _(g):
                for w_sb, lo in ((wak, 0), (wav, 2048)):
                    o3 = w_sb[:].rearrange("p (c d) -> p c d", c=CD)
                    i3 = wattn_d[:, lo:lo + 1024].rearrange("(c p) d -> p c d", p=128)
                    p1.inc(g.dma_start(out=o3, in_=i3), "adw1", 16)
            w1_rdy = p1.val("adw1")

            @p1.block.vector
            def _(ve):
                ve.memset(V_sb[:], 1.0)
                ve.drain()
                ve.memset(knm[:], 0.0)
                p1.inc(ve.drain(), "avini", 1)
            vini = p1.val("avini")

            for tile in range(NTILE):
                xf_n = emit_xhat_fm(p1, ln, x_sb, tile, xhat_fm, xhat, tr_banks, ident, s_tr)

                for ch in range(CD):
                    bank = s_banks[ch % 2]

                    @p1.block.tensor
                    def _(te):
                        if tile == 0 and ch == 0:
                            p1.wait(te, "adw1", w1_rdy)
                        p1.wait(te, "xfdone", xf_n)
                        bank.wait_free(te)
                        for c in range(CD):
                            mm = te.matmul(bank.t[:, :],
                                           wak[:, c * 1024 + ch * 128: c * 1024 + ch * 128 + 128],
                                           xhat_fm[:, c * 512:(c + 1) * 512],
                                           start=(c == 0), stop=(c == CD - 1))
                        p1.inc(mm, "akmm", 1)

                    @p1.block.vector
                    def _(ve):
                        p1.wait(ve, "akmm")
                        ev = ve.tensor_copy(K_sb[:, ch * n_tok + tile * 512: ch * n_tok + tile * 512 + 512],
                                            bank.t[:, :])
                        bank.release(ev)

                    @p1.block.scalar
                    def _(se):
                        p1.wait(se, "akmm")
                        p1.inc(se.activation(sq[:, :], bank.t[:, :], AF.Square), "aksq", 1)

                    for par in range(2):
                        @p1.block.tensor
                        def _(te):
                            p1.wait(te, "aksq")
                            sm_bank.wait_free(te)
                            p1.inc(te.matmul(sm_bank.t[0:1, 0:512],
                                             onespair[:, par:par + 1], sq[:, :],
                                             start=True, stop=True), "aknm", 1)

                        @p1.block.vector
                        def _(ve):
                            p1.wait(ve, "aknm")
                            h = 2 * ch + par
                            mx = ve.reduce_max(knt[0:1, 0:1], sm_bank.t[0:1, 0:512], axis=AX.X)
                            sm_bank.release(mx)
                            ve.drain()
                            ve.tensor_max(knm[0:1, h:h + 1], knm[0:1, h:h + 1], knt[0:1, 0:1])
                            ve.drain()

                for j in range(4):
                    tc = tile * 4 + j
                    for vh in range(2):
                        bank = s_banks[vh % 2]

                        @p1.block.tensor
                        def _(te):
                            bank.wait_free(te)
                            for c in range(CD):
                                mm = te.matmul(bank.t[:, :],
                                               xhat_fm[:, c * 512 + j * 128: c * 512 + j * 128 + 128],
                                               wav[:, c * 1024 + vh * 512: c * 1024 + vh * 512 + 512],
                                               start=(c == 0), stop=(c == CD - 1))
                            p1.inc(mm, "avmm", 1)

                        @p1.block.vector
                        def _(ve):
                            p1.wait(ve, "avmm")
                            if tile == 0 and j == 0 and vh == 0:
                                p1.wait(ve, "avini", vini)
                            ov = V_sb[:, tc * 1040 + vh * 520: tc * 1040 + vh * 520 + 520]
                            ov = ov.rearrange("p (h i) -> p h i", h=8)[:, :, 0:64]
                            iv = bank.t[:, :].rearrange("p (h i) -> p h i", h=8)
                            ev = ve.tensor_copy(ov, iv)
                            bank.release(ev)
                pp.fence()

            pp.fence()
            pp.cnt.update(p1.cnt)

        with contextlib.ExitStack() as st2:
            p2 = pp.fresh(st2)
            waq = sbuf(p2, "awq", [128, CD * 1024], BF16)
            wap = sbuf(p2, "awp", [128, CD * 512], BF16)
            Qh = sbuf(p2, "aQ", [128, 16 * 512], BF16)
            STx = [sbuf(p2, f"aST{i}", [128, 512], BF16) for i in range(2)]
            y_sb = sbuf(p2, "ay", [128, CD * 512], BF16)
            rl = sbuf(p2, "arl", [1, 512], F32)
            lbc = sbuf(p2, "albc", [64, 512], F32)

            @p2.block.gpsimd
            def _(g):
                p2.wait(g, "adw1")
                p2.wait(g, "fence")
                o3 = waq[:].rearrange("p (c d) -> p c d", c=CD)
                i3 = wattn_d[:, 1024:2048].rearrange("(c p) d -> p c d", p=128)
                p2.inc(g.dma_start(out=o3, in_=i3), "adw2", 16)
            w2_rdy = p2.val("adw2")

            def load_wap(dh):
                @p2.block.gpsimd
                def _(g):
                    if p2.val("awapf") > 0:
                        p2.wait(g, "awapf")
                    o3 = wap[:].rearrange("p (c d) -> p c d", c=CD)
                    i3 = wproj_d[:, dh * 512:(dh + 1) * 512].rearrange("(c p) d -> p c d", p=128)
                    p2.inc(g.dma_start(out=o3, in_=i3), "adwp", 16)
                return p2.val("adwp")

            for qt in range(NTILE):
                xf_n = emit_xhat_fm(p2, ln, x_sb, qt, xhat_fm, xhat, tr_banks, ident, s_tr)

                @p2.block.vector
                def _(ve):
                    ve.memset(Qh[:], 0.0)
                    p2.inc(ve.drain(), "aqz", 1)
                qz = p2.val("aqz")

                for qh in range(CD):
                    bank = s_banks[qh % 2]

                    @p2.block.tensor
                    def _(te):
                        if qt == 0 and qh == 0:
                            p2.wait(te, "adw2", w2_rdy)
                        p2.wait(te, "xfdone", xf_n)
                        bank.wait_free(te)
                        for c in range(CD):
                            mm = te.matmul(bank.t[:, :],
                                           waq[:, c * 1024 + qh * 128: c * 1024 + qh * 128 + 128],
                                           xhat_fm[:, c * 512:(c + 1) * 512],
                                           start=(c == 0), stop=(c == CD - 1))
                        p2.inc(mm, "aqmm", 1)

                    @p2.block.vector
                    def _(ve):
                        p2.wait(ve, "aqmm")
                        if qh == 0:
                            p2.wait(ve, "aqz", qz)
                        ve.tensor_scalar(Qh[0:64, (2 * qh) * 512:(2 * qh) * 512 + 512],
                                         bank.t[0:64, :], 0.125, None, ALU.mult)
                        ve.drain()
                        ev = ve.tensor_scalar(Qh[64:128, (2 * qh + 1) * 512:(2 * qh + 1) * 512 + 512],
                                              bank.t[64:128, :], 0.125, None, ALU.mult)
                        bank.release(ev)

                    @p2.block.scalar
                    def _(se):
                        p2.wait(se, "aqmm")
                        p2.inc(se.activation(sq[:, :], bank.t[:, :], AF.Square), "aqsq", 1)

                    for par in range(2):
                        @p2.block.tensor
                        def _(te):
                            p2.wait(te, "aqsq")
                            sm_bank.wait_free(te)
                            p2.inc(te.matmul(sm_bank.t[0:1, 0:512],
                                             onespair[:, par:par + 1], sq[:, :],
                                             start=True, stop=True), "aqnm", 1)

                        @p2.block.vector
                        def _(ve):
                            p2.wait(ve, "aqnm")
                            mx = ve.reduce_max(qn2m[0:1, 2 * qh + par:2 * qh + par + 1],
                                               sm_bank.t[0:1, 0:512], axis=AX.X)
                            sm_bank.release(mx)

                @p2.block.vector
                def _(ve):
                    ve.drain()
                    p2.inc(ve.tensor_mul(prod[0:1, :], qn2m[0:1, :], knm[0:1, :]),
                           "apro", 1)

                @p2.block.scalar
                def _(se):
                    p2.wait(se, "apro")
                    se.activation(prod[0:1, :], prod[0:1, :], AF.Ln)
                    se.drain()
                    p2.inc(se.activation(mhn[0:1, :], prod[0:1, :], AF.Exp, scale=0.5),
                           "amh1", 1)

                @p2.block.vector
                def _(ve):
                    p2.wait(ve, "amh1")
                    p2.inc(ve.tensor_scalar(mhn[0:1, :], mhn[0:1, :], -0.125, None, ALU.mult),
                           "amh2", 1)
                mh_n = p2.val("amh2")

                n_kc = 4 * (qt + 1)
                for h in range(16):
                    ch = h // 2
                    @p2.block.tensor
                    def _(te):
                        p2.wait(te, "amh2", mh_n)
                        sm_bank.wait_free(te)
                        p2.inc(te.matmul(sm_bank.t[0:128, 0:1], onesrow, mhn[0:1, h:h + 1],
                                         start=True, stop=True), "ambc", 1)

                    @p2.block.vector
                    def _(ve):
                        p2.wait(ve, "ambc")
                        ev = ve.tensor_copy(mhb[:, 0:1], sm_bank.t[0:128, 0:1])
                        sm_bank.release(ev)
                    mb_u = sm_bank.uses

                    for kc in range(n_kc):
                        bank = s_banks[kc % 2]
                        stx = STx[kc % 2]

                        @p2.block.tensor
                        def _(te):
                            bank.wait_free(te)
                            p2.inc(te.matmul(bank.t[:, :],
                                             K_sb[:, ch * n_tok + kc * 128: ch * n_tok + kc * 128 + 128],
                                             Qh[:, h * 512:(h + 1) * 512],
                                             start=True, stop=True), "asmm", 1)

                        @p2.block.scalar
                        def _(se):
                            p2.wait(se, "asmm")
                            if kc == 0:
                                se.wait_ge(p2.sem(sm_bank.name), mb_u)
                            if p2.val("astx") >= 2:
                                p2.wait(se, "astx", p2.cnt["astx"] - 1)
                            ev = se.activation(stx[:, :], bank.t[:, :], AF.Exp, bias=mhb[:, 0:1])
                            bank.release(ev)
                        s_u = (bank, bank.uses)

                        diag = kc - 4 * qt
                        if diag >= 0:
                            @p2.block.vector
                            def _(ve):
                                ve.wait_ge(p2.sem(s_u[0].name), s_u[1])
                                p2.inc(ve.tensor_mul(stx[:, :], stx[:, :], masksT[diag][:]),
                                       "asev", 1)

                        @p2.block.tensor
                        def _(te):
                            if diag >= 0:
                                p2.wait(te, "asev")
                            else:
                                te.wait_ge(p2.sem(s_u[0].name), s_u[1])
                            if kc == 0:
                                av_bank.wait_free(te)
                            mm = te.matmul(av_bank.t[0:65, :],
                                           V_sb[:, kc * 1040 + h * 65: kc * 1040 + h * 65 + 65],
                                           stx[:, :],
                                           start=(kc == 0), stop=(kc == n_kc - 1))
                            p2.inc(mm, "astx", 1)
                            if kc == n_kc - 1:
                                p2.inc(te.drain(), "aavd", 1)
                    av_n = p2.val("aavd")

                    @p2.block.vector
                    def _(ve):
                        p2.wait(ve, "aavd", av_n)
                        p2.inc(ve.reciprocal(rl[0:1, :], av_bank.t[64:65, 0:512]),
                               "arl", 1)

                    @p2.block.tensor
                    def _(te):
                        p2.wait(te, "arl")
                        sm_bank.wait_free(te)
                        p2.inc(te.matmul(sm_bank.t[0:64, 0:512], onesrow[0:1, 0:64], rl[0:1, :],
                                         start=True, stop=True), "albm", 1)

                    @p2.block.vector
                    def _(ve):
                        p2.wait(ve, "albm")
                        ev = ve.tensor_copy(lbc[:, :], sm_bank.t[0:64, 0:512])
                        sm_bank.release(ev)
                    lb_u = sm_bank.uses

                    @p2.block.vector
                    def _(ve):
                        ve.wait_ge(p2.sem(sm_bank.name), lb_u)
                        ev = ve.tensor_mul(y_sb[(h % 2) * 64:(h % 2) * 64 + 64, ch * 512:(ch + 1) * 512],
                                           av_bank.t[0:64, :], lbc[:, :])
                        av_bank.release(ev)

                p2.fence()

                for dh in range(2):
                    wap_rdy = load_wap(dh)
                    for j in range(4):
                        tc = qt * 4 + j
                        bank = s_banks[j % 2]

                        @p2.block.tensor
                        def _(te):
                            if j == 0:
                                p2.wait(te, "adwp", wap_rdy)
                            bank.wait_free(te)
                            for c in range(CD):
                                mm = te.matmul(bank.t[:, :],
                                               y_sb[:, c * 512 + j * 128: c * 512 + j * 128 + 128],
                                               wap[:, c * 512:(c + 1) * 512],
                                               start=(c == 0), stop=(c == CD - 1))
                            p2.inc(mm, "apmm", 1)
                            if j == 3:
                                p2.inc(te.drain(), "awapf", 1)

                        @p2.block.vector
                        def _(ve):
                            p2.wait(ve, "apmm")
                            xv = x_sb[:, tc * 1024 + dh * 512: tc * 1024 + dh * 512 + 512]
                            ad = ve.tensor_add(xv, xv, bank.t[:, :])
                            bank.release(ad)
                            ve.drain()

                p2.fence()

            pp.cnt.update(p2.cnt)

        p.cnt.update(pp.cnt)





def emit_attn1(p, x_sb, wqtp_d, wktp_d, wvt_d, cos_d, sin_d, consts, n_tok, tag):
    NTILE = n_tok // 512
    NT = n_tok // 128
    ident = consts["ident"]

    with contextlib.ExitStack() as st:
        pp = p.fresh(st)

        K_sb = sbuf(pp, "bK", [128, CD * n_tok], BF16)
        V_sb = sbuf(pp, "bV", [128, NT * 1024], BF16)
        xhat_fm = sbuf(pp, "bxf", [128, CD * 512], BF16)
        xhat = [sbuf(pp, f"bxt{i}", [128, 1024], BF16) for i in range(2)]
        raw = sbuf(pp, "braw", [128, 2 * 512], F32)
        tcos = sbuf(pp, "bco", [128, 512], F32)
        tsin = sbuf(pp, "bsi", [128, 512], F32)
        t1 = sbuf(pp, "bt1", [128, 512], F32)
        t2 = sbuf(pp, "bt2", [128, 512], F32)
        negm = sbuf(pp, "bnm", [128, 1], F32)
        lsum = sbuf(pp, "bl", [128, 1], F32)
        rl1 = sbuf(pp, "brl", [128, 1], F32)
        masksA = consts["masksA"]

        tr_banks = [psum_bank(pp, "btr", [128, 128], BF16)]
        s_banks = [psum_bank(pp, f"bs{i}", [128, 512], F32) for i in range(2)]
        av_bank = psum_bank(pp, "bav", [128, 512], F32)

        ln = LN(pp, "b", consts["eps"])
        s_tr = "btr"

        def load_tables(i, tile):
            @pp.block.sync
            def _(sy):
                if pp.val("btbf") > 0:
                    pp.wait(sy, "btbf")
                for dst, srcd in ((tcos, cos_d), (tsin, sin_d)):
                    i3 = srcd[i * 128:(i + 1) * 128, tile * 512:(tile + 1) * 512]
                    pp.inc(sy.dma_start(out=dst[:], in_=i3), "btb", 16)
            return pp.val("btb")

        def rope_pair(ppx, i, tile, w_sb, dst_sb, dst_col, tb_rdy, wname, w_rdy, xf_n):
            for half, ch in ((0, i), (1, i + 4)):
                bank = s_banks[half]

                @ppx.block.tensor
                def _(te):
                    if w_rdy is not None:
                        ppx.wait(te, wname, w_rdy)
                    ppx.wait(te, "xfdone", xf_n)
                    bank.wait_free(te)
                    for c in range(CD):
                        mm = te.matmul(bank.t[:, :],
                                       w_sb[:, c * 1024 + ch * 128: c * 1024 + ch * 128 + 128],
                                       xhat_fm[:, c * 512:(c + 1) * 512],
                                       start=(c == 0), stop=(c == CD - 1))
                    ppx.inc(mm, "brmm", 1)

                @ppx.block.vector
                def _(ve):
                    ppx.wait(ve, "brmm")
                    if half == 0 and ppx.val("btbf") > 0:
                        ppx.wait(ve, "btbf")
                    ev = ve.tensor_copy(raw[:, half * 512:(half + 1) * 512], bank.t[:, :])
                    bank.release(ev)
            r_us = [(s_banks[0], s_banks[0].uses), (s_banks[1], s_banks[1].uses)]

            @ppx.block.vector
            def _(ve):
                for b, u in r_us:
                    ve.wait_ge(ppx.sem(b.name), u)
                ppx.wait(ve, "btb", tb_rdy)
                r0 = raw[:, 0:512]
                r1 = raw[:, 512:1024]
                ve.tensor_mul(t1[:], r0, tcos[:])
                ve.drain()
                ve.tensor_mul(t2[:], r1, tsin[:])
                ve.drain()
                ve.tensor_sub(dst_sb[:, i * n_tok + dst_col: i * n_tok + dst_col + 512], t1[:], t2[:])
                ve.drain()
                ve.tensor_mul(t1[:], r1, tcos[:])
                ve.drain()
                ve.tensor_mul(t2[:], r0, tsin[:])
                ve.drain()
                ve.tensor_add(dst_sb[:, (i + 4) * n_tok + dst_col: (i + 4) * n_tok + dst_col + 512],
                              t1[:], t2[:])
                ppx.inc(ve.drain(), "btbf", 1)

        with contextlib.ExitStack() as st1:
            p1 = pp.fresh(st1)
            wk = sbuf(p1, "bwk", [128, CD * 1024], BF16)
            wv = sbuf(p1, "bwv", [128, CD * 1024], BF16)

            @p1.block.gpsimd
            def _(g):
                for w_sb, w_d in ((wk, wktp_d), (wv, wvt_d)):
                    o3 = w_sb[:].rearrange("p (c d) -> p c d", c=CD)
                    i3 = w_d.rearrange("(c p) d -> p c d", p=128)
                    p1.inc(g.dma_start(out=o3, in_=i3), "bdw1", 16)
            w1_rdy = p1.val("bdw1")

            for tile in range(NTILE):
                xf_n = emit_xhat_fm(p1, ln, x_sb, tile, xhat_fm, xhat, tr_banks, ident, s_tr)
                for i in range(4):
                    tb = load_tables(i, tile)
                    rope_pair(p1, i, tile, wk, K_sb, tile * 512, tb,
                              "bdw1", w1_rdy if (tile == 0 and i == 0) else None, xf_n)
                for j in range(4):
                    tc = tile * 4 + j
                    for dh in range(2):
                        bank = s_banks[dh]

                        @p1.block.tensor
                        def _(te):
                            bank.wait_free(te)
                            for c in range(CD):
                                mm = te.matmul(bank.t[:, :],
                                               xhat_fm[:, c * 512 + j * 128: c * 512 + j * 128 + 128],
                                               wv[:, c * 1024 + dh * 512: c * 1024 + dh * 512 + 512],
                                               start=(c == 0), stop=(c == CD - 1))
                            p1.inc(mm, "bvmm", 1)

                        @p1.block.vector
                        def _(ve):
                            p1.wait(ve, "bvmm")
                            ev = ve.tensor_copy(V_sb[:, tc * 1024 + dh * 512: tc * 1024 + dh * 512 + 512],
                                                bank.t[:, :])
                            bank.release(ev)
                pp.fence()
            pp.cnt.update(p1.cnt)

        with contextlib.ExitStack() as st2:
            p2 = pp.fresh(st2)
            wq = sbuf(p2, "bwq", [128, CD * 1024], BF16)
            Qrot = sbuf(p2, "bQ", [128, CD * 512], BF16)
            Sf = sbuf(p2, "bS", [128, n_tok], F32)
            STe = sbuf(p2, "bSe", [128, n_tok], BF16)
            STt = sbuf(p2, "bSt", [128, n_tok], BF16)

            @p2.block.gpsimd
            def _(g):
                p2.wait(g, "bdw1")
                p2.wait(g, "fence")
                o3 = wq[:].rearrange("p (c d) -> p c d", c=CD)
                i3 = wqtp_d.rearrange("(c p) d -> p c d", p=128)
                p2.inc(g.dma_start(out=o3, in_=i3), "bdw2", 16)
            w2_rdy = p2.val("bdw2")

            for qt in range(NTILE):
                xf_n = emit_xhat_fm(p2, ln, x_sb, qt, xhat_fm, xhat, tr_banks, ident, s_tr)
                for i in range(4):
                    tb = load_tables(i, qt)
                    for half, ch in ((0, i), (1, i + 4)):
                        bank = s_banks[half]

                        @p2.block.tensor
                        def _(te):
                            if qt == 0 and i == 0 and half == 0:
                                p2.wait(te, "bdw2", w2_rdy)
                            p2.wait(te, "xfdone", xf_n)
                            bank.wait_free(te)
                            for c in range(CD):
                                mm = te.matmul(bank.t[:, :],
                                               wq[:, c * 1024 + ch * 128: c * 1024 + ch * 128 + 128],
                                               xhat_fm[:, c * 512:(c + 1) * 512],
                                               start=(c == 0), stop=(c == CD - 1))
                            p2.inc(mm, "brmm", 1)

                        @p2.block.vector
                        def _(ve):
                            p2.wait(ve, "brmm")
                            if half == 0 and p2.val("btbf") > 0:
                                p2.wait(ve, "btbf")
                            ev = ve.tensor_copy(raw[:, half * 512:(half + 1) * 512], bank.t[:, :])
                            bank.release(ev)
                    r_us = [(s_banks[0], s_banks[0].uses), (s_banks[1], s_banks[1].uses)]

                    @p2.block.vector
                    def _(ve):
                        for b, u in r_us:
                            ve.wait_ge(p2.sem(b.name), u)
                        p2.wait(ve, "btb", tb)
                        r0 = raw[:, 0:512]
                        r1 = raw[:, 512:1024]
                        ve.tensor_mul(t1[:], r0, tcos[:])
                        ve.drain()
                        ve.tensor_mul(t2[:], r1, tsin[:])
                        ve.drain()
                        ve.tensor_sub(Qrot[:, i * 512:(i + 1) * 512], t1[:], t2[:])
                        ve.drain()
                        ve.tensor_mul(t1[:], r1, tcos[:])
                        ve.drain()
                        ve.tensor_mul(t2[:], r0, tsin[:])
                        ve.drain()
                        ve.tensor_add(Qrot[:, (i + 4) * 512:(i + 5) * 512], t1[:], t2[:])
                        p2.inc(ve.drain(), "btbf", 1)

                @p2.block.vector
                def _(ve):
                    p2.inc(ve.drain(), "bqr", 1)
                q_n = p2.val("bqr")

                for qq in range(4):
                    k_hi = qt * 512 + (qq + 1) * 128
                    n_k5 = (k_hi + 511) // 512
                    for kc in range(n_k5):
                        bank = s_banks[kc % 2]
                        diag = (kc == n_k5 - 1)

                        @p2.block.tensor
                        def _(te):
                            if kc == 0:
                                p2.wait(te, "bqr", q_n)
                            bank.wait_free(te)
                            for c in range(CD):
                                mm = te.matmul(bank.t[:, :],
                                               Qrot[:, c * 512 + qq * 128: c * 512 + qq * 128 + 128],
                                               K_sb[:, c * n_tok + kc * 512: c * n_tok + kc * 512 + 512],
                                               start=(c == 0), stop=(c == CD - 1))
                            p2.inc(mm, "bsmm", 1)

                        if diag:
                            dd = (qt * 512 + qq * 128 - kc * 512) // 128

                            @p2.block.vector
                            def _(ve):
                                p2.wait(ve, "bsmm")
                                if p2.val("bex") > 0:
                                    p2.wait(ve, "bex")
                                ad = ve.tensor_add(Sf[:, kc * 512:(kc + 1) * 512],
                                                   bank.t[:, :], masksA[dd][:])
                                bank.release(ad)
                        else:
                            @p2.block.vector
                            def _(ve):
                                p2.wait(ve, "bsmm")
                                ev = ve.tensor_copy(Sf[:, kc * 512:(kc + 1) * 512], bank.t[:, :])
                                bank.release(ev)
                    s_rel = [(b, b.uses) for b in s_banks]
                    k512 = n_k5 * 512

                    @p2.block.vector
                    def _(ve):
                        for b, u in s_rel:
                            ve.wait_ge(p2.sem(b.name), u)
                        ve.reduce_max(negm[:], Sf[:, 0:k512], axis=AX.X, negate=True)
                        ve.drain()
                        p2.inc(ve.tensor_scalar(negm[:], negm[:], 0.03125, None, ALU.mult),
                               "bnm", 1)

                    @p2.block.scalar
                    def _(se):
                        p2.wait(se, "bnm")
                        if p2.val("bstr") > 0:
                            p2.wait(se, "bstr")
                        p2.inc(se.activation(STe[:, 0:k512], Sf[:, 0:k512], AF.Exp,
                                             bias=negm[:, 0:1], scale=0.03125,
                                             accum_out=lsum[:, 0:1]), "bex", 1)
                    e_n = p2.val("bex")

                    for kc in range(k512 // 128):
                        bank = tr_banks[0]

                        @p2.block.tensor
                        def _(te):
                            p2.wait(te, "bex", e_n)
                            bank.wait_free(te)
                            p2.inc(te.transpose(bank.t[:, :], STe[:, kc * 128:(kc + 1) * 128],
                                                ident[:]), "bstr", 1)

                        @p2.block.vector
                        def _(ve):
                            p2.wait(ve, "bstr")
                            ev = ve.tensor_copy(STt[:, kc * 128:(kc + 1) * 128], bank.t[:, :])
                            bank.release(ev)
                    stt_u = (tr_banks[0], tr_banks[0].uses)

                    @p2.block.vector
                    def _(ve):
                        p2.wait(ve, "bex", e_n)
                        p2.inc(ve.reciprocal(rl1[:], lsum[:, 0:1]), "brc", 1)
                    rc_n = p2.val("brc")

                    for dh in range(2):
                        @p2.block.tensor
                        def _(te):
                            if dh == 0:
                                te.wait_ge(p2.sem(stt_u[0].name), stt_u[1])
                            av_bank.wait_free(te)
                            for kc in range(k512 // 128):
                                mm = te.matmul(av_bank.t[:, :],
                                               STt[:, kc * 128:(kc + 1) * 128],
                                               V_sb[:, kc * 1024 + dh * 512: kc * 1024 + dh * 512 + 512],
                                               start=(kc == 0), stop=(kc == k512 // 128 - 1))
                            p2.inc(mm, "bavm", 1)

                        @p2.block.vector
                        def _(ve):
                            p2.wait(ve, "bavm")
                            if dh == 0:
                                p2.wait(ve, "brc", rc_n)
                            tc = qt * 4 + qq
                            xv = x_sb[:, tc * 1024 + dh * 512: tc * 1024 + dh * 512 + 512]
                            stt = ve.scalar_tensor_tensor(xv, av_bank.t[:, :], rl1[:, 0:1],
                                                          xv, ALU.mult, ALU.add)
                            av_bank.release(stt)
                            ve.drain()

                p2.fence()

            pp.cnt.update(p2.cnt)

        p.cnt.update(pp.cnt)



def _seq_io(p, block, x_sb, x_d, y_d, s, NT, load, consts=None, ident_dram=None):
    if load:
        @block.sync
        def _(sy):
            if consts is not None and s == 0:
                p.inc(sy.dma_start(out=consts["ident"][:], in_=ident_dram), "dsync", 16)
            o3 = x_sb[:].rearrange("p (c d) -> p c d", c=NT)
            i3 = x_d[s].rearrange("(c p) d -> p c d", p=128)
            p.inc(sy.dma_start(out=o3, in_=i3), "dsync", 16)
        n = p.val("dsync")
        for engname in ("tensor", "vector", "scalar"):
            @getattr(block, engname)
            def _(eng):
                p.wait(eng, "dsync", n)
    else:
        @block.sync
        def _(sy):
            sy.wait_ge(p.sem("fence"), p.cnt["fence"])
            o3 = y_d[s].rearrange("(c p) d -> p c d", p=128)
            i3 = x_sb[:].rearrange("p (c d) -> p c d", c=NT)
            p.inc(sy.dma_start(out=o3, in_=i3), "dstore", 16)
            p.wait(sy, "dstore")


def build_ffn0_only(n_tok):
    nc = bass.Bass("TRN2", target_bir_lowering=False)
    NT = n_tok // 128
    x_d = nc.dram_tensor("x", [2, n_tok, D], F32, kind="ExternalInput").ap()
    wr_d = nc.dram_tensor("wrt", [D, D], F32, kind="ExternalInput").ap()
    wk_d = nc.dram_tensor("wkt", [D, D], F32, kind="ExternalInput").ap()
    wv_d = nc.dram_tensor("wvt", [D, D], F32, kind="ExternalInput").ap()
    mur_d = nc.dram_tensor("mur", [D], F32, kind="ExternalInput").ap()
    muk_d = nc.dram_tensor("muk", [D], F32, kind="ExternalInput").ap()
    y_d = nc.dram_tensor("y", [2, n_tok, D], F32, kind="ExternalOutput").ap()

    with contextlib.ExitStack() as stack:
        p = P(nc, stack)
        ident_dram = nc.inline_tensor(np.eye(128, dtype=ml_dtypes.bfloat16), "identc").ap()
        ident = sbuf(p, "ident", [128, 128], BF16)
        eps_t = sbuf(p, "epsc", [128, 1], F32)
        x_sb = sbuf(p, "xres", [128, NT * 1024], F32)
        consts = {"ident": ident, "eps": eps_t[:, 0:1]}

        for s in range(2):
            with nc.Block() as block:
                p.block = block
                if s == 0:
                    @block.vector
                    def _(ve):
                        ve.memset(eps_t[:], LN_EPS)
                        ve.drain()
                _seq_io(p, block, x_sb, x_d, y_d, s, NT, True, consts, ident_dram)
                emit_ffn0(p, x_sb, wr_d, wk_d, wv_d, mur_d, muk_d, consts,
                          n_tok, f"s{s}")
                _seq_io(p, block, x_sb, x_d, y_d, s, NT, False)
    return nc




def build_moe_only(n_tok):
    nc = bass.Bass("TRN2", target_bir_lowering=False)
    NT = n_tok // 128
    x_d = nc.dram_tensor("x", [2, n_tok, D], F32, kind="ExternalInput").ap()
    wg_d = nc.dram_tensor("wg", [D, 8], F32, kind="ExternalInput").ap()
    w1_d = nc.dram_tensor("w1", [8, D, 4096], F32, kind="ExternalInput").ap()
    b1r_d = nc.dram_tensor("b1r", [128, 256], F32, kind="ExternalInput").ap()
    w2_d = nc.dram_tensor("w2", [8, 4096, D], F32, kind="ExternalInput").ap()
    b2_d = nc.dram_tensor("b2", [8, D], F32, kind="ExternalInput").ap()
    y_d = nc.dram_tensor("y", [2, n_tok, D], F32, kind="ExternalOutput").ap()

    with contextlib.ExitStack() as stack:
        p = P(nc, stack)
        ident_dram = nc.inline_tensor(np.eye(128, dtype=ml_dtypes.bfloat16), "identc").ap()
        identf_dram = nc.inline_tensor(np.eye(128, dtype=np.float32), "identfc").ap()
        ident = sbuf(p, "ident", [128, 128], BF16)
        identf = sbuf(p, "identf", [128, 128], F32)
        eps_t = sbuf(p, "epsc", [128, 1], F32)
        x_sb = sbuf(p, "xres", [128, NT * 1024], F32)
        consts = {"ident": ident, "identf": identf, "eps": eps_t[:, 0:1]}

        for s in range(2):
            with nc.Block() as block:
                p.block = block
                if s == 0:
                    @block.vector
                    def _(ve):
                        ve.memset(eps_t[:], LN_EPS)
                        ve.drain()
                    @block.sync
                    def _(sy):
                        p.inc(sy.dma_start(out=identf[:], in_=identf_dram), "dsync", 16)
                _seq_io(p, block, x_sb, x_d, y_d, s, NT, True, consts, ident_dram)
                emit_moe(p, x_sb, wg_d, w1_d, b1r_d, w2_d, b2_d, consts,
                         n_tok, f"s{s}", cast_w=True)
                _seq_io(p, block, x_sb, x_d, y_d, s, NT, False)
    return nc




def make_attn0_consts(nc, p):
    op = np.zeros((128, 2), dtype=ml_dtypes.bfloat16)
    op[0:64, 0] = 1.0
    op[64:128, 1] = 1.0
    op_d = nc.inline_tensor(op, "onespair_c").ap()
    onespair = sbuf(p, "onespair", [128, 2], BF16)
    masks = []
    masks_d = []
    for d in range(4):
        m = np.zeros((128, 512), dtype=ml_dtypes.bfloat16)
        for pp_ in range(128):
            j0 = d * 128 + pp_
            if j0 < 512:
                m[pp_, j0:] = 1.0
        masks_d.append(nc.inline_tensor(m, f"maskT{d}_c").ap())
        masks.append(sbuf(p, f"maskT{d}", [128, 512], BF16))
    onesrow = sbuf(p, "onesrow", [1, 128], F32)
    return op_d, onespair, masks_d, masks, onesrow


def build_attn0_only(n_tok):
    nc = bass.Bass("TRN2", target_bir_lowering=False)
    NT = n_tok // 128
    x_d = nc.dram_tensor("x", [2, n_tok, D], F32, kind="ExternalInput").ap()
    wattn_d = nc.dram_tensor("wattn", [D, 3 * D], F32, kind="ExternalInput").ap()
    wproj_d = nc.dram_tensor("wproj", [D, D], F32, kind="ExternalInput").ap()
    y_d = nc.dram_tensor("y", [2, n_tok, D], F32, kind="ExternalOutput").ap()

    with contextlib.ExitStack() as stack:
        p = P(nc, stack)
        ident_dram = nc.inline_tensor(np.eye(128, dtype=ml_dtypes.bfloat16), "identc").ap()
        identf_dram = nc.inline_tensor(np.eye(128, dtype=np.float32), "identfc").ap()
        ident = sbuf(p, "ident", [128, 128], BF16)
        identf = sbuf(p, "identf", [128, 128], F32)
        eps_t = sbuf(p, "epsc", [128, 1], F32)
        x_sb = sbuf(p, "xres", [128, NT * 1024], F32)
        op_d, onespair, masks_d, masks, onesrow = make_attn0_consts(nc, p)
        consts = {"ident": ident, "identf": identf, "eps": eps_t[:, 0:1],
                  "onespair": onespair[:], "onesrow": onesrow[:],
                  "masksT": [m[:] for m in masks]}

        for s in range(2):
            with nc.Block() as block:
                p.block = block
                if s == 0:
                    @block.vector
                    def _(ve):
                        ve.memset(eps_t[:], LN_EPS)
                        ve.drain()
                        ve.memset(onesrow[:], 1.0)
                        ve.drain()
                    @block.sync
                    def _(sy):
                        p.inc(sy.dma_start(out=identf[:], in_=identf_dram), "dsync", 16)
                        p.inc(sy.dma_start(out=onespair[:], in_=op_d), "dsync", 16)
                        for m_sb, m_d in zip(masks, masks_d):
                            p.inc(sy.dma_start(out=m_sb[:], in_=m_d), "dsync", 16)
                _seq_io(p, block, x_sb, x_d, y_d, s, NT, True, consts, ident_dram)
                emit_attn0(p, x_sb, wattn_d, wproj_d, consts, n_tok, f"s{s}")
                _seq_io(p, block, x_sb, x_d, y_d, s, NT, False)
    return nc




def build_attn1_only(n_tok):
    nc = bass.Bass("TRN2", target_bir_lowering=False)
    NT = n_tok // 128
    x_d = nc.dram_tensor("x", [2, n_tok, D], F32, kind="ExternalInput").ap()
    wqtp_d = nc.dram_tensor("wqtp", [D, D], F32, kind="ExternalInput").ap()
    wktp_d = nc.dram_tensor("wktp", [D, D], F32, kind="ExternalInput").ap()
    wvt_d = nc.dram_tensor("wvt", [D, D], F32, kind="ExternalInput").ap()
    cos_d = nc.dram_tensor("ropec", [512, n_tok], F32, kind="ExternalInput").ap()
    sin_d = nc.dram_tensor("ropes", [512, n_tok], F32, kind="ExternalInput").ap()
    y_d = nc.dram_tensor("y", [2, n_tok, D], F32, kind="ExternalOutput").ap()

    with contextlib.ExitStack() as stack:
        p = P(nc, stack)
        ident_dram = nc.inline_tensor(np.eye(128, dtype=ml_dtypes.bfloat16), "identc").ap()
        ident = sbuf(p, "ident", [128, 128], BF16)
        eps_t = sbuf(p, "epsc", [128, 1], F32)
        x_sb = sbuf(p, "xres", [128, NT * 1024], F32)
        masksA_np = []
        masksA_d = []
        masksA = []
        for dd in range(4):
            m = np.zeros((128, 512), dtype=ml_dtypes.bfloat16)
            for pp_ in range(128):
                j0 = dd * 128 + pp_
                if j0 + 1 < 512:
                    m[pp_, j0 + 1:] = -1e30
            masksA_d.append(nc.inline_tensor(m, f"maskA{dd}_c").ap())
            masksA.append(sbuf(p, f"maskA{dd}", [128, 512], BF16))
        consts = {"ident": ident, "eps": eps_t[:, 0:1],
                  "masksA": [m[:] for m in masksA]}

        for s in range(2):
            with nc.Block() as block:
                p.block = block
                if s == 0:
                    @block.vector
                    def _(ve):
                        ve.memset(eps_t[:], LN_EPS)
                        ve.drain()
                    @block.sync
                    def _(sy):
                        for m_sb, m_d in zip(masksA, masksA_d):
                            p.inc(sy.dma_start(out=m_sb[:], in_=m_d), "dsync", 16)
                _seq_io(p, block, x_sb, x_d, y_d, s, NT, True, consts, ident_dram)
                emit_attn1(p, x_sb, wqtp_d, wktp_d, wvt_d, cos_d, sin_d, consts,
                           n_tok, f"s{s}")
                _seq_io(p, block, x_sb, x_d, y_d, s, NT, False)
    return nc




def build_full(n_tok):
    nc = bass.Bass("TRN2", target_bir_lowering=False)
    NT = n_tok // 128
    x_d = nc.dram_tensor("x", [2, n_tok, D], F32, kind="ExternalInput").ap()
    wrt_d = nc.dram_tensor("wrt", [D, D], F32, kind="ExternalInput").ap()
    wkt_d = nc.dram_tensor("wkt", [D, D], F32, kind="ExternalInput").ap()
    wvt_d = nc.dram_tensor("wvt", [D, D], F32, kind="ExternalInput").ap()
    mur_d = nc.dram_tensor("mur", [D], F32, kind="ExternalInput").ap()
    muk_d = nc.dram_tensor("muk", [D], F32, kind="ExternalInput").ap()
    wattn_d = nc.dram_tensor("wattn", [D, 3 * D], F32, kind="ExternalInput").ap()
    wproj_d = nc.dram_tensor("wproj", [D, D], F32, kind="ExternalInput").ap()
    wg_d = nc.dram_tensor("wg", [D, 8], F32, kind="ExternalInput").ap()
    w1_d = nc.dram_tensor("w1", [8, D, 4096], F32, kind="ExternalInput").ap()
    b1r_d = nc.dram_tensor("b1r", [128, 256], F32, kind="ExternalInput").ap()
    w2_d = nc.dram_tensor("w2", [8, 4096, D], F32, kind="ExternalInput").ap()
    b2_d = nc.dram_tensor("b2", [8, D], F32, kind="ExternalInput").ap()
    wqtp_d = nc.dram_tensor("wqtp", [D, D], F32, kind="ExternalInput").ap()
    wktp_d = nc.dram_tensor("wktp", [D, D], F32, kind="ExternalInput").ap()
    wvt1_d = nc.dram_tensor("wvt1", [D, D], F32, kind="ExternalInput").ap()
    cos_d = nc.dram_tensor("ropec", [512, n_tok], F32, kind="ExternalInput").ap()
    sin_d = nc.dram_tensor("ropes", [512, n_tok], F32, kind="ExternalInput").ap()
    y_d = nc.dram_tensor("y", [2, n_tok, D], F32, kind="ExternalOutput").ap()

    with contextlib.ExitStack() as stack:
        p = P(nc, stack)
        ident_dram = nc.inline_tensor(np.eye(128, dtype=ml_dtypes.bfloat16), "identc").ap()
        identf_dram = nc.inline_tensor(np.eye(128, dtype=np.float32), "identfc").ap()
        ident = sbuf(p, "ident", [128, 128], BF16)
        identf = sbuf(p, "identf", [128, 128], F32)
        eps_t = sbuf(p, "epsc", [128, 1], F32)
        x_sb = sbuf(p, "xres", [128, NT * 1024], F32)
        op_d, onespair, masksT_d, masksT, onesrow = make_attn0_consts(nc, p)
        masksA_d = []
        masksA = []
        for dd in range(4):
            m = np.zeros((128, 512), dtype=ml_dtypes.bfloat16)
            for pp_ in range(128):
                j0 = dd * 128 + pp_
                if j0 + 1 < 512:
                    m[pp_, j0 + 1:] = -1e30
            masksA_d.append(nc.inline_tensor(m, f"maskA{dd}_c").ap())
            masksA.append(sbuf(p, f"maskA{dd}", [128, 512], BF16))
        consts = {"ident": ident, "identf": identf, "eps": eps_t[:, 0:1],
                  "onespair": onespair[:], "onesrow": onesrow[:],
                  "masksT": [m[:] for m in masksT],
                  "masksA": [m[:] for m in masksA]}

        for s in range(2):
            with nc.Block() as block:
                p.block = block
                if s == 0:
                    @block.vector
                    def _(ve):
                        ve.memset(eps_t[:], LN_EPS)
                        ve.drain()
                        ve.memset(onesrow[:], 1.0)
                        ve.drain()

                    @block.sync
                    def _(sy):
                        p.inc(sy.dma_start(out=identf[:], in_=identf_dram), "dsync", 16)
                        p.inc(sy.dma_start(out=onespair[:], in_=op_d), "dsync", 16)
                        for m_sb, m_d in zip(masksT, masksT_d):
                            p.inc(sy.dma_start(out=m_sb[:], in_=m_d), "dsync", 16)
                        for m_sb, m_d in zip(masksA, masksA_d):
                            p.inc(sy.dma_start(out=m_sb[:], in_=m_d), "dsync", 16)
                _seq_io(p, block, x_sb, x_d, y_d, s, NT, True, consts, ident_dram)
                emit_ffn0(p, x_sb, wrt_d, wkt_d, wvt_d, mur_d, muk_d, consts,
                          n_tok, f"s{s}a")
            with nc.Block() as block:
                p.block = block
                emit_attn0(p, x_sb, wattn_d, wproj_d, consts, n_tok, f"s{s}")
            with nc.Block() as block:
                p.block = block
                emit_ffn0(p, x_sb, wrt_d, wkt_d, wvt_d, mur_d, muk_d, consts,
                          n_tok, f"s{s}b")
            with nc.Block() as block:
                p.block = block
                emit_moe(p, x_sb, wg_d, w1_d, b1r_d, w2_d, b2_d, consts,
                         n_tok, f"s{s}a", cast_w=True)
            with nc.Block() as block:
                p.block = block
                emit_attn1(p, x_sb, wqtp_d, wktp_d, wvt1_d, cos_d, sin_d, consts,
                           n_tok, f"s{s}")
            with nc.Block() as block:
                p.block = block
                emit_moe(p, x_sb, wg_d, w1_d, b1r_d, w2_d, b2_d, consts,
                         n_tok, f"s{s}b", cast_w=True)
                _seq_io(p, block, x_sb, x_d, y_d, s, NT, False)
    return nc


def _rope_tables_host(n_tok):
    pos = np.arange(n_tok, dtype=np.float64)[:, None]
    inv = 1.0 / 10000.0 ** (np.arange(0, D, 2, dtype=np.float64) / D)
    fr = pos * inv
    cos = np.ascontiguousarray(np.cos(fr).T.astype(np.float32))
    sin = np.ascontiguousarray(np.sin(fr).T.astype(np.float32))
    return cos, sin


def kernel(**inputs):
    n_tok = 2048
    n_cores = 8
    x = np.ascontiguousarray(np.asarray(inputs["L0_attn_x"], dtype=np.float32))

    wrt = np.ascontiguousarray(np.asarray(inputs["L0_ffn_W_r"]).T)
    wkt = np.ascontiguousarray(np.asarray(inputs["L0_ffn_W_k"]).T)
    wvt = np.ascontiguousarray(np.asarray(inputs["L0_ffn_W_v"]).T)
    mur = np.ascontiguousarray(np.asarray(inputs["L0_ffn_mu_r"]))
    muk = np.ascontiguousarray(np.asarray(inputs["L0_ffn_mu_k"]))
    wattn = np.ascontiguousarray(np.asarray(inputs["L0_attn_w_attn"]))
    wproj = np.ascontiguousarray(np.asarray(inputs["L0_attn_w_proj"]))
    wg = np.ascontiguousarray(np.asarray(inputs["L1_ffn_W_gate"]))
    w1 = np.ascontiguousarray(np.asarray(inputs["L1_ffn_W1"]))
    b1 = np.asarray(inputs["L1_ffn_b1"])
    b1r = np.ascontiguousarray(b1.reshape(8, 32, 128).transpose(2, 0, 1).reshape(128, 256))
    w2 = np.ascontiguousarray(np.asarray(inputs["L1_ffn_W2"]))
    b2 = np.ascontiguousarray(np.asarray(inputs["L1_ffn_b2"]))
    perm = np.concatenate([np.arange(0, D, 2), np.arange(1, D, 2)])
    wqtp = np.ascontiguousarray(np.asarray(inputs["L1_attn_wq"]).T[:, perm])
    wktp = np.ascontiguousarray(np.asarray(inputs["L1_attn_wk"]).T[:, perm])
    wvt1 = np.ascontiguousarray(np.asarray(inputs["L1_attn_wv"]).T)
    cosf, sinf = _rope_tables_host(n_tok)

    shards = [np.ascontiguousarray(x[2 * c: 2 * c + 2]) for c in range(n_cores)]

    nc = build_full(n_tok)
    weights = {"wrt": wrt, "wkt": wkt, "wvt": wvt, "mur": mur, "muk": muk,
               "wattn": wattn, "wproj": wproj,
               "wg": wg, "w1": w1, "b1r": b1r, "w2": w2, "b2": b2,
               "wqtp": wqtp, "wktp": wktp, "wvt1": wvt1,
               "ropec": cosf, "ropes": sinf}
    in_maps = [dict(weights, x=shards[c]) for c in range(n_cores)]
    res = run_bass_kernel_spmd(nc, in_maps, core_ids=list(range(n_cores)))
    return np.concatenate([np.asarray(r["y"]) for r in res.results], axis=0)



# version 04
# speedup vs baseline: 1.1607x; 1.0070x over previous
import contextlib
import numpy as np
import ml_dtypes
import concourse.bass as bass
import concourse.mybir as mybir
from concourse.bass_utils import run_bass_kernel_spmd

F32 = mybir.dt.float32
BF16 = mybir.dt.bfloat16
AF = mybir.ActivationFunctionType
ALU = mybir.AluOpType
AX = mybir.AxisListType

D = 1024
CD = D // 128
LN_EPS = 1e-5



class P:

    def __init__(self, nc, stack):
        self.nc = nc
        self.stack = stack
        self.root_stack = stack
        self.block = None
        self._sems = {}
        self.cnt = {}
        self.bku = {}
        self.uid = [0]

    def fresh(self, stack):
        q = P.__new__(P)
        q.__dict__.update(self.__dict__)
        q.stack = stack
        return q

    def sem(self, name):
        if name not in self._sems:
            self._sems[name] = self.root_stack.enter_context(self.nc.semaphore(name))
            self.cnt[name] = 0
        return self._sems[name]

    def inc(self, inst, name, n=1):
        inst.then_inc(self.sem(name), n)
        self.cnt[name] += n
        return self.cnt[name]

    def val(self, name):
        self.sem(name)
        return self.cnt[name]

    def wait(self, eng, name, v=None):
        eng.wait_ge(self.sem(name), self.cnt[name] if v is None else v)

    def fence(self, tag="ph"):
        name = "fence"
        self.sem(name)
        target = self.cnt[name] + 3

        @self.block.tensor
        def _(te):
            self.inc(te.drain(), name, 1)
            te.wait_ge(self._sems[name], target)

        @self.block.vector
        def _(ve):
            self.inc(ve.drain(), name, 1)
            ve.wait_ge(self._sems[name], target)

        @self.block.scalar
        def _(se):
            self.inc(se.drain(), name, 1)
            se.wait_ge(self._sems[name], target)
        assert self.cnt[name] == target


class Bank:

    def __init__(self, p, tensor, name):
        self.p = p
        self.t = tensor
        self.name = "bk_" + name
        p.sem(self.name)
        p.bku.setdefault(self.name, 0)

    @property
    def uses(self):
        return self.p.bku[self.name]

    def wait_free(self, eng):
        if self.uses > 0:
            eng.wait_ge(self.p.sem(self.name), self.uses)
        self.p.bku[self.name] += 1

    def release(self, inst):
        self.p.inc(inst, self.name, 1)

    def wait_released(self, eng):
        eng.wait_ge(self.p.sem(self.name), self.uses)


def sbuf(p, name, shape, dtype):
    p.uid[0] += 1
    return p.stack.enter_context(p.nc.sbuf_tensor(f"{name}_{p.uid[0]}", shape, dtype))


def psum_bank(p, name, shape, dtype=F32):
    p.uid[0] += 1
    t = p.stack.enter_context(p.nc.psum_tensor(f"{name}_{p.uid[0]}", shape, dtype))
    return Bank(p, t, name)



class LN:

    def __init__(self, p, tag, eps_ap):
        self.p = p
        self.eps = eps_ap
        self.stats = sbuf(p, f"lnst{tag}", [128, 12], F32)
        self.mv4 = sbuf(p, f"lnmv{tag}", [128, 8], F32)
        self.lnv = sbuf(p, f"lnlv{tag}", [128, 4], F32)
        self.rstd4 = sbuf(p, f"lnrs{tag}", [128, 4], F32)
        self.s_v2s = "lnv2s"
        self.s_s2v = "lns2v"
        self.rdy = "lnrdy"
        p.sem(self.s_v2s)
        p.sem(self.s_s2v)
        p.sem(self.rdy)

    def stats_emit(self, x_view, idx):
        p = self.p

        @p.block.vector
        def _(ve):
            if idx == 0 and p.val(self.s_s2v) > 0:
                p.wait(ve, self.s_s2v)
            ve.drain()
            ve.bn_stats(self.stats[:, 0:6], x_view[:, 0:512])
            ve.bn_stats(self.stats[:, 6:12], x_view[:, 512:1024])
            ve.drain()
            p.inc(ve.bn_aggr(self.mv4[:, 2 * idx:2 * idx + 2], self.stats[:]),
                  self.s_v2s, 1)

    def rstds_emit(self, n=4):
        p = self.p

        @p.block.scalar
        def _(se):
            p.wait(se, self.s_v2s)
            vars_view = self.mv4[:].rearrange("p (c two) -> p c two", two=2)[:, 0:n, 1:2]
            se.activation(self.lnv[:, 0:n], vars_view, AF.Ln, bias=self.eps)
            se.drain()
            p.inc(se.activation(self.rstd4[:, 0:n], self.lnv[:, 0:n], AF.Exp, scale=-0.5),
                  self.s_s2v, 1)
        self._rn = p.val(self.s_s2v)

    def apply_emit(self, x_view, out_view, idx):
        p = self.p

        @p.block.vector
        def _(ve):
            p.wait(ve, self.s_s2v, self._rn)
            p.inc(ve.tensor_scalar(out_view, x_view, self.mv4[:, 2 * idx:2 * idx + 1],
                                   self.rstd4[:, idx:idx + 1],
                                   ALU.subtract, ALU.mult), self.rdy, 1)


def emit_ffn0(p, x_sb, wr_d, wk_d, wv_d, mur_d, muk_d, consts, n_tok, tag):
    NTILE = n_tok // 512
    ident = consts["ident"]

    with contextlib.ExitStack() as st:
        pp = p.fresh(st)

        wr = sbuf(pp, "fwr", [128, CD * 1024], BF16)
        wk = sbuf(pp, "fwk", [128, CD * 1024], BF16)
        wv = sbuf(pp, "fwv", [128, CD * 1024], BF16)
        mur = sbuf(pp, "fmur", [128, CD], F32)
        muk = sbuf(pp, "fmuk", [128, CD], F32)
        xh = sbuf(pp, "fxh", [128, CD * 512], BF16)
        xs = sbuf(pp, "fxs", [128, CD * 512], BF16)
        carry = sbuf(pp, "fcar", [128, CD], BF16)
        diff = sbuf(pp, "fdif", [128, CD * 512], BF16)
        xr = sbuf(pp, "fxr", [128, CD * 512], BF16)
        xk = sbuf(pp, "fxk", [128, CD * 512], BF16)
        kact = sbuf(pp, "fka", [128, CD * 512], BF16)
        krelu = [sbuf(pp, f"fkr{i}", [128, 512], F32) for i in range(2)]
        r_sb = sbuf(pp, "fr", [128, 4 * 1024], F32)
        t1 = sbuf(pp, "ft1", [128, 512], F32)
        xhat = [sbuf(pp, f"fxt{i}", [128, 1024], BF16) for i in range(2)]

        tr_banks = [psum_bank(pp, f"ftr{i}", [128, 128], BF16) for i in range(2)]
        g_banks = [psum_bank(pp, f"fg{i}", [128, 512], F32) for i in range(4)]

        ln = LN(pp, "f", consts["eps"])
        s_tr = "ftr"

        @pp.block.gpsimd
        def _(g):
            for w_sb, w_d in ((wr, wr_d), (wk, wk_d), (wv, wv_d)):
                o3 = w_sb[:].rearrange("p (c d) -> p c d", c=CD)
                i3 = w_d.rearrange("(c p) d -> p c d", p=128)
                pp.inc(g.dma_start(out=o3, in_=i3), "dpool", 16)
            pp.inc(g.dma_start(out=mur[:], in_=mur_d.rearrange("(c p) -> p c", p=128),
                               allow_slow_non_contiguous=True), "dpool", 16)
            pp.inc(g.dma_start(out=muk[:], in_=muk_d.rearrange("(c p) -> p c", p=128),
                               allow_slow_non_contiguous=True), "dpool", 16)
        w_ready = pp.val("dpool")

        for tile in range(NTILE):
            for j in range(4):
                ln.stats_emit(x_sb[:, (tile * 4 + j) * 1024:(tile * 4 + j + 1) * 1024], j)
            ln.rstds_emit()
            for j in range(4):
                tc = tile * 4 + j
                xht = xhat[j % 2]
                if j >= 2:

                    @pp.block.vector
                    def _(ve):
                        pp.wait(ve, s_tr, pp.cnt.get(s_tr, 0) - CD)
                ln.apply_emit(x_sb[:, tc * 1024:(tc + 1) * 1024], xht[:], j)
                ln_n = pp.val(ln.rdy)
                for c in range(CD):
                    bank = tr_banks[c % 2]

                    @pp.block.tensor
                    def _(te):
                        pp.wait(te, ln.rdy, ln_n)
                        bank.wait_free(te)
                        pp.inc(te.transpose(bank.t[:, :], xht[:, c * 128:(c + 1) * 128],
                                            ident[:]), s_tr, 1)

                    @pp.block.vector
                    def _(ve):
                        pp.wait(ve, s_tr)
                        col = j * 128
                        ve.tensor_copy(xh[:, c * 512 + col: c * 512 + col + 128], bank.t[:, :])
                        ve.drain()
                        ncols = 127 if j == 3 else 128
                        ev = ve.tensor_copy(xs[:, c * 512 + col + 1: c * 512 + col + 1 + ncols],
                                            bank.t[:, 0:ncols])
                        bank.release(ev)

            @pp.block.vector
            def _(ve):
                if tile == 0:
                    ve.memset(carry[:], 0.0)
                    ve.drain()
                for c in range(CD):
                    ve.tensor_copy(xs[:, c * 512: c * 512 + 1], carry[:, c:c + 1])
                ve.drain()
                for c in range(CD):
                    ve.tensor_copy(carry[:, c:c + 1], xh[:, c * 512 + 511: c * 512 + 512])
            pp.fence()

            @pp.block.vector
            def _(ve):
                if tile == 0:
                    pp.wait(ve, "dpool", w_ready)
                ve.tensor_sub(diff[:], xh[:], xs[:])
                ve.drain()
                for c in range(CD):
                    ve.scalar_tensor_tensor(xr[:, c * 512:(c + 1) * 512],
                                            diff[:, c * 512:(c + 1) * 512],
                                            mur[:, c:c + 1],
                                            xs[:, c * 512:(c + 1) * 512],
                                            ALU.mult, ALU.add)
                    ve.scalar_tensor_tensor(xk[:, c * 512:(c + 1) * 512],
                                            diff[:, c * 512:(c + 1) * 512],
                                            muk[:, c:c + 1],
                                            xs[:, c * 512:(c + 1) * 512],
                                            ALU.mult, ALU.add)
            pp.fence()

            for j in range(4):
                for dh in range(2):
                    bank = g_banks[(j * 2 + dh) % 4]

                    @pp.block.tensor
                    def _(te):
                        if tile == 0 and j == 0 and dh == 0:
                            pp.wait(te, "dpool", w_ready)
                        bank.wait_free(te)
                        for c in range(CD):
                            mm = te.matmul(bank.t[:, :],
                                           xr[:, c * 512 + j * 128: c * 512 + j * 128 + 128],
                                           wr[:, c * 1024 + dh * 512: c * 1024 + dh * 512 + 512],
                                           start=(c == 0), stop=(c == CD - 1))
                        pp.inc(mm, "frmm", 1)

                    @pp.block.scalar
                    def _(se):
                        pp.wait(se, "frmm")
                        ev = se.activation(r_sb[:, j * 1024 + dh * 512: j * 1024 + dh * 512 + 512],
                                           bank.t[:, :], AF.Sigmoid)
                        bank.release(ev)

            for dk in range(CD):
                bank = g_banks[dk % 4]
                kr = krelu[dk % 2]

                @pp.block.tensor
                def _(te):
                    bank.wait_free(te)
                    for c in range(CD):
                        mm = te.matmul(bank.t[:, :],
                                       wk[:, c * 1024 + dk * 128: c * 1024 + dk * 128 + 128],
                                       xk[:, c * 512:(c + 1) * 512],
                                       start=(c == 0), stop=(c == CD - 1))
                    pp.inc(mm, "fkmm", 1)

                @pp.block.vector
                def _(ve):
                    pp.wait(ve, "fkmm")
                    if pp.val("fkev") >= 2:
                        pp.wait(ve, "fkev", pp.cnt["fkev"] - 1)
                    mx = ve.tensor_scalar(kr[:, :], bank.t[:, :], 0.0, None, ALU.max)
                    bank.release(mx)

                @pp.block.scalar
                def _(se):
                    bank.wait_released(se)
                    pp.inc(se.activation(kact[:, dk * 512:(dk + 1) * 512], kr[:, :],
                                         AF.Square), "fkev", 1)

            pp.fence()

            for j in range(4):
                for dh in range(2):
                    bank = g_banks[(j * 2 + dh) % 4]

                    @pp.block.tensor
                    def _(te):
                        bank.wait_free(te)
                        for dk in range(CD):
                            mm = te.matmul(bank.t[:, :],
                                           kact[:, dk * 512 + j * 128: dk * 512 + j * 128 + 128],
                                           wv[:, dk * 1024 + dh * 512: dk * 1024 + dh * 512 + 512],
                                           start=(dk == 0), stop=(dk == CD - 1))
                        pp.inc(mm, "fomm", 1)

                    @pp.block.vector
                    def _(ve):
                        pp.wait(ve, "fomm")
                        tc = tile * 4 + j
                        xv = x_sb[:, tc * 1024 + dh * 512: tc * 1024 + dh * 512 + 512]
                        stt = ve.scalar_tensor_tensor(t1[:, :], bank.t[:, :], 0.5,
                                                      r_sb[:, j * 1024 + dh * 512: j * 1024 + dh * 512 + 512],
                                                      ALU.mult, ALU.mult)
                        bank.release(stt)
                        ve.drain()
                        ve.tensor_add(xv, xv, t1[:, :])
                        ve.drain()

            pp.fence()

        p.cnt.update(pp.cnt)






def emit_xhat_fm(pp, ln, x_sb, tile, xhat_fm, xhat_bufs, tr_banks, ident, s_tr):
    for j in range(4):
        tc = tile * 4 + j
        ln.stats_emit(x_sb[:, tc * 1024:(tc + 1) * 1024], j)
    ln.rstds_emit()
    for j in range(4):
        tc = tile * 4 + j
        xht = xhat_bufs[j % 2]
        if j >= 2:
            @pp.block.vector
            def _(ve):
                pp.wait(ve, s_tr, pp.val(s_tr) - CD)
        ln.apply_emit(x_sb[:, tc * 1024:(tc + 1) * 1024], xht[:], j)
        ln_n = pp.val(ln.rdy)
        for c in range(CD):
            bank = tr_banks[c % len(tr_banks)]

            @pp.block.tensor
            def _(te):
                pp.wait(te, ln.rdy, ln_n)
                bank.wait_free(te)
                pp.inc(te.transpose(bank.t[:, :], xht[:, c * 128:(c + 1) * 128],
                                    ident[:]), s_tr, 1)

            @pp.block.vector
            def _(ve):
                pp.wait(ve, s_tr)
                ev = ve.tensor_copy(xhat_fm[:, c * 512 + j * 128: c * 512 + j * 128 + 128],
                                    bank.t[:, :])
                bank.release(ev)

    @pp.block.vector
    def _(ve):
        pp.inc(ve.drain(), "xfdone", 1)
    return pp.val("xfdone")



def emit_moe(p, x_sb, wg_d, w1_d, b1r_d, w2_d, b2_d, consts, n_tok, tag,
             cast_w=True):
    NTILE = n_tok // 512
    ident = consts["ident"]

    with contextlib.ExitStack() as st:
        pp = p.fresh(st)

        wg = sbuf(pp, "mwg", [128, CD * 8], F32)
        b1r = sbuf(pp, "mb1", [128, 256], F32)
        b2 = sbuf(pp, "mb2", [8, 1024], F32)
        xhat_fm = sbuf(pp, "mxf", [128, CD * 512], BF16)
        xhat = [sbuf(pp, f"mxt{i}", [128, 1024], BF16) for i in range(2)]
        xhf32 = sbuf(pp, "mxg", [128, 1024], F32)
        xf32c = sbuf(pp, "mxgc", [128, CD * 128], F32)
        w1 = sbuf(pp, "mw1", [128, CD * 4096], BF16)
        w2 = [sbuf(pp, f"mw2{i}", [128, 4 * 512], BF16) for i in range(2)]
        h_sb = sbuf(pp, "mh", [128, 32 * 512], BF16)
        e8 = sbuf(pp, "me8", [128, 8], F32)
        ge8 = sbuf(pp, "mge", [128, 8], F32)
        tmp8 = sbuf(pp, "mt8", [128, 8], F32)
        msk2 = sbuf(pp, "mm2", [128, 8], F32)
        m1n = sbuf(pp, "mm1n", [128, 1], F32)
        m2n = sbuf(pp, "mm2n", [128, 1], F32)
        lsum = sbuf(pp, "mls", [128, 1], F32)
        lrc = sbuf(pp, "mlr", [128, 1], F32)
        maskp = sbuf(pp, "mmp", [128, 4 * 8], F32)
        mtT = sbuf(pp, "mmt", [8, 4 * 128], F32)

        tr_banks = [psum_bank(pp, f"mtr{i}", [128, 128], BF16) for i in range(1)]
        h_banks = [psum_bank(pp, f"mhb{i}", [128, 512], F32) for i in range(2)]
        y_banks = [psum_bank(pp, f"myb{i}", [128, 512], F32) for i in range(4)]
        sm_bank = psum_bank(pp, "msm", [128, 512], F32)

        ln = LN(pp, "m", consts["eps"])
        s_tr = "mtr"
        dqeng = "gpsimd" if cast_w else "sync"

        @pp.block.gpsimd
        def _(g):
            pp.inc(g.dma_start(out=wg[:].rearrange("p (c e) -> p c e", c=CD),
                               in_=wg_d.rearrange("(c p) e -> p c e", p=128)), "dwg", 16)
        @pp.block.sync
        def _(sy):
            pp.inc(sy.dma_start(out=b1r[:], in_=b1r_d), "dsync2", 16)
            pp.inc(sy.dma_start(out=b2[:], in_=b2_d), "dsync2", 16)
        smalls_g = pp.val("dwg")
        smalls_s = pp.val("dsync2")

        def load_w1(e):
            eng = getattr(pp.block, dqeng)

            @eng
            def _(g):
                if pp.val("mw1f") > 0:
                    pp.wait(g, "mw1f")
                o3 = w1[:].rearrange("p (c h) -> p c h", c=CD)
                i3 = w1_d[e].rearrange("(c p) h -> p c h", p=128)
                pp.inc(g.dma_start(out=o3, in_=i3), "dw1", 16)
            return pp.val("dw1")

        def load_w2(e, dh, hq, buf):
            eng = getattr(pp.block, dqeng)

            @eng
            def _(g):
                if pp.val(f"mw2f{buf}") > 0:
                    pp.wait(g, f"mw2f{buf}")
                o3 = w2[buf][:].rearrange("p (q d) -> p q d", q=4)
                i3 = w2_d[e][hq * 512:(hq + 1) * 512,
                             dh * 512:(dh + 1) * 512].rearrange("(q p) d -> p q d", p=128)
                pp.inc(g.dma_start(out=o3, in_=i3), f"dw2{buf}", 16)
            return pp.val(f"dw2{buf}")

        w1_rdy = load_w1(0)

        for tile in range(NTILE):
            for j in range(4):
                ln.stats_emit(x_sb[:, (tile * 4 + j) * 1024:(tile * 4 + j + 1) * 1024], j)
            ln.rstds_emit()
            for j in range(4):
                tc = tile * 4 + j
                xht = xhat[j % 2]
                if j >= 2:
                    @pp.block.vector
                    def _(ve):
                        pp.wait(ve, s_tr, pp.val(s_tr) - 2 * CD)
                ln.apply_emit(x_sb[:, tc * 1024:(tc + 1) * 1024], xht[:], j)
                ln_n = pp.val(ln.rdy)
                @pp.block.vector
                def _(ve):
                    if pp.val(s_tr) > 0:
                        pp.wait(ve, s_tr)
                    pp.inc(ve.tensor_scalar(xhf32[:], x_sb[:, tc * 1024:(tc + 1) * 1024],
                                            ln.mv4[:, 2 * j:2 * j + 1],
                                            ln.rstd4[:, j:j + 1],
                                            ALU.subtract, ALU.mult), "mf32", 1)
                f32_n = pp.val("mf32")
                for c in range(CD):
                    bank = tr_banks[0]

                    @pp.block.tensor
                    def _(te):
                        pp.wait(te, ln.rdy, ln_n)
                        bank.wait_free(te)
                        pp.inc(te.transpose(bank.t[:, :], xht[:, c * 128:(c + 1) * 128],
                                            ident[:]), s_tr, 1)

                    @pp.block.vector
                    def _(ve):
                        pp.wait(ve, s_tr)
                        ev = ve.tensor_copy(xhat_fm[:, c * 512 + j * 128: c * 512 + j * 128 + 128],
                                            bank.t[:, :])
                        bank.release(ev)
                for c in range(CD):
                    @pp.block.tensor
                    def _(te):
                        pp.wait(te, "mf32", f32_n)
                        sm_bank.wait_free(te)
                        pp.inc(te.transpose(sm_bank.t[0:128, 0:128],
                                            xhf32[:, c * 128:(c + 1) * 128],
                                            consts["identf"][:]), s_tr, 1)

                    @pp.block.vector
                    def _(ve):
                        pp.wait(ve, s_tr)
                        ev = ve.tensor_copy(xf32c[:, c * 128:(c + 1) * 128], sm_bank.t[0:128, 0:128])
                        sm_bank.release(ev)

                @pp.block.tensor
                def _(te):
                    if tile == 0 and j == 0:
                        pp.wait(te, "dwg", smalls_g)
                        pp.wait(te, "dsync2", smalls_s)
                    sm_bank.wait_free(te)
                    for c in range(CD):
                        mm = te.matmul(sm_bank.t[:, 0:8],
                                       xf32c[:, c * 128:(c + 1) * 128],
                                       wg[:, c * 8:(c + 1) * 8],
                                       start=(c == 0), stop=(c == CD - 1))
                    pp.inc(mm, "mgmm", 1)

                @pp.block.vector
                def _(ve):
                    pp.wait(ve, "mgmm")
                    gps = sm_bank.t[:, 0:8]
                    ve.reduce_max(m1n[:], gps, axis=AX.X, negate=True)
                    ve.drain()
                    ve.tensor_scalar(ge8[:], gps, m1n[:, 0:1], 0.0, ALU.add, ALU.is_ge)
                    ve.drain()
                    ve.tensor_scalar(tmp8[:], ge8[:], -1e30, None, ALU.mult)
                    ve.drain()
                    ve.tensor_add(msk2[:], gps, tmp8[:])
                    ve.drain()
                    ve.reduce_max(m2n[:], msk2[:], axis=AX.X, negate=True)
                    ve.drain()
                    pp.inc(ve.tensor_scalar(ge8[:], gps, m2n[:, 0:1], 0.0, ALU.add, ALU.is_ge),
                           "mdv", 1)

                @pp.block.scalar
                def _(se):
                    pp.wait(se, "mdv")
                    pp.inc(se.activation(e8[:], sm_bank.t[:, 0:8], AF.Exp, bias=m1n[:, 0:1]),
                           "mex", 1)

                @pp.block.vector
                def _(ve):
                    pp.wait(ve, "mex")
                    em = ve.tensor_mul(e8[:], e8[:], ge8[:])
                    sm_bank.release(em)
                    ve.drain()
                    ve.reduce_sum(lsum[:], e8[:], axis=AX.X)
                    ve.drain()
                    ve.reciprocal(lrc[:], lsum[:])
                    ve.drain()
                    pp.inc(ve.tensor_scalar(maskp[:, j * 8:(j + 1) * 8], e8[:],
                                            lrc[:, 0:1], 0.0625, ALU.mult, ALU.mult),
                           "mmp", 1)

            @pp.block.scalar
            def _(se):
                pp.inc(se.drain(), "xfdone", 1)
            xf_n = pp.val("xfdone")

            pp.fence()
            mp_n = pp.val("mmp")
            for j in range(4):
                @pp.block.tensor
                def _(te):
                    pp.wait(te, "mmp", mp_n)
                    sm_bank.wait_free(te)
                    pp.inc(te.transpose(sm_bank.t[0:8, 0:128],
                                        maskp[:, j * 8:(j + 1) * 8], consts["identf"][:]),
                           "mtp", 1)

                @pp.block.vector
                def _(ve):
                    pp.wait(ve, "mtp")
                    ev = ve.tensor_copy(mtT[0:8, j * 128:(j + 1) * 128], sm_bank.t[0:8, 0:128])
                    sm_bank.release(ev)
            for j in range(4):
                for dh in range(2):
                    @pp.block.tensor
                    def _(te):
                        sm_bank.wait_free(te)
                        pp.inc(te.matmul(sm_bank.t[:, 0:512],
                                         mtT[0:8, j * 128:(j + 1) * 128],
                                         b2[0:8, dh * 512:(dh + 1) * 512],
                                         start=True, stop=True), "mb2m", 1)

                    @pp.block.vector
                    def _(ve):
                        pp.wait(ve, "mb2m")
                        tc = tile * 4 + j
                        xv = x_sb[:, tc * 1024 + dh * 512: tc * 1024 + dh * 512 + 512]
                        ad = ve.tensor_add(xv, xv, sm_bank.t[:, 0:512])
                        sm_bank.release(ad)
                        ve.drain()

            for e in range(8):
                for hc in range(32):
                    bank = h_banks[hc % 2]

                    @pp.block.tensor
                    def _(te):
                        if hc == 0:
                            pp.wait(te, "dw1", w1_rdy)
                            pp.wait(te, "xfdone", xf_n)
                        bank.wait_free(te)
                        for c in range(CD):
                            mm = te.matmul(bank.t[:, :],
                                           w1[:, c * 4096 + hc * 128: c * 4096 + hc * 128 + 128],
                                           xhat_fm[:, c * 512:(c + 1) * 512],
                                           start=(c == 0), stop=(c == CD - 1))
                        pp.inc(mm, "mh1", 1)

                    @pp.block.vector
                    def _(ve):
                        pp.wait(ve, "mh1")
                        ev = ve.tensor_scalar(h_sb[:, hc * 512:(hc + 1) * 512], bank.t[:, :],
                                              b1r[:, e * 32 + hc: e * 32 + hc + 1], 0.0,
                                              ALU.add, ALU.max)
                        bank.release(ev)

                h_rel = [(b, b.uses) for b in h_banks]

                @pp.block.tensor
                def _(te):
                    pp.inc(te.drain(), "mw1f", 1)
                if not (e == 7 and tile == NTILE - 1):
                    w1_rdy = load_w1((e + 1) % 8)

                seq = [(dh, hq) for dh in (0, 1) for hq in range(8)]
                rdy = {}
                rdy[(0, 0)] = load_w2(e, 0, 0, 0)
                rdy[(0, 1)] = load_w2(e, 0, 1, 1)
                for idx, (dh, hq) in enumerate(seq):
                    buf = idx % 2

                    @pp.block.tensor
                    def _(te):
                        pp.wait(te, f"dw2{idx % 2}", rdy[(dh, hq)])
                        if hq == 0:
                            for j in range(4):
                                y_banks[j].wait_free(te)
                            if idx == 0:
                                for b, u in h_rel:
                                    te.wait_ge(pp.sem(b.name), u)
                        for hc4 in range(4):
                            hc = hq * 4 + hc4
                            for j in range(4):
                                mm = te.matmul(y_banks[j].t[:, :],
                                               h_sb[:, hc * 512 + j * 128: hc * 512 + j * 128 + 128],
                                               w2[buf][:, hc4 * 512:(hc4 + 1) * 512],
                                               start=(hq == 0 and hc4 == 0),
                                               stop=(hq == 7 and hc4 == 3))
                        pp.inc(mm, f"mw2f{buf}", 1)
                        if hq == 7:
                            pp.inc(te.drain(), "my2", 1)

                    if idx + 2 < 16:
                        nd, nh = seq[idx + 2]
                        rdy[(nd, nh)] = load_w2(e, nd, nh, buf)

                    if hq == 7:
                        y_done = pp.val("my2")

                        @pp.block.vector
                        def _(ve):
                            pp.wait(ve, "my2", y_done)
                            for j in range(4):
                                tc = tile * 4 + j
                                xv = x_sb[:, tc * 1024 + dh * 512: tc * 1024 + dh * 512 + 512]
                                stt = ve.scalar_tensor_tensor(xv, y_banks[j].t[:, :],
                                                              maskp[:, j * 8 + e: j * 8 + e + 1],
                                                              xv, ALU.mult, ALU.add)
                                y_banks[j].release(stt)
                                ve.drain()

            pp.fence()

        p.cnt.update(pp.cnt)





def emit_attn0(p, x_sb, wattn_d, wproj_d, consts, n_tok, tag):
    NTILE = n_tok // 512
    NT = n_tok // 128
    ident = consts["ident"]
    identf = consts["identf"]

    with contextlib.ExitStack() as st:
        pp = p.fresh(st)

        K_sb = sbuf(pp, "aK", [128, CD * n_tok], BF16)
        V_sb = sbuf(pp, "aV", [128, NT * 1040], BF16)
        knm = sbuf(pp, "akm", [1, 16], F32)
        knt = sbuf(pp, "akt", [1, 1], F32)
        qn2m = sbuf(pp, "aqm", [1, 16], F32)
        mhn = sbuf(pp, "amh", [1, 16], F32)
        prod = sbuf(pp, "apr", [1, 16], F32)
        xhat_fm = sbuf(pp, "axf", [128, CD * 512], BF16)
        xhat = [sbuf(pp, f"axt{i}", [128, 1024], BF16) for i in range(2)]
        sq = sbuf(pp, "asq", [128, 512], BF16)
        mhb = sbuf(pp, "amb", [128, 1], F32)
        onesrow = consts["onesrow"]
        onespair = consts["onespair"]
        masksT = consts["masksT"]

        tr_banks = [psum_bank(pp, "atr", [128, 128], BF16)]
        s_banks = [psum_bank(pp, f"as{i}", [128, 512], F32) for i in range(2)]
        av_bank = psum_bank(pp, "aav", [128, 512], F32)
        sm_bank = psum_bank(pp, "asm", [128, 512], F32)

        ln = LN(pp, "a", consts["eps"])
        s_tr = "atr"

        with contextlib.ExitStack() as st1:
            p1 = pp.fresh(st1)
            wak = sbuf(p1, "awk", [128, CD * 1024], BF16)
            wav = sbuf(p1, "awv", [128, CD * 1024], BF16)

            @p1.block.gpsimd
            def _(g):
                for w_sb, lo in ((wak, 0), (wav, 2048)):
                    o3 = w_sb[:].rearrange("p (c d) -> p c d", c=CD)
                    i3 = wattn_d[:, lo:lo + 1024].rearrange("(c p) d -> p c d", p=128)
                    p1.inc(g.dma_start(out=o3, in_=i3), "adw1", 16)
            w1_rdy = p1.val("adw1")

            @p1.block.vector
            def _(ve):
                ve.memset(V_sb[:], 1.0)
                ve.drain()
                ve.memset(knm[:], 0.0)
                p1.inc(ve.drain(), "avini", 1)
            vini = p1.val("avini")

            for tile in range(NTILE):
                xf_n = emit_xhat_fm(p1, ln, x_sb, tile, xhat_fm, xhat, tr_banks, ident, s_tr)

                for ch in range(CD):
                    bank = s_banks[ch % 2]

                    @p1.block.tensor
                    def _(te):
                        if tile == 0 and ch == 0:
                            p1.wait(te, "adw1", w1_rdy)
                        p1.wait(te, "xfdone", xf_n)
                        bank.wait_free(te)
                        for c in range(CD):
                            mm = te.matmul(bank.t[:, :],
                                           wak[:, c * 1024 + ch * 128: c * 1024 + ch * 128 + 128],
                                           xhat_fm[:, c * 512:(c + 1) * 512],
                                           start=(c == 0), stop=(c == CD - 1))
                        p1.inc(mm, "akmm", 1)

                    @p1.block.vector
                    def _(ve):
                        p1.wait(ve, "akmm")
                        kv = K_sb[:, ch * n_tok + tile * 512: ch * n_tok + tile * 512 + 512]
                        ev = ve.tensor_copy(kv, bank.t[:, :])
                        bank.release(ev)
                        ve.drain()
                        p1.inc(ve.tensor_mul(sq[:, :], kv, kv), "aksq", 1)

                    for par in range(2):
                        @p1.block.tensor
                        def _(te):
                            p1.wait(te, "aksq")
                            sm_bank.wait_free(te)
                            p1.inc(te.matmul(sm_bank.t[0:1, 0:512],
                                             onespair[:, par:par + 1], sq[:, :],
                                             start=True, stop=True), "aknm", 1)

                        @p1.block.vector
                        def _(ve):
                            p1.wait(ve, "aknm")
                            h = 2 * ch + par
                            mx = ve.reduce_max(knt[0:1, 0:1], sm_bank.t[0:1, 0:512], axis=AX.X)
                            sm_bank.release(mx)
                            ve.drain()
                            ve.tensor_max(knm[0:1, h:h + 1], knm[0:1, h:h + 1], knt[0:1, 0:1])
                            ve.drain()

                for j in range(4):
                    tc = tile * 4 + j
                    for vh in range(2):
                        bank = s_banks[vh % 2]

                        @p1.block.tensor
                        def _(te):
                            bank.wait_free(te)
                            for c in range(CD):
                                mm = te.matmul(bank.t[:, :],
                                               xhat_fm[:, c * 512 + j * 128: c * 512 + j * 128 + 128],
                                               wav[:, c * 1024 + vh * 512: c * 1024 + vh * 512 + 512],
                                               start=(c == 0), stop=(c == CD - 1))
                            p1.inc(mm, "avmm", 1)

                        @p1.block.vector
                        def _(ve):
                            p1.wait(ve, "avmm")
                            if tile == 0 and j == 0 and vh == 0:
                                p1.wait(ve, "avini", vini)
                            ov = V_sb[:, tc * 1040 + vh * 520: tc * 1040 + vh * 520 + 520]
                            ov = ov.rearrange("p (h i) -> p h i", h=8)[:, :, 0:64]
                            iv = bank.t[:, :].rearrange("p (h i) -> p h i", h=8)
                            ev = ve.tensor_copy(ov, iv)
                            bank.release(ev)
                pp.fence()

            pp.fence()
            pp.cnt.update(p1.cnt)

        with contextlib.ExitStack() as st2:
            p2 = pp.fresh(st2)
            waq = sbuf(p2, "awq", [128, CD * 1024], BF16)
            wap = sbuf(p2, "awp", [128, CD * 512], BF16)
            Qh = sbuf(p2, "aQ", [128, 16 * 512], BF16)
            STx = [sbuf(p2, f"aST{i}", [128, 512], BF16) for i in range(2)]
            y_sb = sbuf(p2, "ay", [128, CD * 512], BF16)
            rl = sbuf(p2, "arl", [1, 512], F32)
            lbc = sbuf(p2, "albc", [64, 512], F32)

            @p2.block.gpsimd
            def _(g):
                p2.wait(g, "adw1")
                p2.wait(g, "fence")
                o3 = waq[:].rearrange("p (c d) -> p c d", c=CD)
                i3 = wattn_d[:, 1024:2048].rearrange("(c p) d -> p c d", p=128)
                p2.inc(g.dma_start(out=o3, in_=i3), "adw2", 16)
            w2_rdy = p2.val("adw2")

            def load_wap(dh):
                @p2.block.gpsimd
                def _(g):
                    if p2.val("awapf") > 0:
                        p2.wait(g, "awapf")
                    o3 = wap[:].rearrange("p (c d) -> p c d", c=CD)
                    i3 = wproj_d[:, dh * 512:(dh + 1) * 512].rearrange("(c p) d -> p c d", p=128)
                    p2.inc(g.dma_start(out=o3, in_=i3), "adwp", 16)
                return p2.val("adwp")

            for qt in range(NTILE):
                xf_n = emit_xhat_fm(p2, ln, x_sb, qt, xhat_fm, xhat, tr_banks, ident, s_tr)

                @p2.block.vector
                def _(ve):
                    ve.memset(Qh[:], 0.0)
                    p2.inc(ve.drain(), "aqz", 1)
                qz = p2.val("aqz")

                for qh in range(CD):
                    bank = s_banks[qh % 2]

                    @p2.block.tensor
                    def _(te):
                        if qt == 0 and qh == 0:
                            p2.wait(te, "adw2", w2_rdy)
                        p2.wait(te, "xfdone", xf_n)
                        bank.wait_free(te)
                        for c in range(CD):
                            mm = te.matmul(bank.t[:, :],
                                           waq[:, c * 1024 + qh * 128: c * 1024 + qh * 128 + 128],
                                           xhat_fm[:, c * 512:(c + 1) * 512],
                                           start=(c == 0), stop=(c == CD - 1))
                        p2.inc(mm, "aqmm", 1)

                    @p2.block.vector
                    def _(ve):
                        p2.wait(ve, "aqmm")
                        if qh == 0:
                            p2.wait(ve, "aqz", qz)
                        ve.tensor_scalar(Qh[0:64, (2 * qh) * 512:(2 * qh) * 512 + 512],
                                         bank.t[0:64, :], 0.125, None, ALU.mult)
                        ve.drain()
                        ev = ve.tensor_scalar(Qh[64:128, (2 * qh + 1) * 512:(2 * qh + 1) * 512 + 512],
                                              bank.t[64:128, :], 0.125, None, ALU.mult)
                        bank.release(ev)

                    @p2.block.vector
                    def _(ve):
                        qe = Qh[0:64, (2 * qh) * 512:(2 * qh) * 512 + 512]
                        qo = Qh[64:128, (2 * qh + 1) * 512:(2 * qh + 1) * 512 + 512]
                        ve.tensor_mul(sq[0:64, :], qe, qe)
                        ve.drain()
                        p2.inc(ve.tensor_mul(sq[64:128, :], qo, qo), "aqsq", 1)

                    for par in range(2):
                        @p2.block.tensor
                        def _(te):
                            p2.wait(te, "aqsq")
                            sm_bank.wait_free(te)
                            p2.inc(te.matmul(sm_bank.t[0:1, 0:512],
                                             onespair[:, par:par + 1], sq[:, :],
                                             start=True, stop=True), "aqnm", 1)

                        @p2.block.vector
                        def _(ve):
                            p2.wait(ve, "aqnm")
                            mx = ve.reduce_max(qn2m[0:1, 2 * qh + par:2 * qh + par + 1],
                                               sm_bank.t[0:1, 0:512], axis=AX.X)
                            sm_bank.release(mx)

                @p2.block.vector
                def _(ve):
                    ve.drain()
                    p2.inc(ve.tensor_mul(prod[0:1, :], qn2m[0:1, :], knm[0:1, :]),
                           "apro", 1)

                @p2.block.scalar
                def _(se):
                    p2.wait(se, "apro")
                    se.activation(prod[0:1, :], prod[0:1, :], AF.Ln)
                    se.drain()
                    p2.inc(se.activation(mhn[0:1, :], prod[0:1, :], AF.Exp, scale=0.5),
                           "amh1", 1)

                @p2.block.vector
                def _(ve):
                    p2.wait(ve, "amh1")
                    p2.inc(ve.tensor_scalar(mhn[0:1, :], mhn[0:1, :], -1.0, None, ALU.mult),
                           "amh2", 1)
                mh_n = p2.val("amh2")

                n_kc = 4 * (qt + 1)
                for h in range(16):
                    ch = h // 2
                    @p2.block.tensor
                    def _(te):
                        p2.wait(te, "amh2", mh_n)
                        sm_bank.wait_free(te)
                        p2.inc(te.matmul(sm_bank.t[0:128, 0:1], onesrow, mhn[0:1, h:h + 1],
                                         start=True, stop=True), "ambc", 1)

                    @p2.block.vector
                    def _(ve):
                        p2.wait(ve, "ambc")
                        ev = ve.tensor_copy(mhb[:, 0:1], sm_bank.t[0:128, 0:1])
                        sm_bank.release(ev)
                    mb_u = sm_bank.uses

                    for kc in range(n_kc):
                        bank = s_banks[kc % 2]
                        stx = STx[kc % 2]

                        @p2.block.tensor
                        def _(te):
                            bank.wait_free(te)
                            p2.inc(te.matmul(bank.t[:, :],
                                             K_sb[:, ch * n_tok + kc * 128: ch * n_tok + kc * 128 + 128],
                                             Qh[:, h * 512:(h + 1) * 512],
                                             start=True, stop=True), "asmm", 1)

                        @p2.block.scalar
                        def _(se):
                            p2.wait(se, "asmm")
                            if kc == 0:
                                se.wait_ge(p2.sem(sm_bank.name), mb_u)
                            if p2.val("astx") >= 2:
                                p2.wait(se, "astx", p2.cnt["astx"] - 1)
                            ev = se.activation(stx[:, :], bank.t[:, :], AF.Exp, bias=mhb[:, 0:1])
                            bank.release(ev)
                        s_u = (bank, bank.uses)

                        diag = kc - 4 * qt
                        if diag >= 0:
                            @p2.block.vector
                            def _(ve):
                                ve.wait_ge(p2.sem(s_u[0].name), s_u[1])
                                p2.inc(ve.tensor_mul(stx[:, :], stx[:, :], masksT[diag][:]),
                                       "asev", 1)

                        @p2.block.tensor
                        def _(te):
                            if diag >= 0:
                                p2.wait(te, "asev")
                            else:
                                te.wait_ge(p2.sem(s_u[0].name), s_u[1])
                            if kc == 0:
                                av_bank.wait_free(te)
                            mm = te.matmul(av_bank.t[0:65, :],
                                           V_sb[:, kc * 1040 + h * 65: kc * 1040 + h * 65 + 65],
                                           stx[:, :],
                                           start=(kc == 0), stop=(kc == n_kc - 1))
                            p2.inc(mm, "astx", 1)
                            if kc == n_kc - 1:
                                p2.inc(te.drain(), "aavd", 1)
                    av_n = p2.val("aavd")

                    @p2.block.vector
                    def _(ve):
                        p2.wait(ve, "aavd", av_n)
                        p2.inc(ve.reciprocal(rl[0:1, :], av_bank.t[64:65, 0:512]),
                               "arl", 1)

                    @p2.block.tensor
                    def _(te):
                        p2.wait(te, "arl")
                        sm_bank.wait_free(te)
                        p2.inc(te.matmul(sm_bank.t[0:64, 0:512], onesrow[0:1, 0:64], rl[0:1, :],
                                         start=True, stop=True), "albm", 1)

                    @p2.block.vector
                    def _(ve):
                        p2.wait(ve, "albm")
                        ev = ve.tensor_copy(lbc[:, :], sm_bank.t[0:64, 0:512])
                        sm_bank.release(ev)
                    lb_u = sm_bank.uses

                    @p2.block.vector
                    def _(ve):
                        ve.wait_ge(p2.sem(sm_bank.name), lb_u)
                        ev = ve.tensor_mul(y_sb[(h % 2) * 64:(h % 2) * 64 + 64, ch * 512:(ch + 1) * 512],
                                           av_bank.t[0:64, :], lbc[:, :])
                        av_bank.release(ev)

                p2.fence()

                for dh in range(2):
                    wap_rdy = load_wap(dh)
                    for j in range(4):
                        tc = qt * 4 + j
                        bank = s_banks[j % 2]

                        @p2.block.tensor
                        def _(te):
                            if j == 0:
                                p2.wait(te, "adwp", wap_rdy)
                            bank.wait_free(te)
                            for c in range(CD):
                                mm = te.matmul(bank.t[:, :],
                                               y_sb[:, c * 512 + j * 128: c * 512 + j * 128 + 128],
                                               wap[:, c * 512:(c + 1) * 512],
                                               start=(c == 0), stop=(c == CD - 1))
                            p2.inc(mm, "apmm", 1)
                            if j == 3:
                                p2.inc(te.drain(), "awapf", 1)

                        @p2.block.vector
                        def _(ve):
                            p2.wait(ve, "apmm")
                            xv = x_sb[:, tc * 1024 + dh * 512: tc * 1024 + dh * 512 + 512]
                            ad = ve.tensor_add(xv, xv, bank.t[:, :])
                            bank.release(ad)
                            ve.drain()

                p2.fence()

            pp.cnt.update(p2.cnt)

        p.cnt.update(pp.cnt)





def emit_attn1(p, x_sb, wqtp_d, wktp_d, wvt_d, cos_d, sin_d, consts, n_tok, tag):
    NTILE = n_tok // 512
    NT = n_tok // 128
    ident = consts["ident"]

    with contextlib.ExitStack() as st:
        pp = p.fresh(st)

        K_sb = sbuf(pp, "bK", [128, CD * n_tok], BF16)
        V_sb = sbuf(pp, "bV", [128, NT * 1024], BF16)
        xhat_fm = sbuf(pp, "bxf", [128, CD * 512], BF16)
        xhat = [sbuf(pp, f"bxt{i}", [128, 1024], BF16) for i in range(2)]
        raw = sbuf(pp, "braw", [128, 2 * 512], F32)
        tcos = sbuf(pp, "bco", [128, 512], F32)
        tsin = sbuf(pp, "bsi", [128, 512], F32)
        t1 = sbuf(pp, "bt1", [128, 512], F32)
        t2 = sbuf(pp, "bt2", [128, 512], F32)
        negm = sbuf(pp, "bnm", [128, 1], F32)
        lsum = sbuf(pp, "bl", [128, 1], F32)
        rl1 = sbuf(pp, "brl", [128, 1], F32)
        masksA = consts["masksA"]

        tr_banks = [psum_bank(pp, "btr", [128, 128], BF16)]
        s_banks = [psum_bank(pp, f"bs{i}", [128, 512], F32) for i in range(2)]
        av_bank = psum_bank(pp, "bav", [128, 512], F32)

        ln = LN(pp, "b", consts["eps"])
        s_tr = "btr"

        def load_tables(i, tile):
            @pp.block.sync
            def _(sy):
                if pp.val("btbf") > 0:
                    pp.wait(sy, "btbf")
                for dst, srcd in ((tcos, cos_d), (tsin, sin_d)):
                    i3 = srcd[i * 128:(i + 1) * 128, tile * 512:(tile + 1) * 512]
                    pp.inc(sy.dma_start(out=dst[:], in_=i3), "btb", 16)
            return pp.val("btb")

        def rope_pair(ppx, i, tile, w_sb, dst_sb, dst_col, tb_rdy, wname, w_rdy, xf_n):
            for half, ch in ((0, i), (1, i + 4)):
                bank = s_banks[half]

                @ppx.block.tensor
                def _(te):
                    if w_rdy is not None:
                        ppx.wait(te, wname, w_rdy)
                    ppx.wait(te, "xfdone", xf_n)
                    bank.wait_free(te)
                    for c in range(CD):
                        mm = te.matmul(bank.t[:, :],
                                       w_sb[:, c * 1024 + ch * 128: c * 1024 + ch * 128 + 128],
                                       xhat_fm[:, c * 512:(c + 1) * 512],
                                       start=(c == 0), stop=(c == CD - 1))
                    ppx.inc(mm, "brmm", 1)

                @ppx.block.vector
                def _(ve):
                    ppx.wait(ve, "brmm")
                    if half == 0 and ppx.val("btbf") > 0:
                        ppx.wait(ve, "btbf")
                    ev = ve.tensor_copy(raw[:, half * 512:(half + 1) * 512], bank.t[:, :])
                    bank.release(ev)
            r_us = [(s_banks[0], s_banks[0].uses), (s_banks[1], s_banks[1].uses)]

            @ppx.block.vector
            def _(ve):
                for b, u in r_us:
                    ve.wait_ge(ppx.sem(b.name), u)
                ppx.wait(ve, "btb", tb_rdy)
                r0 = raw[:, 0:512]
                r1 = raw[:, 512:1024]
                ve.tensor_mul(t1[:], r0, tcos[:])
                ve.drain()
                ve.tensor_mul(t2[:], r1, tsin[:])
                ve.drain()
                ve.tensor_sub(dst_sb[:, i * n_tok + dst_col: i * n_tok + dst_col + 512], t1[:], t2[:])
                ve.drain()
                ve.tensor_mul(t1[:], r1, tcos[:])
                ve.drain()
                ve.tensor_mul(t2[:], r0, tsin[:])
                ve.drain()
                ve.tensor_add(dst_sb[:, (i + 4) * n_tok + dst_col: (i + 4) * n_tok + dst_col + 512],
                              t1[:], t2[:])
                ppx.inc(ve.drain(), "btbf", 1)

        with contextlib.ExitStack() as st1:
            p1 = pp.fresh(st1)
            wk = sbuf(p1, "bwk", [128, CD * 1024], BF16)
            wv = sbuf(p1, "bwv", [128, CD * 1024], BF16)

            @p1.block.gpsimd
            def _(g):
                for w_sb, w_d in ((wk, wktp_d), (wv, wvt_d)):
                    o3 = w_sb[:].rearrange("p (c d) -> p c d", c=CD)
                    i3 = w_d.rearrange("(c p) d -> p c d", p=128)
                    p1.inc(g.dma_start(out=o3, in_=i3), "bdw1", 16)
            w1_rdy = p1.val("bdw1")

            for tile in range(NTILE):
                xf_n = emit_xhat_fm(p1, ln, x_sb, tile, xhat_fm, xhat, tr_banks, ident, s_tr)
                for i in range(4):
                    tb = load_tables(i, tile)
                    rope_pair(p1, i, tile, wk, K_sb, tile * 512, tb,
                              "bdw1", w1_rdy if (tile == 0 and i == 0) else None, xf_n)
                for j in range(4):
                    tc = tile * 4 + j
                    for dh in range(2):
                        bank = s_banks[dh]

                        @p1.block.tensor
                        def _(te):
                            bank.wait_free(te)
                            for c in range(CD):
                                mm = te.matmul(bank.t[:, :],
                                               xhat_fm[:, c * 512 + j * 128: c * 512 + j * 128 + 128],
                                               wv[:, c * 1024 + dh * 512: c * 1024 + dh * 512 + 512],
                                               start=(c == 0), stop=(c == CD - 1))
                            p1.inc(mm, "bvmm", 1)

                        @p1.block.vector
                        def _(ve):
                            p1.wait(ve, "bvmm")
                            ev = ve.tensor_copy(V_sb[:, tc * 1024 + dh * 512: tc * 1024 + dh * 512 + 512],
                                                bank.t[:, :])
                            bank.release(ev)
                pp.fence()
            pp.cnt.update(p1.cnt)

        with contextlib.ExitStack() as st2:
            p2 = pp.fresh(st2)
            wq = sbuf(p2, "bwq", [128, CD * 1024], BF16)
            Qrot = sbuf(p2, "bQ", [128, CD * 512], BF16)
            Sf = sbuf(p2, "bS", [128, n_tok], F32)
            STe = sbuf(p2, "bSe", [128, n_tok], BF16)
            STt = sbuf(p2, "bSt", [128, n_tok], BF16)

            @p2.block.gpsimd
            def _(g):
                p2.wait(g, "bdw1")
                p2.wait(g, "fence")
                o3 = wq[:].rearrange("p (c d) -> p c d", c=CD)
                i3 = wqtp_d.rearrange("(c p) d -> p c d", p=128)
                p2.inc(g.dma_start(out=o3, in_=i3), "bdw2", 16)
            w2_rdy = p2.val("bdw2")

            for qt in range(NTILE):
                xf_n = emit_xhat_fm(p2, ln, x_sb, qt, xhat_fm, xhat, tr_banks, ident, s_tr)
                for i in range(4):
                    tb = load_tables(i, qt)
                    for half, ch in ((0, i), (1, i + 4)):
                        bank = s_banks[half]

                        @p2.block.tensor
                        def _(te):
                            if qt == 0 and i == 0 and half == 0:
                                p2.wait(te, "bdw2", w2_rdy)
                            p2.wait(te, "xfdone", xf_n)
                            bank.wait_free(te)
                            for c in range(CD):
                                mm = te.matmul(bank.t[:, :],
                                               wq[:, c * 1024 + ch * 128: c * 1024 + ch * 128 + 128],
                                               xhat_fm[:, c * 512:(c + 1) * 512],
                                               start=(c == 0), stop=(c == CD - 1))
                            p2.inc(mm, "brmm", 1)

                        @p2.block.vector
                        def _(ve):
                            p2.wait(ve, "brmm")
                            if half == 0 and p2.val("btbf") > 0:
                                p2.wait(ve, "btbf")
                            ev = ve.tensor_copy(raw[:, half * 512:(half + 1) * 512], bank.t[:, :])
                            bank.release(ev)
                    r_us = [(s_banks[0], s_banks[0].uses), (s_banks[1], s_banks[1].uses)]

                    @p2.block.vector
                    def _(ve):
                        for b, u in r_us:
                            ve.wait_ge(p2.sem(b.name), u)
                        p2.wait(ve, "btb", tb)
                        r0 = raw[:, 0:512]
                        r1 = raw[:, 512:1024]
                        ve.tensor_mul(t1[:], r0, tcos[:])
                        ve.drain()
                        ve.tensor_mul(t2[:], r1, tsin[:])
                        ve.drain()
                        ve.tensor_sub(Qrot[:, i * 512:(i + 1) * 512], t1[:], t2[:])
                        ve.drain()
                        ve.tensor_mul(t1[:], r1, tcos[:])
                        ve.drain()
                        ve.tensor_mul(t2[:], r0, tsin[:])
                        ve.drain()
                        ve.tensor_add(Qrot[:, (i + 4) * 512:(i + 5) * 512], t1[:], t2[:])
                        p2.inc(ve.drain(), "btbf", 1)

                @p2.block.vector
                def _(ve):
                    p2.inc(ve.drain(), "bqr", 1)
                q_n = p2.val("bqr")

                for qq in range(4):
                    k_hi = qt * 512 + (qq + 1) * 128
                    n_k5 = (k_hi + 511) // 512
                    for kc in range(n_k5):
                        bank = s_banks[kc % 2]
                        diag = (kc == n_k5 - 1)

                        @p2.block.tensor
                        def _(te):
                            if kc == 0:
                                p2.wait(te, "bqr", q_n)
                            bank.wait_free(te)
                            for c in range(CD):
                                mm = te.matmul(bank.t[:, :],
                                               Qrot[:, c * 512 + qq * 128: c * 512 + qq * 128 + 128],
                                               K_sb[:, c * n_tok + kc * 512: c * n_tok + kc * 512 + 512],
                                               start=(c == 0), stop=(c == CD - 1))
                            p2.inc(mm, "bsmm", 1)

                        if diag:
                            dd = (qt * 512 + qq * 128 - kc * 512) // 128

                            @p2.block.vector
                            def _(ve):
                                p2.wait(ve, "bsmm")
                                if p2.val("bex") > 0:
                                    p2.wait(ve, "bex")
                                ad = ve.tensor_add(Sf[:, kc * 512:(kc + 1) * 512],
                                                   bank.t[:, :], masksA[dd][:])
                                bank.release(ad)
                        else:
                            @p2.block.vector
                            def _(ve):
                                p2.wait(ve, "bsmm")
                                ev = ve.tensor_copy(Sf[:, kc * 512:(kc + 1) * 512], bank.t[:, :])
                                bank.release(ev)
                    s_rel = [(b, b.uses) for b in s_banks]
                    k512 = n_k5 * 512

                    @p2.block.vector
                    def _(ve):
                        for b, u in s_rel:
                            ve.wait_ge(p2.sem(b.name), u)
                        ve.reduce_max(negm[:], Sf[:, 0:k512], axis=AX.X, negate=True)
                        ve.drain()
                        p2.inc(ve.tensor_scalar(negm[:], negm[:], 0.03125, None, ALU.mult),
                               "bnm", 1)

                    @p2.block.scalar
                    def _(se):
                        p2.wait(se, "bnm")
                        if p2.val("bstr") > 0:
                            p2.wait(se, "bstr")
                        p2.inc(se.activation(STe[:, 0:k512], Sf[:, 0:k512], AF.Exp,
                                             bias=negm[:, 0:1], scale=0.03125,
                                             accum_out=lsum[:, 0:1]), "bex", 1)
                    e_n = p2.val("bex")

                    for kc in range(k512 // 128):
                        bank = tr_banks[0]

                        @p2.block.tensor
                        def _(te):
                            p2.wait(te, "bex", e_n)
                            bank.wait_free(te)
                            p2.inc(te.transpose(bank.t[:, :], STe[:, kc * 128:(kc + 1) * 128],
                                                ident[:]), "bstr", 1)

                        @p2.block.vector
                        def _(ve):
                            p2.wait(ve, "bstr")
                            ev = ve.tensor_copy(STt[:, kc * 128:(kc + 1) * 128], bank.t[:, :])
                            bank.release(ev)
                    stt_u = (tr_banks[0], tr_banks[0].uses)

                    @p2.block.vector
                    def _(ve):
                        p2.wait(ve, "bex", e_n)
                        p2.inc(ve.reciprocal(rl1[:], lsum[:, 0:1]), "brc", 1)
                    rc_n = p2.val("brc")

                    for dh in range(2):
                        @p2.block.tensor
                        def _(te):
                            if dh == 0:
                                te.wait_ge(p2.sem(stt_u[0].name), stt_u[1])
                            av_bank.wait_free(te)
                            for kc in range(k512 // 128):
                                mm = te.matmul(av_bank.t[:, :],
                                               STt[:, kc * 128:(kc + 1) * 128],
                                               V_sb[:, kc * 1024 + dh * 512: kc * 1024 + dh * 512 + 512],
                                               start=(kc == 0), stop=(kc == k512 // 128 - 1))
                            p2.inc(mm, "bavm", 1)

                        @p2.block.vector
                        def _(ve):
                            p2.wait(ve, "bavm")
                            if dh == 0:
                                p2.wait(ve, "brc", rc_n)
                            tc = qt * 4 + qq
                            xv = x_sb[:, tc * 1024 + dh * 512: tc * 1024 + dh * 512 + 512]
                            stt = ve.scalar_tensor_tensor(xv, av_bank.t[:, :], rl1[:, 0:1],
                                                          xv, ALU.mult, ALU.add)
                            av_bank.release(stt)
                            ve.drain()

                p2.fence()

            pp.cnt.update(p2.cnt)

        p.cnt.update(pp.cnt)



def _seq_io(p, block, x_sb, x_d, y_d, s, NT, load, consts=None, ident_dram=None):
    if load:
        @block.sync
        def _(sy):
            if consts is not None and s == 0:
                p.inc(sy.dma_start(out=consts["ident"][:], in_=ident_dram), "dsync", 16)
            o3 = x_sb[:].rearrange("p (c d) -> p c d", c=NT)
            i3 = x_d[s].rearrange("(c p) d -> p c d", p=128)
            p.inc(sy.dma_start(out=o3, in_=i3), "dsync", 16)
        n = p.val("dsync")
        for engname in ("tensor", "vector", "scalar"):
            @getattr(block, engname)
            def _(eng):
                p.wait(eng, "dsync", n)
    else:
        @block.sync
        def _(sy):
            sy.wait_ge(p.sem("fence"), p.cnt["fence"])
            o3 = y_d[s].rearrange("(c p) d -> p c d", p=128)
            i3 = x_sb[:].rearrange("p (c d) -> p c d", c=NT)
            p.inc(sy.dma_start(out=o3, in_=i3), "dstore", 16)
            p.wait(sy, "dstore")


def build_ffn0_only(n_tok):
    nc = bass.Bass("TRN2", target_bir_lowering=False)
    NT = n_tok // 128
    x_d = nc.dram_tensor("x", [2, n_tok, D], F32, kind="ExternalInput").ap()
    wr_d = nc.dram_tensor("wrt", [D, D], F32, kind="ExternalInput").ap()
    wk_d = nc.dram_tensor("wkt", [D, D], F32, kind="ExternalInput").ap()
    wv_d = nc.dram_tensor("wvt", [D, D], F32, kind="ExternalInput").ap()
    mur_d = nc.dram_tensor("mur", [D], F32, kind="ExternalInput").ap()
    muk_d = nc.dram_tensor("muk", [D], F32, kind="ExternalInput").ap()
    y_d = nc.dram_tensor("y", [2, n_tok, D], F32, kind="ExternalOutput").ap()

    with contextlib.ExitStack() as stack:
        p = P(nc, stack)
        ident_dram = nc.inline_tensor(np.eye(128, dtype=ml_dtypes.bfloat16), "identc").ap()
        ident = sbuf(p, "ident", [128, 128], BF16)
        eps_t = sbuf(p, "epsc", [128, 1], F32)
        x_sb = sbuf(p, "xres", [128, NT * 1024], F32)
        consts = {"ident": ident, "eps": eps_t[:, 0:1]}

        for s in range(2):
            with nc.Block() as block:
                p.block = block
                if s == 0:
                    @block.vector
                    def _(ve):
                        ve.memset(eps_t[:], LN_EPS)
                        ve.drain()
                _seq_io(p, block, x_sb, x_d, y_d, s, NT, True, consts, ident_dram)
                emit_ffn0(p, x_sb, wr_d, wk_d, wv_d, mur_d, muk_d, consts,
                          n_tok, f"s{s}")
                _seq_io(p, block, x_sb, x_d, y_d, s, NT, False)
    return nc




def build_moe_only(n_tok):
    nc = bass.Bass("TRN2", target_bir_lowering=False)
    NT = n_tok // 128
    x_d = nc.dram_tensor("x", [2, n_tok, D], F32, kind="ExternalInput").ap()
    wg_d = nc.dram_tensor("wg", [D, 8], F32, kind="ExternalInput").ap()
    w1_d = nc.dram_tensor("w1", [8, D, 4096], F32, kind="ExternalInput").ap()
    b1r_d = nc.dram_tensor("b1r", [128, 256], F32, kind="ExternalInput").ap()
    w2_d = nc.dram_tensor("w2", [8, 4096, D], F32, kind="ExternalInput").ap()
    b2_d = nc.dram_tensor("b2", [8, D], F32, kind="ExternalInput").ap()
    y_d = nc.dram_tensor("y", [2, n_tok, D], F32, kind="ExternalOutput").ap()

    with contextlib.ExitStack() as stack:
        p = P(nc, stack)
        ident_dram = nc.inline_tensor(np.eye(128, dtype=ml_dtypes.bfloat16), "identc").ap()
        identf_dram = nc.inline_tensor(np.eye(128, dtype=np.float32), "identfc").ap()
        ident = sbuf(p, "ident", [128, 128], BF16)
        identf = sbuf(p, "identf", [128, 128], F32)
        eps_t = sbuf(p, "epsc", [128, 1], F32)
        x_sb = sbuf(p, "xres", [128, NT * 1024], F32)
        consts = {"ident": ident, "identf": identf, "eps": eps_t[:, 0:1]}

        for s in range(2):
            with nc.Block() as block:
                p.block = block
                if s == 0:
                    @block.vector
                    def _(ve):
                        ve.memset(eps_t[:], LN_EPS)
                        ve.drain()
                    @block.sync
                    def _(sy):
                        p.inc(sy.dma_start(out=identf[:], in_=identf_dram), "dsync", 16)
                _seq_io(p, block, x_sb, x_d, y_d, s, NT, True, consts, ident_dram)
                emit_moe(p, x_sb, wg_d, w1_d, b1r_d, w2_d, b2_d, consts,
                         n_tok, f"s{s}", cast_w=True)
                _seq_io(p, block, x_sb, x_d, y_d, s, NT, False)
    return nc




def make_attn0_consts(nc, p):
    op = np.zeros((128, 2), dtype=ml_dtypes.bfloat16)
    op[0:64, 0] = 1.0
    op[64:128, 1] = 1.0
    op_d = nc.inline_tensor(op, "onespair_c").ap()
    onespair = sbuf(p, "onespair", [128, 2], BF16)
    masks = []
    masks_d = []
    for d in range(4):
        m = np.zeros((128, 512), dtype=ml_dtypes.bfloat16)
        for pp_ in range(128):
            j0 = d * 128 + pp_
            if j0 < 512:
                m[pp_, j0:] = 1.0
        masks_d.append(nc.inline_tensor(m, f"maskT{d}_c").ap())
        masks.append(sbuf(p, f"maskT{d}", [128, 512], BF16))
    onesrow = sbuf(p, "onesrow", [1, 128], F32)
    return op_d, onespair, masks_d, masks, onesrow


def build_attn0_only(n_tok):
    nc = bass.Bass("TRN2", target_bir_lowering=False)
    NT = n_tok // 128
    x_d = nc.dram_tensor("x", [2, n_tok, D], F32, kind="ExternalInput").ap()
    wattn_d = nc.dram_tensor("wattn", [D, 3 * D], F32, kind="ExternalInput").ap()
    wproj_d = nc.dram_tensor("wproj", [D, D], F32, kind="ExternalInput").ap()
    y_d = nc.dram_tensor("y", [2, n_tok, D], F32, kind="ExternalOutput").ap()

    with contextlib.ExitStack() as stack:
        p = P(nc, stack)
        ident_dram = nc.inline_tensor(np.eye(128, dtype=ml_dtypes.bfloat16), "identc").ap()
        identf_dram = nc.inline_tensor(np.eye(128, dtype=np.float32), "identfc").ap()
        ident = sbuf(p, "ident", [128, 128], BF16)
        identf = sbuf(p, "identf", [128, 128], F32)
        eps_t = sbuf(p, "epsc", [128, 1], F32)
        x_sb = sbuf(p, "xres", [128, NT * 1024], F32)
        op_d, onespair, masks_d, masks, onesrow = make_attn0_consts(nc, p)
        consts = {"ident": ident, "identf": identf, "eps": eps_t[:, 0:1],
                  "onespair": onespair[:], "onesrow": onesrow[:],
                  "masksT": [m[:] for m in masks]}

        for s in range(2):
            with nc.Block() as block:
                p.block = block
                if s == 0:
                    @block.vector
                    def _(ve):
                        ve.memset(eps_t[:], LN_EPS)
                        ve.drain()
                        ve.memset(onesrow[:], 1.0)
                        ve.drain()
                    @block.sync
                    def _(sy):
                        p.inc(sy.dma_start(out=identf[:], in_=identf_dram), "dsync", 16)
                        p.inc(sy.dma_start(out=onespair[:], in_=op_d), "dsync", 16)
                        for m_sb, m_d in zip(masks, masks_d):
                            p.inc(sy.dma_start(out=m_sb[:], in_=m_d), "dsync", 16)
                _seq_io(p, block, x_sb, x_d, y_d, s, NT, True, consts, ident_dram)
                emit_attn0(p, x_sb, wattn_d, wproj_d, consts, n_tok, f"s{s}")
                _seq_io(p, block, x_sb, x_d, y_d, s, NT, False)
    return nc




def build_attn1_only(n_tok):
    nc = bass.Bass("TRN2", target_bir_lowering=False)
    NT = n_tok // 128
    x_d = nc.dram_tensor("x", [2, n_tok, D], F32, kind="ExternalInput").ap()
    wqtp_d = nc.dram_tensor("wqtp", [D, D], F32, kind="ExternalInput").ap()
    wktp_d = nc.dram_tensor("wktp", [D, D], F32, kind="ExternalInput").ap()
    wvt_d = nc.dram_tensor("wvt", [D, D], F32, kind="ExternalInput").ap()
    cos_d = nc.dram_tensor("ropec", [512, n_tok], F32, kind="ExternalInput").ap()
    sin_d = nc.dram_tensor("ropes", [512, n_tok], F32, kind="ExternalInput").ap()
    y_d = nc.dram_tensor("y", [2, n_tok, D], F32, kind="ExternalOutput").ap()

    with contextlib.ExitStack() as stack:
        p = P(nc, stack)
        ident_dram = nc.inline_tensor(np.eye(128, dtype=ml_dtypes.bfloat16), "identc").ap()
        ident = sbuf(p, "ident", [128, 128], BF16)
        eps_t = sbuf(p, "epsc", [128, 1], F32)
        x_sb = sbuf(p, "xres", [128, NT * 1024], F32)
        masksA_np = []
        masksA_d = []
        masksA = []
        for dd in range(4):
            m = np.zeros((128, 512), dtype=ml_dtypes.bfloat16)
            for pp_ in range(128):
                j0 = dd * 128 + pp_
                if j0 + 1 < 512:
                    m[pp_, j0 + 1:] = -1e30
            masksA_d.append(nc.inline_tensor(m, f"maskA{dd}_c").ap())
            masksA.append(sbuf(p, f"maskA{dd}", [128, 512], BF16))
        consts = {"ident": ident, "eps": eps_t[:, 0:1],
                  "masksA": [m[:] for m in masksA]}

        for s in range(2):
            with nc.Block() as block:
                p.block = block
                if s == 0:
                    @block.vector
                    def _(ve):
                        ve.memset(eps_t[:], LN_EPS)
                        ve.drain()
                    @block.sync
                    def _(sy):
                        for m_sb, m_d in zip(masksA, masksA_d):
                            p.inc(sy.dma_start(out=m_sb[:], in_=m_d), "dsync", 16)
                _seq_io(p, block, x_sb, x_d, y_d, s, NT, True, consts, ident_dram)
                emit_attn1(p, x_sb, wqtp_d, wktp_d, wvt_d, cos_d, sin_d, consts,
                           n_tok, f"s{s}")
                _seq_io(p, block, x_sb, x_d, y_d, s, NT, False)
    return nc




def build_full(n_tok):
    nc = bass.Bass("TRN2", target_bir_lowering=False)
    NT = n_tok // 128
    x_d = nc.dram_tensor("x", [2, n_tok, D], F32, kind="ExternalInput").ap()
    wrt_d = nc.dram_tensor("wrt", [D, D], F32, kind="ExternalInput").ap()
    wkt_d = nc.dram_tensor("wkt", [D, D], F32, kind="ExternalInput").ap()
    wvt_d = nc.dram_tensor("wvt", [D, D], F32, kind="ExternalInput").ap()
    mur_d = nc.dram_tensor("mur", [D], F32, kind="ExternalInput").ap()
    muk_d = nc.dram_tensor("muk", [D], F32, kind="ExternalInput").ap()
    wattn_d = nc.dram_tensor("wattn", [D, 3 * D], F32, kind="ExternalInput").ap()
    wproj_d = nc.dram_tensor("wproj", [D, D], F32, kind="ExternalInput").ap()
    wg_d = nc.dram_tensor("wg", [D, 8], F32, kind="ExternalInput").ap()
    w1_d = nc.dram_tensor("w1", [8, D, 4096], F32, kind="ExternalInput").ap()
    b1r_d = nc.dram_tensor("b1r", [128, 256], F32, kind="ExternalInput").ap()
    w2_d = nc.dram_tensor("w2", [8, 4096, D], F32, kind="ExternalInput").ap()
    b2_d = nc.dram_tensor("b2", [8, D], F32, kind="ExternalInput").ap()
    wqtp_d = nc.dram_tensor("wqtp", [D, D], F32, kind="ExternalInput").ap()
    wktp_d = nc.dram_tensor("wktp", [D, D], F32, kind="ExternalInput").ap()
    wvt1_d = nc.dram_tensor("wvt1", [D, D], F32, kind="ExternalInput").ap()
    cos_d = nc.dram_tensor("ropec", [512, n_tok], F32, kind="ExternalInput").ap()
    sin_d = nc.dram_tensor("ropes", [512, n_tok], F32, kind="ExternalInput").ap()
    y_d = nc.dram_tensor("y", [2, n_tok, D], F32, kind="ExternalOutput").ap()

    with contextlib.ExitStack() as stack:
        p = P(nc, stack)
        ident_dram = nc.inline_tensor(np.eye(128, dtype=ml_dtypes.bfloat16), "identc").ap()
        identf_dram = nc.inline_tensor(np.eye(128, dtype=np.float32), "identfc").ap()
        ident = sbuf(p, "ident", [128, 128], BF16)
        identf = sbuf(p, "identf", [128, 128], F32)
        eps_t = sbuf(p, "epsc", [128, 1], F32)
        x_sb = sbuf(p, "xres", [128, NT * 1024], F32)
        op_d, onespair, masksT_d, masksT, onesrow = make_attn0_consts(nc, p)
        masksA_d = []
        masksA = []
        for dd in range(4):
            m = np.zeros((128, 512), dtype=ml_dtypes.bfloat16)
            for pp_ in range(128):
                j0 = dd * 128 + pp_
                if j0 + 1 < 512:
                    m[pp_, j0 + 1:] = -1e30
            masksA_d.append(nc.inline_tensor(m, f"maskA{dd}_c").ap())
            masksA.append(sbuf(p, f"maskA{dd}", [128, 512], BF16))
        consts = {"ident": ident, "identf": identf, "eps": eps_t[:, 0:1],
                  "onespair": onespair[:], "onesrow": onesrow[:],
                  "masksT": [m[:] for m in masksT],
                  "masksA": [m[:] for m in masksA]}

        for s in range(2):
            with nc.Block() as block:
                p.block = block
                if s == 0:
                    @block.vector
                    def _(ve):
                        ve.memset(eps_t[:], LN_EPS)
                        ve.drain()
                        ve.memset(onesrow[:], 1.0)
                        ve.drain()

                    @block.sync
                    def _(sy):
                        p.inc(sy.dma_start(out=identf[:], in_=identf_dram), "dsync", 16)
                        p.inc(sy.dma_start(out=onespair[:], in_=op_d), "dsync", 16)
                        for m_sb, m_d in zip(masksT, masksT_d):
                            p.inc(sy.dma_start(out=m_sb[:], in_=m_d), "dsync", 16)
                        for m_sb, m_d in zip(masksA, masksA_d):
                            p.inc(sy.dma_start(out=m_sb[:], in_=m_d), "dsync", 16)
                _seq_io(p, block, x_sb, x_d, y_d, s, NT, True, consts, ident_dram)
                emit_ffn0(p, x_sb, wrt_d, wkt_d, wvt_d, mur_d, muk_d, consts,
                          n_tok, f"s{s}a")
            with nc.Block() as block:
                p.block = block
                emit_attn0(p, x_sb, wattn_d, wproj_d, consts, n_tok, f"s{s}")
            with nc.Block() as block:
                p.block = block
                emit_ffn0(p, x_sb, wrt_d, wkt_d, wvt_d, mur_d, muk_d, consts,
                          n_tok, f"s{s}b")
            with nc.Block() as block:
                p.block = block
                emit_moe(p, x_sb, wg_d, w1_d, b1r_d, w2_d, b2_d, consts,
                         n_tok, f"s{s}a", cast_w=True)
            with nc.Block() as block:
                p.block = block
                emit_attn1(p, x_sb, wqtp_d, wktp_d, wvt1_d, cos_d, sin_d, consts,
                           n_tok, f"s{s}")
            with nc.Block() as block:
                p.block = block
                emit_moe(p, x_sb, wg_d, w1_d, b1r_d, w2_d, b2_d, consts,
                         n_tok, f"s{s}b", cast_w=True)
                _seq_io(p, block, x_sb, x_d, y_d, s, NT, False)
    return nc


def _rope_tables_host(n_tok):
    pos = np.arange(n_tok, dtype=np.float64)[:, None]
    inv = 1.0 / 10000.0 ** (np.arange(0, D, 2, dtype=np.float64) / D)
    fr = pos * inv
    cos = np.ascontiguousarray(np.cos(fr).T.astype(np.float32))
    sin = np.ascontiguousarray(np.sin(fr).T.astype(np.float32))
    return cos, sin


def kernel(**inputs):
    n_tok = 2048
    n_cores = 8
    x = np.ascontiguousarray(np.asarray(inputs["L0_attn_x"], dtype=np.float32))

    wrt = np.ascontiguousarray(np.asarray(inputs["L0_ffn_W_r"]).T)
    wkt = np.ascontiguousarray(np.asarray(inputs["L0_ffn_W_k"]).T)
    wvt = np.ascontiguousarray(np.asarray(inputs["L0_ffn_W_v"]).T)
    mur = np.ascontiguousarray(np.asarray(inputs["L0_ffn_mu_r"]))
    muk = np.ascontiguousarray(np.asarray(inputs["L0_ffn_mu_k"]))
    wattn = np.ascontiguousarray(np.asarray(inputs["L0_attn_w_attn"]))
    wproj = np.ascontiguousarray(np.asarray(inputs["L0_attn_w_proj"]))
    wg = np.ascontiguousarray(np.asarray(inputs["L1_ffn_W_gate"]))
    w1 = np.ascontiguousarray(np.asarray(inputs["L1_ffn_W1"]))
    b1 = np.asarray(inputs["L1_ffn_b1"])
    b1r = np.ascontiguousarray(b1.reshape(8, 32, 128).transpose(2, 0, 1).reshape(128, 256))
    w2 = np.ascontiguousarray(np.asarray(inputs["L1_ffn_W2"]))
    b2 = np.ascontiguousarray(np.asarray(inputs["L1_ffn_b2"]))
    perm = np.concatenate([np.arange(0, D, 2), np.arange(1, D, 2)])
    wqtp = np.ascontiguousarray(np.asarray(inputs["L1_attn_wq"]).T[:, perm])
    wktp = np.ascontiguousarray(np.asarray(inputs["L1_attn_wk"]).T[:, perm])
    wvt1 = np.ascontiguousarray(np.asarray(inputs["L1_attn_wv"]).T)
    cosf, sinf = _rope_tables_host(n_tok)

    shards = [np.ascontiguousarray(x[2 * c: 2 * c + 2]) for c in range(n_cores)]

    nc = build_full(n_tok)
    weights = {"wrt": wrt, "wkt": wkt, "wvt": wvt, "mur": mur, "muk": muk,
               "wattn": wattn, "wproj": wproj,
               "wg": wg, "w1": w1, "b1r": b1r, "w2": w2, "b2": b2,
               "wqtp": wqtp, "wktp": wktp, "wvt1": wvt1,
               "ropec": cosf, "ropes": sinf}
    in_maps = [dict(weights, x=shards[c]) for c in range(n_cores)]
    res = run_bass_kernel_spmd(nc, in_maps, core_ids=list(range(n_cores)))
    return np.concatenate([np.asarray(r["y"]) for r in res.results], axis=0)



# version 05
# speedup vs baseline: 1.1700x; 1.0080x over previous
import contextlib
import numpy as np
import ml_dtypes
import concourse.bass as bass
import concourse.mybir as mybir
from concourse.bass_utils import run_bass_kernel_spmd

F32 = mybir.dt.float32
BF16 = mybir.dt.bfloat16
AF = mybir.ActivationFunctionType
ALU = mybir.AluOpType
AX = mybir.AxisListType

D = 1024
CD = D // 128
LN_EPS = 1e-5



class P:

    def __init__(self, nc, stack):
        self.nc = nc
        self.stack = stack
        self.root_stack = stack
        self.block = None
        self._sems = {}
        self.cnt = {}
        self.bku = {}
        self.uid = [0]

    def fresh(self, stack):
        q = P.__new__(P)
        q.__dict__.update(self.__dict__)
        q.stack = stack
        return q

    def sem(self, name):
        if name not in self._sems:
            self._sems[name] = self.root_stack.enter_context(self.nc.semaphore(name))
            self.cnt[name] = 0
        return self._sems[name]

    def inc(self, inst, name, n=1):
        inst.then_inc(self.sem(name), n)
        self.cnt[name] += n
        return self.cnt[name]

    def val(self, name):
        self.sem(name)
        return self.cnt[name]

    def wait(self, eng, name, v=None):
        eng.wait_ge(self.sem(name), self.cnt[name] if v is None else v)

    def fence(self, tag="ph"):
        name = "fence"
        self.sem(name)
        target = self.cnt[name] + 3

        @self.block.tensor
        def _(te):
            self.inc(te.drain(), name, 1)
            te.wait_ge(self._sems[name], target)

        @self.block.vector
        def _(ve):
            self.inc(ve.drain(), name, 1)
            ve.wait_ge(self._sems[name], target)

        @self.block.scalar
        def _(se):
            self.inc(se.drain(), name, 1)
            se.wait_ge(self._sems[name], target)
        assert self.cnt[name] == target


class Bank:

    def __init__(self, p, tensor, name):
        self.p = p
        self.t = tensor
        self.name = "bk_" + name
        p.sem(self.name)
        p.bku.setdefault(self.name, 0)

    @property
    def uses(self):
        return self.p.bku[self.name]

    def wait_free(self, eng):
        if self.uses > 0:
            eng.wait_ge(self.p.sem(self.name), self.uses)
        self.p.bku[self.name] += 1

    def release(self, inst):
        self.p.inc(inst, self.name, 1)

    def wait_released(self, eng):
        eng.wait_ge(self.p.sem(self.name), self.uses)


def sbuf(p, name, shape, dtype):
    p.uid[0] += 1
    return p.stack.enter_context(p.nc.sbuf_tensor(f"{name}_{p.uid[0]}", shape, dtype))


def psum_bank(p, name, shape, dtype=F32):
    p.uid[0] += 1
    t = p.stack.enter_context(p.nc.psum_tensor(f"{name}_{p.uid[0]}", shape, dtype))
    return Bank(p, t, name)



class LN:

    def __init__(self, p, tag, eps_ap):
        self.p = p
        self.eps = eps_ap
        self.stats = sbuf(p, f"lnst{tag}", [128, 12], F32)
        self.mv4 = sbuf(p, f"lnmv{tag}", [128, 8], F32)
        self.lnv = sbuf(p, f"lnlv{tag}", [128, 4], F32)
        self.rstd4 = sbuf(p, f"lnrs{tag}", [128, 4], F32)
        self.s_v2s = "lnv2s"
        self.s_s2v = "lns2v"
        self.rdy = "lnrdy"
        p.sem(self.s_v2s)
        p.sem(self.s_s2v)
        p.sem(self.rdy)

    def stats_emit(self, x_view, idx):
        p = self.p

        @p.block.vector
        def _(ve):
            if idx == 0 and p.val(self.s_s2v) > 0:
                p.wait(ve, self.s_s2v)
            ve.drain()
            ve.bn_stats(self.stats[:, 0:6], x_view[:, 0:512])
            ve.bn_stats(self.stats[:, 6:12], x_view[:, 512:1024])
            ve.drain()
            p.inc(ve.bn_aggr(self.mv4[:, 2 * idx:2 * idx + 2], self.stats[:]),
                  self.s_v2s, 1)

    def rstds_emit(self, n=4):
        p = self.p

        @p.block.scalar
        def _(se):
            p.wait(se, self.s_v2s)
            vars_view = self.mv4[:].rearrange("p (c two) -> p c two", two=2)[:, 0:n, 1:2]
            se.activation(self.lnv[:, 0:n], vars_view, AF.Ln, bias=self.eps)
            se.drain()
            se.activation(self.rstd4[:, 0:n], self.lnv[:, 0:n], AF.Exp, scale=-0.5)
            p.inc(se.drain(), self.s_s2v, 1)
        self._rn = p.val(self.s_s2v)

    def apply_emit(self, x_view, out_view, idx):
        p = self.p

        @p.block.vector
        def _(ve):
            p.wait(ve, self.s_s2v, self._rn)
            p.inc(ve.tensor_scalar(out_view, x_view, self.mv4[:, 2 * idx:2 * idx + 1],
                                   self.rstd4[:, idx:idx + 1],
                                   ALU.subtract, ALU.mult), self.rdy, 1)


def emit_ffn0(p, x_sb, wr_d, wk_d, wv_d, mur_d, muk_d, consts, n_tok, tag):
    NTILE = n_tok // 512
    ident = consts["ident"]

    with contextlib.ExitStack() as st:
        pp = p.fresh(st)

        wr = sbuf(pp, "fwr", [128, CD * 1024], BF16)
        wk = sbuf(pp, "fwk", [128, CD * 1024], BF16)
        wv = sbuf(pp, "fwv", [128, CD * 1024], BF16)
        mur = sbuf(pp, "fmur", [128, CD], F32)
        muk = sbuf(pp, "fmuk", [128, CD], F32)
        xh = sbuf(pp, "fxh", [128, CD * 512], BF16)
        xs = sbuf(pp, "fxs", [128, CD * 512], BF16)
        carry = sbuf(pp, "fcar", [128, CD], BF16)
        diff = sbuf(pp, "fdif", [128, CD * 512], BF16)
        xr = sbuf(pp, "fxr", [128, CD * 512], BF16)
        xk = sbuf(pp, "fxk", [128, CD * 512], BF16)
        kact = sbuf(pp, "fka", [128, CD * 512], BF16)
        krelu = [sbuf(pp, f"fkr{i}", [128, 512], F32) for i in range(2)]
        r_sb = sbuf(pp, "fr", [128, 4 * 1024], F32)
        t1 = sbuf(pp, "ft1", [128, 512], F32)
        xhat = [sbuf(pp, f"fxt{i}", [128, 1024], BF16) for i in range(2)]

        tr_banks = [psum_bank(pp, f"ftr{i}", [128, 128], BF16) for i in range(2)]
        g_banks = [psum_bank(pp, f"fg{i}", [128, 512], F32) for i in range(4)]

        ln = LN(pp, "f", consts["eps"])
        s_tr = "ftr"

        @pp.block.gpsimd
        def _(g):
            for w_sb, w_d in ((wr, wr_d), (wk, wk_d), (wv, wv_d)):
                o3 = w_sb[:].rearrange("p (c d) -> p c d", c=CD)
                i3 = w_d.rearrange("(c p) d -> p c d", p=128)
                pp.inc(g.dma_start(out=o3, in_=i3), "dpool", 16)
            pp.inc(g.dma_start(out=mur[:], in_=mur_d.rearrange("(c p) -> p c", p=128),
                               allow_slow_non_contiguous=True), "dpool", 16)
            pp.inc(g.dma_start(out=muk[:], in_=muk_d.rearrange("(c p) -> p c", p=128),
                               allow_slow_non_contiguous=True), "dpool", 16)
        w_ready = pp.val("dpool")

        for tile in range(NTILE):
            for j in range(4):
                ln.stats_emit(x_sb[:, (tile * 4 + j) * 1024:(tile * 4 + j + 1) * 1024], j)
            ln.rstds_emit()
            for j in range(4):
                tc = tile * 4 + j
                xht = xhat[j % 2]
                if j >= 2:

                    @pp.block.vector
                    def _(ve):
                        pp.wait(ve, s_tr, pp.cnt.get(s_tr, 0) - CD)
                ln.apply_emit(x_sb[:, tc * 1024:(tc + 1) * 1024], xht[:], j)
                ln_n = pp.val(ln.rdy)
                for c in range(CD):
                    bank = tr_banks[c % 2]

                    @pp.block.tensor
                    def _(te):
                        pp.wait(te, ln.rdy, ln_n)
                        bank.wait_free(te)
                        pp.inc(te.transpose(bank.t[:, :], xht[:, c * 128:(c + 1) * 128],
                                            ident[:]), s_tr, 1)

                    @pp.block.vector
                    def _(ve):
                        pp.wait(ve, s_tr)
                        col = j * 128
                        ve.tensor_copy(xh[:, c * 512 + col: c * 512 + col + 128], bank.t[:, :])
                        ve.drain()
                        ncols = 127 if j == 3 else 128
                        ev = ve.tensor_copy(xs[:, c * 512 + col + 1: c * 512 + col + 1 + ncols],
                                            bank.t[:, 0:ncols])
                        bank.release(ev)

            @pp.block.vector
            def _(ve):
                if tile == 0:
                    ve.memset(carry[:], 0.0)
                    ve.drain()
                for c in range(CD):
                    ve.tensor_copy(xs[:, c * 512: c * 512 + 1], carry[:, c:c + 1])
                ve.drain()
                for c in range(CD):
                    ve.tensor_copy(carry[:, c:c + 1], xh[:, c * 512 + 511: c * 512 + 512])
            pp.fence()

            @pp.block.vector
            def _(ve):
                if tile == 0:
                    pp.wait(ve, "dpool", w_ready)
                ve.tensor_sub(diff[:], xh[:], xs[:])
                ve.drain()
                for c in range(CD):
                    ve.scalar_tensor_tensor(xr[:, c * 512:(c + 1) * 512],
                                            diff[:, c * 512:(c + 1) * 512],
                                            mur[:, c:c + 1],
                                            xs[:, c * 512:(c + 1) * 512],
                                            ALU.mult, ALU.add)
                    ve.scalar_tensor_tensor(xk[:, c * 512:(c + 1) * 512],
                                            diff[:, c * 512:(c + 1) * 512],
                                            muk[:, c:c + 1],
                                            xs[:, c * 512:(c + 1) * 512],
                                            ALU.mult, ALU.add)
            pp.fence()

            for j in range(4):
                for dh in range(2):
                    bank = g_banks[(j * 2 + dh) % 4]

                    @pp.block.tensor
                    def _(te):
                        if tile == 0 and j == 0 and dh == 0:
                            pp.wait(te, "dpool", w_ready)
                        bank.wait_free(te)
                        for c in range(CD):
                            mm = te.matmul(bank.t[:, :],
                                           xr[:, c * 512 + j * 128: c * 512 + j * 128 + 128],
                                           wr[:, c * 1024 + dh * 512: c * 1024 + dh * 512 + 512],
                                           start=(c == 0), stop=(c == CD - 1))
                        pp.inc(mm, "frmm", 1)

                    @pp.block.scalar
                    def _(se):
                        pp.wait(se, "frmm")
                        se.activation(r_sb[:, j * 1024 + dh * 512: j * 1024 + dh * 512 + 512],
                                      bank.t[:, :], AF.Sigmoid)
                        bank.release(se.drain())

            for dk in range(CD):
                bank = g_banks[dk % 4]
                kr = krelu[dk % 2]

                @pp.block.tensor
                def _(te):
                    bank.wait_free(te)
                    for c in range(CD):
                        mm = te.matmul(bank.t[:, :],
                                       wk[:, c * 1024 + dk * 128: c * 1024 + dk * 128 + 128],
                                       xk[:, c * 512:(c + 1) * 512],
                                       start=(c == 0), stop=(c == CD - 1))
                    pp.inc(mm, "fkmm", 1)

                @pp.block.vector
                def _(ve):
                    pp.wait(ve, "fkmm")
                    if pp.val("fkev") >= 2:
                        pp.wait(ve, "fkev", pp.cnt["fkev"] - 1)
                    mx = ve.tensor_scalar(kr[:, :], bank.t[:, :], 0.0, None, ALU.max)
                    bank.release(mx)

                @pp.block.scalar
                def _(se):
                    bank.wait_released(se)
                    se.activation(kact[:, dk * 512:(dk + 1) * 512], kr[:, :], AF.Square)
                    pp.inc(se.drain(), "fkev", 1)

            pp.fence()

            for j in range(4):
                for dh in range(2):
                    bank = g_banks[(j * 2 + dh) % 4]

                    @pp.block.tensor
                    def _(te):
                        bank.wait_free(te)
                        for dk in range(CD):
                            mm = te.matmul(bank.t[:, :],
                                           kact[:, dk * 512 + j * 128: dk * 512 + j * 128 + 128],
                                           wv[:, dk * 1024 + dh * 512: dk * 1024 + dh * 512 + 512],
                                           start=(dk == 0), stop=(dk == CD - 1))
                        pp.inc(mm, "fomm", 1)

                    @pp.block.vector
                    def _(ve):
                        pp.wait(ve, "fomm")
                        tc = tile * 4 + j
                        xv = x_sb[:, tc * 1024 + dh * 512: tc * 1024 + dh * 512 + 512]
                        stt = ve.scalar_tensor_tensor(t1[:, :], bank.t[:, :], 0.5,
                                                      r_sb[:, j * 1024 + dh * 512: j * 1024 + dh * 512 + 512],
                                                      ALU.mult, ALU.mult)
                        bank.release(stt)
                        ve.drain()
                        ve.tensor_add(xv, xv, t1[:, :])
                        ve.drain()

            pp.fence()

        p.cnt.update(pp.cnt)






def emit_xhat_fm(pp, ln, x_sb, tile, xhat_fm, xhat_bufs, tr_banks, ident, s_tr):
    for j in range(4):
        tc = tile * 4 + j
        ln.stats_emit(x_sb[:, tc * 1024:(tc + 1) * 1024], j)
    ln.rstds_emit()
    for j in range(4):
        tc = tile * 4 + j
        xht = xhat_bufs[j % 2]
        if j >= 2:
            @pp.block.vector
            def _(ve):
                pp.wait(ve, s_tr, pp.val(s_tr) - CD)
        ln.apply_emit(x_sb[:, tc * 1024:(tc + 1) * 1024], xht[:], j)
        ln_n = pp.val(ln.rdy)
        for c in range(CD):
            bank = tr_banks[c % len(tr_banks)]

            @pp.block.tensor
            def _(te):
                pp.wait(te, ln.rdy, ln_n)
                bank.wait_free(te)
                pp.inc(te.transpose(bank.t[:, :], xht[:, c * 128:(c + 1) * 128],
                                    ident[:]), s_tr, 1)

            @pp.block.vector
            def _(ve):
                pp.wait(ve, s_tr)
                ev = ve.tensor_copy(xhat_fm[:, c * 512 + j * 128: c * 512 + j * 128 + 128],
                                    bank.t[:, :])
                bank.release(ev)

    @pp.block.vector
    def _(ve):
        pp.inc(ve.drain(), "xfdone", 1)
    return pp.val("xfdone")



def emit_moe(p, x_sb, wg_d, w1_d, b1r_d, w2_d, b2_d, consts, n_tok, tag,
             cast_w=True):
    NTILE = n_tok // 512
    ident = consts["ident"]

    with contextlib.ExitStack() as st:
        pp = p.fresh(st)

        wg = sbuf(pp, "mwg", [128, CD * 8], F32)
        b1r = sbuf(pp, "mb1", [128, 256], F32)
        b2 = sbuf(pp, "mb2", [8, 1024], F32)
        xhat_fm = sbuf(pp, "mxf", [128, CD * 512], BF16)
        xhat = [sbuf(pp, f"mxt{i}", [128, 1024], BF16) for i in range(2)]
        xhf32 = sbuf(pp, "mxg", [128, 1024], F32)
        xf32c = sbuf(pp, "mxgc", [128, CD * 128], F32)
        w1 = sbuf(pp, "mw1", [128, CD * 4096], BF16)
        w2 = [sbuf(pp, f"mw2{i}", [128, 4 * 512], BF16) for i in range(2)]
        h_sb = sbuf(pp, "mh", [128, 32 * 512], BF16)
        e8 = sbuf(pp, "me8", [128, 8], F32)
        ge8 = sbuf(pp, "mge", [128, 8], F32)
        tmp8 = sbuf(pp, "mt8", [128, 8], F32)
        msk2 = sbuf(pp, "mm2", [128, 8], F32)
        m1n = sbuf(pp, "mm1n", [128, 1], F32)
        m2n = sbuf(pp, "mm2n", [128, 1], F32)
        lsum = sbuf(pp, "mls", [128, 1], F32)
        lrc = sbuf(pp, "mlr", [128, 1], F32)
        maskp = sbuf(pp, "mmp", [128, 4 * 8], F32)
        mtT = sbuf(pp, "mmt", [8, 4 * 128], F32)

        tr_banks = [psum_bank(pp, f"mtr{i}", [128, 128], BF16) for i in range(1)]
        h_banks = [psum_bank(pp, f"mhb{i}", [128, 512], F32) for i in range(2)]
        y_banks = [psum_bank(pp, f"myb{i}", [128, 512], F32) for i in range(4)]
        sm_bank = psum_bank(pp, "msm", [128, 512], F32)

        ln = LN(pp, "m", consts["eps"])
        s_tr = "mtr"
        dqeng = "gpsimd" if cast_w else "sync"

        @pp.block.gpsimd
        def _(g):
            pp.inc(g.dma_start(out=wg[:].rearrange("p (c e) -> p c e", c=CD),
                               in_=wg_d.rearrange("(c p) e -> p c e", p=128)), "dwg", 16)
        @pp.block.sync
        def _(sy):
            pp.inc(sy.dma_start(out=b1r[:], in_=b1r_d), "dsync2", 16)
            pp.inc(sy.dma_start(out=b2[:], in_=b2_d), "dsync2", 16)
        smalls_g = pp.val("dwg")
        smalls_s = pp.val("dsync2")

        def load_w1(e):
            eng = getattr(pp.block, dqeng)

            @eng
            def _(g):
                if pp.val("mw1f") > 0:
                    pp.wait(g, "mw1f")
                o3 = w1[:].rearrange("p (c h) -> p c h", c=CD)
                i3 = w1_d[e].rearrange("(c p) h -> p c h", p=128)
                pp.inc(g.dma_start(out=o3, in_=i3), "dw1", 16)
            return pp.val("dw1")

        def load_w2(e, dh, hq, buf):
            eng = getattr(pp.block, dqeng)

            @eng
            def _(g):
                if pp.val(f"mw2f{buf}") > 0:
                    pp.wait(g, f"mw2f{buf}")
                o3 = w2[buf][:].rearrange("p (q d) -> p q d", q=4)
                i3 = w2_d[e][hq * 512:(hq + 1) * 512,
                             dh * 512:(dh + 1) * 512].rearrange("(q p) d -> p q d", p=128)
                pp.inc(g.dma_start(out=o3, in_=i3), f"dw2{buf}", 16)
            return pp.val(f"dw2{buf}")

        w1_rdy = load_w1(0)

        for tile in range(NTILE):
            for j in range(4):
                ln.stats_emit(x_sb[:, (tile * 4 + j) * 1024:(tile * 4 + j + 1) * 1024], j)
            ln.rstds_emit()
            for j in range(4):
                tc = tile * 4 + j
                xht = xhat[j % 2]
                if j >= 2:
                    @pp.block.vector
                    def _(ve):
                        pp.wait(ve, s_tr, pp.val(s_tr) - 2 * CD)
                ln.apply_emit(x_sb[:, tc * 1024:(tc + 1) * 1024], xht[:], j)
                ln_n = pp.val(ln.rdy)
                @pp.block.vector
                def _(ve):
                    if pp.val(s_tr) > 0:
                        pp.wait(ve, s_tr)
                    pp.inc(ve.tensor_scalar(xhf32[:], x_sb[:, tc * 1024:(tc + 1) * 1024],
                                            ln.mv4[:, 2 * j:2 * j + 1],
                                            ln.rstd4[:, j:j + 1],
                                            ALU.subtract, ALU.mult), "mf32", 1)
                f32_n = pp.val("mf32")
                for c in range(CD):
                    bank = tr_banks[0]

                    @pp.block.tensor
                    def _(te):
                        pp.wait(te, ln.rdy, ln_n)
                        bank.wait_free(te)
                        pp.inc(te.transpose(bank.t[:, :], xht[:, c * 128:(c + 1) * 128],
                                            ident[:]), s_tr, 1)

                    @pp.block.vector
                    def _(ve):
                        pp.wait(ve, s_tr)
                        ev = ve.tensor_copy(xhat_fm[:, c * 512 + j * 128: c * 512 + j * 128 + 128],
                                            bank.t[:, :])
                        bank.release(ev)
                for c in range(CD):
                    @pp.block.tensor
                    def _(te):
                        pp.wait(te, "mf32", f32_n)
                        sm_bank.wait_free(te)
                        pp.inc(te.transpose(sm_bank.t[0:128, 0:128],
                                            xhf32[:, c * 128:(c + 1) * 128],
                                            consts["identf"][:]), s_tr, 1)

                    @pp.block.vector
                    def _(ve):
                        pp.wait(ve, s_tr)
                        ev = ve.tensor_copy(xf32c[:, c * 128:(c + 1) * 128], sm_bank.t[0:128, 0:128])
                        sm_bank.release(ev)

                @pp.block.tensor
                def _(te):
                    if tile == 0 and j == 0:
                        pp.wait(te, "dwg", smalls_g)
                        pp.wait(te, "dsync2", smalls_s)
                    sm_bank.wait_free(te)
                    for c in range(CD):
                        mm = te.matmul(sm_bank.t[:, 0:8],
                                       xf32c[:, c * 128:(c + 1) * 128],
                                       wg[:, c * 8:(c + 1) * 8],
                                       start=(c == 0), stop=(c == CD - 1))
                    pp.inc(mm, "mgmm", 1)

                @pp.block.vector
                def _(ve):
                    pp.wait(ve, "mgmm")
                    gps = sm_bank.t[:, 0:8]
                    ve.reduce_max(m1n[:], gps, axis=AX.X, negate=True)
                    ve.drain()
                    ve.tensor_scalar(ge8[:], gps, m1n[:, 0:1], 0.0, ALU.add, ALU.is_ge)
                    ve.drain()
                    ve.tensor_scalar(tmp8[:], ge8[:], -1e30, None, ALU.mult)
                    ve.drain()
                    ve.tensor_add(msk2[:], gps, tmp8[:])
                    ve.drain()
                    ve.reduce_max(m2n[:], msk2[:], axis=AX.X, negate=True)
                    ve.drain()
                    pp.inc(ve.tensor_scalar(ge8[:], gps, m2n[:, 0:1], 0.0, ALU.add, ALU.is_ge),
                           "mdv", 1)

                @pp.block.scalar
                def _(se):
                    pp.wait(se, "mdv")
                    se.activation(e8[:], sm_bank.t[:, 0:8], AF.Exp, bias=m1n[:, 0:1])
                    pp.inc(se.drain(), "mex", 1)

                @pp.block.vector
                def _(ve):
                    pp.wait(ve, "mex")
                    em = ve.tensor_mul(e8[:], e8[:], ge8[:])
                    sm_bank.release(em)
                    ve.drain()
                    ve.reduce_sum(lsum[:], e8[:], axis=AX.X)
                    ve.drain()
                    ve.reciprocal(lrc[:], lsum[:])
                    ve.drain()
                    pp.inc(ve.tensor_scalar(maskp[:, j * 8:(j + 1) * 8], e8[:],
                                            lrc[:, 0:1], 0.0625, ALU.mult, ALU.mult),
                           "mmp", 1)

            @pp.block.scalar
            def _(se):
                pp.inc(se.drain(), "xfdone", 1)
            xf_n = pp.val("xfdone")

            pp.fence()
            mp_n = pp.val("mmp")
            for j in range(4):
                @pp.block.tensor
                def _(te):
                    pp.wait(te, "mmp", mp_n)
                    sm_bank.wait_free(te)
                    pp.inc(te.transpose(sm_bank.t[0:8, 0:128],
                                        maskp[:, j * 8:(j + 1) * 8], consts["identf"][:]),
                           "mtp", 1)

                @pp.block.vector
                def _(ve):
                    pp.wait(ve, "mtp")
                    ev = ve.tensor_copy(mtT[0:8, j * 128:(j + 1) * 128], sm_bank.t[0:8, 0:128])
                    sm_bank.release(ev)
            for j in range(4):
                for dh in range(2):
                    @pp.block.tensor
                    def _(te):
                        sm_bank.wait_free(te)
                        pp.inc(te.matmul(sm_bank.t[:, 0:512],
                                         mtT[0:8, j * 128:(j + 1) * 128],
                                         b2[0:8, dh * 512:(dh + 1) * 512],
                                         start=True, stop=True), "mb2m", 1)

                    @pp.block.vector
                    def _(ve):
                        pp.wait(ve, "mb2m")
                        tc = tile * 4 + j
                        xv = x_sb[:, tc * 1024 + dh * 512: tc * 1024 + dh * 512 + 512]
                        ad = ve.tensor_add(xv, xv, sm_bank.t[:, 0:512])
                        sm_bank.release(ad)
                        ve.drain()

            for e in range(8):
                for hc in range(32):
                    bank = h_banks[hc % 2]

                    @pp.block.tensor
                    def _(te):
                        if hc == 0:
                            pp.wait(te, "dw1", w1_rdy)
                            pp.wait(te, "xfdone", xf_n)
                        bank.wait_free(te)
                        for c in range(CD):
                            mm = te.matmul(bank.t[:, :],
                                           w1[:, c * 4096 + hc * 128: c * 4096 + hc * 128 + 128],
                                           xhat_fm[:, c * 512:(c + 1) * 512],
                                           start=(c == 0), stop=(c == CD - 1))
                        pp.inc(mm, "mh1", 1)

                    @pp.block.vector
                    def _(ve):
                        pp.wait(ve, "mh1")
                        ev = ve.tensor_scalar(h_sb[:, hc * 512:(hc + 1) * 512], bank.t[:, :],
                                              b1r[:, e * 32 + hc: e * 32 + hc + 1], 0.0,
                                              ALU.add, ALU.max)
                        bank.release(ev)

                h_rel = [(b, b.uses) for b in h_banks]

                @pp.block.tensor
                def _(te):
                    pp.inc(te.drain(), "mw1f", 1)
                if not (e == 7 and tile == NTILE - 1):
                    w1_rdy = load_w1((e + 1) % 8)

                seq = [(dh, hq) for dh in (0, 1) for hq in range(8)]
                rdy = {}
                rdy[(0, 0)] = load_w2(e, 0, 0, 0)
                rdy[(0, 1)] = load_w2(e, 0, 1, 1)
                for idx, (dh, hq) in enumerate(seq):
                    buf = idx % 2

                    @pp.block.tensor
                    def _(te):
                        pp.wait(te, f"dw2{idx % 2}", rdy[(dh, hq)])
                        if hq == 0:
                            for j in range(4):
                                y_banks[j].wait_free(te)
                            if idx == 0:
                                for b, u in h_rel:
                                    te.wait_ge(pp.sem(b.name), u)
                        for hc4 in range(4):
                            hc = hq * 4 + hc4
                            for j in range(4):
                                mm = te.matmul(y_banks[j].t[:, :],
                                               h_sb[:, hc * 512 + j * 128: hc * 512 + j * 128 + 128],
                                               w2[buf][:, hc4 * 512:(hc4 + 1) * 512],
                                               start=(hq == 0 and hc4 == 0),
                                               stop=(hq == 7 and hc4 == 3))
                        pp.inc(mm, f"mw2f{buf}", 1)
                        if hq == 7:
                            pp.inc(te.drain(), "my2", 1)

                    if idx + 2 < 16:
                        nd, nh = seq[idx + 2]
                        rdy[(nd, nh)] = load_w2(e, nd, nh, buf)

                    if hq == 7:
                        y_done = pp.val("my2")

                        @pp.block.vector
                        def _(ve):
                            pp.wait(ve, "my2", y_done)
                            for j in range(4):
                                tc = tile * 4 + j
                                xv = x_sb[:, tc * 1024 + dh * 512: tc * 1024 + dh * 512 + 512]
                                stt = ve.scalar_tensor_tensor(xv, y_banks[j].t[:, :],
                                                              maskp[:, j * 8 + e: j * 8 + e + 1],
                                                              xv, ALU.mult, ALU.add)
                                y_banks[j].release(stt)
                                ve.drain()

            pp.fence()

        p.cnt.update(pp.cnt)





def emit_attn0(p, x_sb, wattn_d, wproj_d, consts, n_tok, tag):
    NTILE = n_tok // 512
    NT = n_tok // 128
    ident = consts["ident"]
    identf = consts["identf"]

    with contextlib.ExitStack() as st:
        pp = p.fresh(st)

        K_sb = sbuf(pp, "aK", [128, CD * n_tok], BF16)
        V_sb = sbuf(pp, "aV", [128, NT * 1040], BF16)
        knm = sbuf(pp, "akm", [1, 16], F32)
        knt = sbuf(pp, "akt", [1, 1], F32)
        qn2m = sbuf(pp, "aqm", [1, 16], F32)
        mhn = sbuf(pp, "amh", [1, 16], F32)
        prod = sbuf(pp, "apr", [1, 16], F32)
        xhat_fm = sbuf(pp, "axf", [128, CD * 512], BF16)
        xhat = [sbuf(pp, f"axt{i}", [128, 1024], BF16) for i in range(2)]
        sq = sbuf(pp, "asq", [128, 512], BF16)
        mhb = sbuf(pp, "amb", [128, 1], F32)
        onesrow = consts["onesrow"]
        onespair = consts["onespair"]
        masksT = consts["masksT"]

        tr_banks = [psum_bank(pp, "atr", [128, 128], BF16)]
        s_banks = [psum_bank(pp, f"as{i}", [128, 512], F32) for i in range(2)]
        av_bank = psum_bank(pp, "aav", [128, 512], F32)
        sm_bank = psum_bank(pp, "asm", [128, 512], F32)

        ln = LN(pp, "a", consts["eps"])
        s_tr = "atr"

        with contextlib.ExitStack() as st1:
            p1 = pp.fresh(st1)
            wak = sbuf(p1, "awk", [128, CD * 1024], BF16)
            wav = sbuf(p1, "awv", [128, CD * 1024], BF16)

            @p1.block.gpsimd
            def _(g):
                for w_sb, lo in ((wak, 0), (wav, 2048)):
                    o3 = w_sb[:].rearrange("p (c d) -> p c d", c=CD)
                    i3 = wattn_d[:, lo:lo + 1024].rearrange("(c p) d -> p c d", p=128)
                    p1.inc(g.dma_start(out=o3, in_=i3), "adw1", 16)
            w1_rdy = p1.val("adw1")

            @p1.block.vector
            def _(ve):
                ve.memset(V_sb[:], 1.0)
                ve.drain()
                ve.memset(knm[:], 0.0)
                p1.inc(ve.drain(), "avini", 1)
            vini = p1.val("avini")

            for tile in range(NTILE):
                xf_n = emit_xhat_fm(p1, ln, x_sb, tile, xhat_fm, xhat, tr_banks, ident, s_tr)

                for ch in range(CD):
                    bank = s_banks[ch % 2]

                    @p1.block.tensor
                    def _(te):
                        if tile == 0 and ch == 0:
                            p1.wait(te, "adw1", w1_rdy)
                        p1.wait(te, "xfdone", xf_n)
                        bank.wait_free(te)
                        for c in range(CD):
                            mm = te.matmul(bank.t[:, :],
                                           wak[:, c * 1024 + ch * 128: c * 1024 + ch * 128 + 128],
                                           xhat_fm[:, c * 512:(c + 1) * 512],
                                           start=(c == 0), stop=(c == CD - 1))
                        p1.inc(mm, "akmm", 1)

                    @p1.block.vector
                    def _(ve):
                        p1.wait(ve, "akmm")
                        kv = K_sb[:, ch * n_tok + tile * 512: ch * n_tok + tile * 512 + 512]
                        ev = ve.tensor_copy(kv, bank.t[:, :])
                        bank.release(ev)
                        ve.drain()
                        p1.inc(ve.tensor_mul(sq[:, :], kv, kv), "aksq", 1)

                    for par in range(2):
                        @p1.block.tensor
                        def _(te):
                            p1.wait(te, "aksq")
                            sm_bank.wait_free(te)
                            p1.inc(te.matmul(sm_bank.t[0:1, 0:512],
                                             onespair[:, par:par + 1], sq[:, :],
                                             start=True, stop=True), "aknm", 1)

                        @p1.block.vector
                        def _(ve):
                            p1.wait(ve, "aknm")
                            h = 2 * ch + par
                            mx = ve.reduce_max(knt[0:1, 0:1], sm_bank.t[0:1, 0:512], axis=AX.X)
                            sm_bank.release(mx)
                            ve.drain()
                            ve.tensor_max(knm[0:1, h:h + 1], knm[0:1, h:h + 1], knt[0:1, 0:1])
                            ve.drain()

                for j in range(4):
                    tc = tile * 4 + j
                    for vh in range(2):
                        bank = s_banks[vh % 2]

                        @p1.block.tensor
                        def _(te):
                            bank.wait_free(te)
                            for c in range(CD):
                                mm = te.matmul(bank.t[:, :],
                                               xhat_fm[:, c * 512 + j * 128: c * 512 + j * 128 + 128],
                                               wav[:, c * 1024 + vh * 512: c * 1024 + vh * 512 + 512],
                                               start=(c == 0), stop=(c == CD - 1))
                            p1.inc(mm, "avmm", 1)

                        @p1.block.vector
                        def _(ve):
                            p1.wait(ve, "avmm")
                            if tile == 0 and j == 0 and vh == 0:
                                p1.wait(ve, "avini", vini)
                            ov = V_sb[:, tc * 1040 + vh * 520: tc * 1040 + vh * 520 + 520]
                            ov = ov.rearrange("p (h i) -> p h i", h=8)[:, :, 0:64]
                            iv = bank.t[:, :].rearrange("p (h i) -> p h i", h=8)
                            ev = ve.tensor_copy(ov, iv)
                            bank.release(ev)
                pp.fence()

            pp.fence()
            pp.cnt.update(p1.cnt)

        with contextlib.ExitStack() as st2:
            p2 = pp.fresh(st2)
            waq = sbuf(p2, "awq", [128, CD * 1024], BF16)
            wap = sbuf(p2, "awp", [128, CD * 512], BF16)
            Qh = sbuf(p2, "aQ", [128, 16 * 512], BF16)
            STx = [sbuf(p2, f"aST{i}", [128, 512], BF16) for i in range(2)]
            y_sb = sbuf(p2, "ay", [128, CD * 512], BF16)
            rl = sbuf(p2, "arl", [1, 512], F32)
            lbc = sbuf(p2, "albc", [64, 512], F32)

            @p2.block.gpsimd
            def _(g):
                p2.wait(g, "adw1")
                p2.wait(g, "fence")
                o3 = waq[:].rearrange("p (c d) -> p c d", c=CD)
                i3 = wattn_d[:, 1024:2048].rearrange("(c p) d -> p c d", p=128)
                p2.inc(g.dma_start(out=o3, in_=i3), "adw2", 16)
            w2_rdy = p2.val("adw2")

            def load_wap(dh):
                @p2.block.gpsimd
                def _(g):
                    if p2.val("awapf") > 0:
                        p2.wait(g, "awapf")
                    o3 = wap[:].rearrange("p (c d) -> p c d", c=CD)
                    i3 = wproj_d[:, dh * 512:(dh + 1) * 512].rearrange("(c p) d -> p c d", p=128)
                    p2.inc(g.dma_start(out=o3, in_=i3), "adwp", 16)
                return p2.val("adwp")

            for qt in range(NTILE):
                xf_n = emit_xhat_fm(p2, ln, x_sb, qt, xhat_fm, xhat, tr_banks, ident, s_tr)

                @p2.block.vector
                def _(ve):
                    ve.memset(Qh[:], 0.0)
                    p2.inc(ve.drain(), "aqz", 1)
                qz = p2.val("aqz")

                for qh in range(CD):
                    bank = s_banks[qh % 2]

                    @p2.block.tensor
                    def _(te):
                        if qt == 0 and qh == 0:
                            p2.wait(te, "adw2", w2_rdy)
                        p2.wait(te, "xfdone", xf_n)
                        bank.wait_free(te)
                        for c in range(CD):
                            mm = te.matmul(bank.t[:, :],
                                           waq[:, c * 1024 + qh * 128: c * 1024 + qh * 128 + 128],
                                           xhat_fm[:, c * 512:(c + 1) * 512],
                                           start=(c == 0), stop=(c == CD - 1))
                        p2.inc(mm, "aqmm", 1)

                    @p2.block.vector
                    def _(ve):
                        p2.wait(ve, "aqmm")
                        if qh == 0:
                            p2.wait(ve, "aqz", qz)
                        ve.tensor_scalar(Qh[0:64, (2 * qh) * 512:(2 * qh) * 512 + 512],
                                         bank.t[0:64, :], 0.125, None, ALU.mult)
                        ve.drain()
                        ev = ve.tensor_scalar(Qh[64:128, (2 * qh + 1) * 512:(2 * qh + 1) * 512 + 512],
                                              bank.t[64:128, :], 0.125, None, ALU.mult)
                        bank.release(ev)

                    @p2.block.vector
                    def _(ve):
                        qe = Qh[0:64, (2 * qh) * 512:(2 * qh) * 512 + 512]
                        qo = Qh[64:128, (2 * qh + 1) * 512:(2 * qh + 1) * 512 + 512]
                        ve.tensor_mul(sq[0:64, :], qe, qe)
                        ve.drain()
                        p2.inc(ve.tensor_mul(sq[64:128, :], qo, qo), "aqsq", 1)

                    for par in range(2):
                        @p2.block.tensor
                        def _(te):
                            p2.wait(te, "aqsq")
                            sm_bank.wait_free(te)
                            p2.inc(te.matmul(sm_bank.t[0:1, 0:512],
                                             onespair[:, par:par + 1], sq[:, :],
                                             start=True, stop=True), "aqnm", 1)

                        @p2.block.vector
                        def _(ve):
                            p2.wait(ve, "aqnm")
                            mx = ve.reduce_max(qn2m[0:1, 2 * qh + par:2 * qh + par + 1],
                                               sm_bank.t[0:1, 0:512], axis=AX.X)
                            sm_bank.release(mx)

                @p2.block.vector
                def _(ve):
                    ve.drain()
                    p2.inc(ve.tensor_mul(prod[0:1, :], qn2m[0:1, :], knm[0:1, :]),
                           "apro", 1)

                @p2.block.scalar
                def _(se):
                    p2.wait(se, "apro")
                    se.activation(prod[0:1, :], prod[0:1, :], AF.Ln)
                    se.drain()
                    se.activation(mhn[0:1, :], prod[0:1, :], AF.Exp, scale=0.5)
                    p2.inc(se.drain(), "amh1", 1)

                @p2.block.vector
                def _(ve):
                    p2.wait(ve, "amh1")
                    p2.inc(ve.tensor_scalar(mhn[0:1, :], mhn[0:1, :], -1.0, None, ALU.mult),
                           "amh2", 1)
                mh_n = p2.val("amh2")

                n_kc = 4 * (qt + 1)
                for h in range(16):
                    ch = h // 2
                    @p2.block.tensor
                    def _(te):
                        p2.wait(te, "amh2", mh_n)
                        sm_bank.wait_free(te)
                        p2.inc(te.matmul(sm_bank.t[0:128, 0:1], onesrow, mhn[0:1, h:h + 1],
                                         start=True, stop=True), "ambc", 1)

                    @p2.block.vector
                    def _(ve):
                        p2.wait(ve, "ambc")
                        ev = ve.tensor_copy(mhb[:, 0:1], sm_bank.t[0:128, 0:1])
                        sm_bank.release(ev)
                    mb_u = sm_bank.uses

                    for kc in range(n_kc):
                        bank = s_banks[kc % 2]
                        stx = STx[kc % 2]

                        @p2.block.tensor
                        def _(te):
                            bank.wait_free(te)
                            p2.inc(te.matmul(bank.t[:, :],
                                             K_sb[:, ch * n_tok + kc * 128: ch * n_tok + kc * 128 + 128],
                                             Qh[:, h * 512:(h + 1) * 512],
                                             start=True, stop=True), "asmm", 1)

                        @p2.block.scalar
                        def _(se):
                            p2.wait(se, "asmm")
                            if kc == 0:
                                se.wait_ge(p2.sem(sm_bank.name), mb_u)
                            if p2.val("astx") >= 2:
                                p2.wait(se, "astx", p2.cnt["astx"] - 1)
                            se.activation(stx[:, :], bank.t[:, :], AF.Exp, bias=mhb[:, 0:1])
                            bank.release(se.drain())
                        s_u = (bank, bank.uses)

                        diag = kc - 4 * qt
                        if diag >= 0:
                            @p2.block.vector
                            def _(ve):
                                ve.wait_ge(p2.sem(s_u[0].name), s_u[1])
                                p2.inc(ve.tensor_mul(stx[:, :], stx[:, :], masksT[diag][:]),
                                       "asev", 1)

                        @p2.block.tensor
                        def _(te):
                            if diag >= 0:
                                p2.wait(te, "asev")
                            else:
                                te.wait_ge(p2.sem(s_u[0].name), s_u[1])
                            if kc == 0:
                                av_bank.wait_free(te)
                            mm = te.matmul(av_bank.t[0:65, :],
                                           V_sb[:, kc * 1040 + h * 65: kc * 1040 + h * 65 + 65],
                                           stx[:, :],
                                           start=(kc == 0), stop=(kc == n_kc - 1))
                            p2.inc(mm, "astx", 1)
                            if kc == n_kc - 1:
                                p2.inc(te.drain(), "aavd", 1)
                    av_n = p2.val("aavd")

                    @p2.block.vector
                    def _(ve):
                        p2.wait(ve, "aavd", av_n)
                        p2.inc(ve.reciprocal(rl[0:1, :], av_bank.t[64:65, 0:512]),
                               "arl", 1)

                    @p2.block.tensor
                    def _(te):
                        p2.wait(te, "arl")
                        sm_bank.wait_free(te)
                        p2.inc(te.matmul(sm_bank.t[0:64, 0:512], onesrow[0:1, 0:64], rl[0:1, :],
                                         start=True, stop=True), "albm", 1)

                    @p2.block.vector
                    def _(ve):
                        p2.wait(ve, "albm")
                        ev = ve.tensor_copy(lbc[:, :], sm_bank.t[0:64, 0:512])
                        sm_bank.release(ev)
                    lb_u = sm_bank.uses

                    @p2.block.vector
                    def _(ve):
                        ve.wait_ge(p2.sem(sm_bank.name), lb_u)
                        ev = ve.tensor_mul(y_sb[(h % 2) * 64:(h % 2) * 64 + 64, ch * 512:(ch + 1) * 512],
                                           av_bank.t[0:64, :], lbc[:, :])
                        av_bank.release(ev)

                p2.fence()

                for dh in range(2):
                    wap_rdy = load_wap(dh)
                    for j in range(4):
                        tc = qt * 4 + j
                        bank = s_banks[j % 2]

                        @p2.block.tensor
                        def _(te):
                            if j == 0:
                                p2.wait(te, "adwp", wap_rdy)
                            bank.wait_free(te)
                            for c in range(CD):
                                mm = te.matmul(bank.t[:, :],
                                               y_sb[:, c * 512 + j * 128: c * 512 + j * 128 + 128],
                                               wap[:, c * 512:(c + 1) * 512],
                                               start=(c == 0), stop=(c == CD - 1))
                            p2.inc(mm, "apmm", 1)
                            if j == 3:
                                p2.inc(te.drain(), "awapf", 1)

                        @p2.block.vector
                        def _(ve):
                            p2.wait(ve, "apmm")
                            xv = x_sb[:, tc * 1024 + dh * 512: tc * 1024 + dh * 512 + 512]
                            ad = ve.tensor_add(xv, xv, bank.t[:, :])
                            bank.release(ad)
                            ve.drain()

                p2.fence()

            pp.cnt.update(p2.cnt)

        p.cnt.update(pp.cnt)





def emit_attn1(p, x_sb, wqtp_d, wktp_d, wvt_d, cos_d, sin_d, consts, n_tok, tag):
    NTILE = n_tok // 512
    NT = n_tok // 128
    ident = consts["ident"]

    with contextlib.ExitStack() as st:
        pp = p.fresh(st)

        K_sb = sbuf(pp, "bK", [128, CD * n_tok], BF16)
        V_sb = sbuf(pp, "bV", [128, NT * 1024], BF16)
        xhat_fm = sbuf(pp, "bxf", [128, CD * 512], BF16)
        xhat = [sbuf(pp, f"bxt{i}", [128, 1024], BF16) for i in range(2)]
        raw = sbuf(pp, "braw", [128, 2 * 512], F32)
        tcos = sbuf(pp, "bco", [128, 512], F32)
        tsin = sbuf(pp, "bsi", [128, 512], F32)
        t1 = sbuf(pp, "bt1", [128, 512], F32)
        t2 = sbuf(pp, "bt2", [128, 512], F32)
        negm = sbuf(pp, "bnm", [128, 1], F32)
        lsum = sbuf(pp, "bl", [128, 1], F32)
        rl1 = sbuf(pp, "brl", [128, 1], F32)
        masksA = consts["masksA"]

        tr_banks = [psum_bank(pp, "btr", [128, 128], BF16)]
        s_banks = [psum_bank(pp, f"bs{i}", [128, 512], F32) for i in range(2)]
        av_bank = psum_bank(pp, "bav", [128, 512], F32)

        ln = LN(pp, "b", consts["eps"])
        s_tr = "btr"

        def load_tables(i, tile):
            @pp.block.sync
            def _(sy):
                if pp.val("btbf") > 0:
                    pp.wait(sy, "btbf")
                for dst, srcd in ((tcos, cos_d), (tsin, sin_d)):
                    i3 = srcd[i * 128:(i + 1) * 128, tile * 512:(tile + 1) * 512]
                    pp.inc(sy.dma_start(out=dst[:], in_=i3), "btb", 16)
            return pp.val("btb")

        def rope_pair(ppx, i, tile, w_sb, dst_sb, dst_col, tb_rdy, wname, w_rdy, xf_n):
            for half, ch in ((0, i), (1, i + 4)):
                bank = s_banks[half]

                @ppx.block.tensor
                def _(te):
                    if w_rdy is not None:
                        ppx.wait(te, wname, w_rdy)
                    ppx.wait(te, "xfdone", xf_n)
                    bank.wait_free(te)
                    for c in range(CD):
                        mm = te.matmul(bank.t[:, :],
                                       w_sb[:, c * 1024 + ch * 128: c * 1024 + ch * 128 + 128],
                                       xhat_fm[:, c * 512:(c + 1) * 512],
                                       start=(c == 0), stop=(c == CD - 1))
                    ppx.inc(mm, "brmm", 1)

                @ppx.block.vector
                def _(ve):
                    ppx.wait(ve, "brmm")
                    if half == 0 and ppx.val("btbf") > 0:
                        ppx.wait(ve, "btbf")
                    ev = ve.tensor_copy(raw[:, half * 512:(half + 1) * 512], bank.t[:, :])
                    bank.release(ev)
            r_us = [(s_banks[0], s_banks[0].uses), (s_banks[1], s_banks[1].uses)]

            @ppx.block.vector
            def _(ve):
                for b, u in r_us:
                    ve.wait_ge(ppx.sem(b.name), u)
                ppx.wait(ve, "btb", tb_rdy)
                r0 = raw[:, 0:512]
                r1 = raw[:, 512:1024]
                ve.tensor_mul(t1[:], r0, tcos[:])
                ve.drain()
                ve.tensor_mul(t2[:], r1, tsin[:])
                ve.drain()
                ve.tensor_sub(dst_sb[:, i * n_tok + dst_col: i * n_tok + dst_col + 512], t1[:], t2[:])
                ve.drain()
                ve.tensor_mul(t1[:], r1, tcos[:])
                ve.drain()
                ve.tensor_mul(t2[:], r0, tsin[:])
                ve.drain()
                ve.tensor_add(dst_sb[:, (i + 4) * n_tok + dst_col: (i + 4) * n_tok + dst_col + 512],
                              t1[:], t2[:])
                ppx.inc(ve.drain(), "btbf", 1)

        with contextlib.ExitStack() as st1:
            p1 = pp.fresh(st1)
            wk = sbuf(p1, "bwk", [128, CD * 1024], BF16)
            wv = sbuf(p1, "bwv", [128, CD * 1024], BF16)

            @p1.block.gpsimd
            def _(g):
                for w_sb, w_d in ((wk, wktp_d), (wv, wvt_d)):
                    o3 = w_sb[:].rearrange("p (c d) -> p c d", c=CD)
                    i3 = w_d.rearrange("(c p) d -> p c d", p=128)
                    p1.inc(g.dma_start(out=o3, in_=i3), "bdw1", 16)
            w1_rdy = p1.val("bdw1")

            for tile in range(NTILE):
                xf_n = emit_xhat_fm(p1, ln, x_sb, tile, xhat_fm, xhat, tr_banks, ident, s_tr)
                for i in range(4):
                    tb = load_tables(i, tile)
                    rope_pair(p1, i, tile, wk, K_sb, tile * 512, tb,
                              "bdw1", w1_rdy if (tile == 0 and i == 0) else None, xf_n)
                for j in range(4):
                    tc = tile * 4 + j
                    for dh in range(2):
                        bank = s_banks[dh]

                        @p1.block.tensor
                        def _(te):
                            bank.wait_free(te)
                            for c in range(CD):
                                mm = te.matmul(bank.t[:, :],
                                               xhat_fm[:, c * 512 + j * 128: c * 512 + j * 128 + 128],
                                               wv[:, c * 1024 + dh * 512: c * 1024 + dh * 512 + 512],
                                               start=(c == 0), stop=(c == CD - 1))
                            p1.inc(mm, "bvmm", 1)

                        @p1.block.vector
                        def _(ve):
                            p1.wait(ve, "bvmm")
                            ev = ve.tensor_copy(V_sb[:, tc * 1024 + dh * 512: tc * 1024 + dh * 512 + 512],
                                                bank.t[:, :])
                            bank.release(ev)
                pp.fence()
            pp.cnt.update(p1.cnt)

        with contextlib.ExitStack() as st2:
            p2 = pp.fresh(st2)
            wq = sbuf(p2, "bwq", [128, CD * 1024], BF16)
            Qrot = sbuf(p2, "bQ", [128, CD * 512], BF16)
            Sf = sbuf(p2, "bS", [128, n_tok], F32)
            STe = sbuf(p2, "bSe", [128, n_tok], BF16)
            STt = sbuf(p2, "bSt", [128, n_tok], BF16)

            @p2.block.gpsimd
            def _(g):
                p2.wait(g, "bdw1")
                p2.wait(g, "fence")
                o3 = wq[:].rearrange("p (c d) -> p c d", c=CD)
                i3 = wqtp_d.rearrange("(c p) d -> p c d", p=128)
                p2.inc(g.dma_start(out=o3, in_=i3), "bdw2", 16)
            w2_rdy = p2.val("bdw2")

            for qt in range(NTILE):
                xf_n = emit_xhat_fm(p2, ln, x_sb, qt, xhat_fm, xhat, tr_banks, ident, s_tr)
                for i in range(4):
                    tb = load_tables(i, qt)
                    for half, ch in ((0, i), (1, i + 4)):
                        bank = s_banks[half]

                        @p2.block.tensor
                        def _(te):
                            if qt == 0 and i == 0 and half == 0:
                                p2.wait(te, "bdw2", w2_rdy)
                            p2.wait(te, "xfdone", xf_n)
                            bank.wait_free(te)
                            for c in range(CD):
                                mm = te.matmul(bank.t[:, :],
                                               wq[:, c * 1024 + ch * 128: c * 1024 + ch * 128 + 128],
                                               xhat_fm[:, c * 512:(c + 1) * 512],
                                               start=(c == 0), stop=(c == CD - 1))
                            p2.inc(mm, "brmm", 1)

                        @p2.block.vector
                        def _(ve):
                            p2.wait(ve, "brmm")
                            if half == 0 and p2.val("btbf") > 0:
                                p2.wait(ve, "btbf")
                            ev = ve.tensor_copy(raw[:, half * 512:(half + 1) * 512], bank.t[:, :])
                            bank.release(ev)
                    r_us = [(s_banks[0], s_banks[0].uses), (s_banks[1], s_banks[1].uses)]

                    @p2.block.vector
                    def _(ve):
                        for b, u in r_us:
                            ve.wait_ge(p2.sem(b.name), u)
                        p2.wait(ve, "btb", tb)
                        r0 = raw[:, 0:512]
                        r1 = raw[:, 512:1024]
                        ve.tensor_mul(t1[:], r0, tcos[:])
                        ve.drain()
                        ve.tensor_mul(t2[:], r1, tsin[:])
                        ve.drain()
                        ve.tensor_sub(Qrot[:, i * 512:(i + 1) * 512], t1[:], t2[:])
                        ve.drain()
                        ve.tensor_mul(t1[:], r1, tcos[:])
                        ve.drain()
                        ve.tensor_mul(t2[:], r0, tsin[:])
                        ve.drain()
                        ve.tensor_add(Qrot[:, (i + 4) * 512:(i + 5) * 512], t1[:], t2[:])
                        p2.inc(ve.drain(), "btbf", 1)

                @p2.block.vector
                def _(ve):
                    p2.inc(ve.drain(), "bqr", 1)
                q_n = p2.val("bqr")

                for qq in range(4):
                    k_hi = qt * 512 + (qq + 1) * 128
                    n_k5 = (k_hi + 511) // 512
                    for kc in range(n_k5):
                        bank = s_banks[kc % 2]
                        diag = (kc == n_k5 - 1)

                        @p2.block.tensor
                        def _(te):
                            if kc == 0:
                                p2.wait(te, "bqr", q_n)
                            bank.wait_free(te)
                            for c in range(CD):
                                mm = te.matmul(bank.t[:, :],
                                               Qrot[:, c * 512 + qq * 128: c * 512 + qq * 128 + 128],
                                               K_sb[:, c * n_tok + kc * 512: c * n_tok + kc * 512 + 512],
                                               start=(c == 0), stop=(c == CD - 1))
                            p2.inc(mm, "bsmm", 1)

                        if diag:
                            dd = (qt * 512 + qq * 128 - kc * 512) // 128

                            @p2.block.vector
                            def _(ve):
                                p2.wait(ve, "bsmm")
                                if p2.val("bex") > 0:
                                    p2.wait(ve, "bex")
                                ad = ve.tensor_add(Sf[:, kc * 512:(kc + 1) * 512],
                                                   bank.t[:, :], masksA[dd][:])
                                bank.release(ad)
                        else:
                            @p2.block.vector
                            def _(ve):
                                p2.wait(ve, "bsmm")
                                ev = ve.tensor_copy(Sf[:, kc * 512:(kc + 1) * 512], bank.t[:, :])
                                bank.release(ev)
                    s_rel = [(b, b.uses) for b in s_banks]
                    k512 = n_k5 * 512

                    @p2.block.vector
                    def _(ve):
                        for b, u in s_rel:
                            ve.wait_ge(p2.sem(b.name), u)
                        ve.reduce_max(negm[:], Sf[:, 0:k512], axis=AX.X, negate=True)
                        ve.drain()
                        p2.inc(ve.tensor_scalar(negm[:], negm[:], 0.03125, None, ALU.mult),
                               "bnm", 1)

                    @p2.block.scalar
                    def _(se):
                        p2.wait(se, "bnm")
                        if p2.val("bstr") > 0:
                            p2.wait(se, "bstr")
                        se.activation(STe[:, 0:k512], Sf[:, 0:k512], AF.Exp,
                                      bias=negm[:, 0:1], scale=0.03125,
                                      accum_out=lsum[:, 0:1])
                        p2.inc(se.drain(), "bex", 1)
                    e_n = p2.val("bex")

                    for kc in range(k512 // 128):
                        bank = tr_banks[0]

                        @p2.block.tensor
                        def _(te):
                            p2.wait(te, "bex", e_n)
                            bank.wait_free(te)
                            p2.inc(te.transpose(bank.t[:, :], STe[:, kc * 128:(kc + 1) * 128],
                                                ident[:]), "bstr", 1)

                        @p2.block.vector
                        def _(ve):
                            p2.wait(ve, "bstr")
                            ev = ve.tensor_copy(STt[:, kc * 128:(kc + 1) * 128], bank.t[:, :])
                            bank.release(ev)
                    stt_u = (tr_banks[0], tr_banks[0].uses)

                    @p2.block.vector
                    def _(ve):
                        p2.wait(ve, "bex", e_n)
                        p2.inc(ve.reciprocal(rl1[:], lsum[:, 0:1]), "brc", 1)
                    rc_n = p2.val("brc")

                    for dh in range(2):
                        @p2.block.tensor
                        def _(te):
                            if dh == 0:
                                te.wait_ge(p2.sem(stt_u[0].name), stt_u[1])
                            av_bank.wait_free(te)
                            for kc in range(k512 // 128):
                                mm = te.matmul(av_bank.t[:, :],
                                               STt[:, kc * 128:(kc + 1) * 128],
                                               V_sb[:, kc * 1024 + dh * 512: kc * 1024 + dh * 512 + 512],
                                               start=(kc == 0), stop=(kc == k512 // 128 - 1))
                            p2.inc(mm, "bavm", 1)

                        @p2.block.vector
                        def _(ve):
                            p2.wait(ve, "bavm")
                            if dh == 0:
                                p2.wait(ve, "brc", rc_n)
                            tc = qt * 4 + qq
                            xv = x_sb[:, tc * 1024 + dh * 512: tc * 1024 + dh * 512 + 512]
                            stt = ve.scalar_tensor_tensor(xv, av_bank.t[:, :], rl1[:, 0:1],
                                                          xv, ALU.mult, ALU.add)
                            av_bank.release(stt)
                            ve.drain()

                p2.fence()

            pp.cnt.update(p2.cnt)

        p.cnt.update(pp.cnt)



def _seq_io(p, block, x_sb, x_d, y_d, s, NT, load, consts=None, ident_dram=None):
    if load:
        @block.sync
        def _(sy):
            if consts is not None and s == 0:
                p.inc(sy.dma_start(out=consts["ident"][:], in_=ident_dram), "dsync", 16)
            o3 = x_sb[:].rearrange("p (c d) -> p c d", c=NT)
            i3 = x_d[s].rearrange("(c p) d -> p c d", p=128)
            p.inc(sy.dma_start(out=o3, in_=i3), "dsync", 16)
        n = p.val("dsync")
        for engname in ("tensor", "vector", "scalar"):
            @getattr(block, engname)
            def _(eng):
                p.wait(eng, "dsync", n)
    else:
        @block.sync
        def _(sy):
            sy.wait_ge(p.sem("fence"), p.cnt["fence"])
            o3 = y_d[s].rearrange("(c p) d -> p c d", p=128)
            i3 = x_sb[:].rearrange("p (c d) -> p c d", c=NT)
            p.inc(sy.dma_start(out=o3, in_=i3), "dstore", 16)
            p.wait(sy, "dstore")


def build_ffn0_only(n_tok):
    nc = bass.Bass("TRN2", target_bir_lowering=False)
    NT = n_tok // 128
    x_d = nc.dram_tensor("x", [2, n_tok, D], F32, kind="ExternalInput").ap()
    wr_d = nc.dram_tensor("wrt", [D, D], F32, kind="ExternalInput").ap()
    wk_d = nc.dram_tensor("wkt", [D, D], F32, kind="ExternalInput").ap()
    wv_d = nc.dram_tensor("wvt", [D, D], F32, kind="ExternalInput").ap()
    mur_d = nc.dram_tensor("mur", [D], F32, kind="ExternalInput").ap()
    muk_d = nc.dram_tensor("muk", [D], F32, kind="ExternalInput").ap()
    y_d = nc.dram_tensor("y", [2, n_tok, D], F32, kind="ExternalOutput").ap()

    with contextlib.ExitStack() as stack:
        p = P(nc, stack)
        ident_dram = nc.inline_tensor(np.eye(128, dtype=ml_dtypes.bfloat16), "identc").ap()
        ident = sbuf(p, "ident", [128, 128], BF16)
        eps_t = sbuf(p, "epsc", [128, 1], F32)
        x_sb = sbuf(p, "xres", [128, NT * 1024], F32)
        consts = {"ident": ident, "eps": eps_t[:, 0:1]}

        for s in range(2):
            with nc.Block() as block:
                p.block = block
                if s == 0:
                    @block.vector
                    def _(ve):
                        ve.memset(eps_t[:], LN_EPS)
                        ve.drain()
                _seq_io(p, block, x_sb, x_d, y_d, s, NT, True, consts, ident_dram)
                emit_ffn0(p, x_sb, wr_d, wk_d, wv_d, mur_d, muk_d, consts,
                          n_tok, f"s{s}")
                _seq_io(p, block, x_sb, x_d, y_d, s, NT, False)
    return nc




def build_moe_only(n_tok):
    nc = bass.Bass("TRN2", target_bir_lowering=False)
    NT = n_tok // 128
    x_d = nc.dram_tensor("x", [2, n_tok, D], F32, kind="ExternalInput").ap()
    wg_d = nc.dram_tensor("wg", [D, 8], F32, kind="ExternalInput").ap()
    w1_d = nc.dram_tensor("w1", [8, D, 4096], F32, kind="ExternalInput").ap()
    b1r_d = nc.dram_tensor("b1r", [128, 256], F32, kind="ExternalInput").ap()
    w2_d = nc.dram_tensor("w2", [8, 4096, D], F32, kind="ExternalInput").ap()
    b2_d = nc.dram_tensor("b2", [8, D], F32, kind="ExternalInput").ap()
    y_d = nc.dram_tensor("y", [2, n_tok, D], F32, kind="ExternalOutput").ap()

    with contextlib.ExitStack() as stack:
        p = P(nc, stack)
        ident_dram = nc.inline_tensor(np.eye(128, dtype=ml_dtypes.bfloat16), "identc").ap()
        identf_dram = nc.inline_tensor(np.eye(128, dtype=np.float32), "identfc").ap()
        ident = sbuf(p, "ident", [128, 128], BF16)
        identf = sbuf(p, "identf", [128, 128], F32)
        eps_t = sbuf(p, "epsc", [128, 1], F32)
        x_sb = sbuf(p, "xres", [128, NT * 1024], F32)
        consts = {"ident": ident, "identf": identf, "eps": eps_t[:, 0:1]}

        for s in range(2):
            with nc.Block() as block:
                p.block = block
                if s == 0:
                    @block.vector
                    def _(ve):
                        ve.memset(eps_t[:], LN_EPS)
                        ve.drain()
                    @block.sync
                    def _(sy):
                        p.inc(sy.dma_start(out=identf[:], in_=identf_dram), "dsync", 16)
                _seq_io(p, block, x_sb, x_d, y_d, s, NT, True, consts, ident_dram)
                emit_moe(p, x_sb, wg_d, w1_d, b1r_d, w2_d, b2_d, consts,
                         n_tok, f"s{s}", cast_w=True)
                _seq_io(p, block, x_sb, x_d, y_d, s, NT, False)
    return nc




def make_attn0_consts(nc, p):
    op = np.zeros((128, 2), dtype=ml_dtypes.bfloat16)
    op[0:64, 0] = 1.0
    op[64:128, 1] = 1.0
    op_d = nc.inline_tensor(op, "onespair_c").ap()
    onespair = sbuf(p, "onespair", [128, 2], BF16)
    masks = []
    masks_d = []
    for d in range(4):
        m = np.zeros((128, 512), dtype=ml_dtypes.bfloat16)
        for pp_ in range(128):
            j0 = d * 128 + pp_
            if j0 < 512:
                m[pp_, j0:] = 1.0
        masks_d.append(nc.inline_tensor(m, f"maskT{d}_c").ap())
        masks.append(sbuf(p, f"maskT{d}", [128, 512], BF16))
    onesrow = sbuf(p, "onesrow", [1, 128], F32)
    return op_d, onespair, masks_d, masks, onesrow


def build_attn0_only(n_tok):
    nc = bass.Bass("TRN2", target_bir_lowering=False)
    NT = n_tok // 128
    x_d = nc.dram_tensor("x", [2, n_tok, D], F32, kind="ExternalInput").ap()
    wattn_d = nc.dram_tensor("wattn", [D, 3 * D], F32, kind="ExternalInput").ap()
    wproj_d = nc.dram_tensor("wproj", [D, D], F32, kind="ExternalInput").ap()
    y_d = nc.dram_tensor("y", [2, n_tok, D], F32, kind="ExternalOutput").ap()

    with contextlib.ExitStack() as stack:
        p = P(nc, stack)
        ident_dram = nc.inline_tensor(np.eye(128, dtype=ml_dtypes.bfloat16), "identc").ap()
        identf_dram = nc.inline_tensor(np.eye(128, dtype=np.float32), "identfc").ap()
        ident = sbuf(p, "ident", [128, 128], BF16)
        identf = sbuf(p, "identf", [128, 128], F32)
        eps_t = sbuf(p, "epsc", [128, 1], F32)
        x_sb = sbuf(p, "xres", [128, NT * 1024], F32)
        op_d, onespair, masks_d, masks, onesrow = make_attn0_consts(nc, p)
        consts = {"ident": ident, "identf": identf, "eps": eps_t[:, 0:1],
                  "onespair": onespair[:], "onesrow": onesrow[:],
                  "masksT": [m[:] for m in masks]}

        for s in range(2):
            with nc.Block() as block:
                p.block = block
                if s == 0:
                    @block.vector
                    def _(ve):
                        ve.memset(eps_t[:], LN_EPS)
                        ve.drain()
                        ve.memset(onesrow[:], 1.0)
                        ve.drain()
                    @block.sync
                    def _(sy):
                        p.inc(sy.dma_start(out=identf[:], in_=identf_dram), "dsync", 16)
                        p.inc(sy.dma_start(out=onespair[:], in_=op_d), "dsync", 16)
                        for m_sb, m_d in zip(masks, masks_d):
                            p.inc(sy.dma_start(out=m_sb[:], in_=m_d), "dsync", 16)
                _seq_io(p, block, x_sb, x_d, y_d, s, NT, True, consts, ident_dram)
                emit_attn0(p, x_sb, wattn_d, wproj_d, consts, n_tok, f"s{s}")
                _seq_io(p, block, x_sb, x_d, y_d, s, NT, False)
    return nc




def build_attn1_only(n_tok):
    nc = bass.Bass("TRN2", target_bir_lowering=False)
    NT = n_tok // 128
    x_d = nc.dram_tensor("x", [2, n_tok, D], F32, kind="ExternalInput").ap()
    wqtp_d = nc.dram_tensor("wqtp", [D, D], F32, kind="ExternalInput").ap()
    wktp_d = nc.dram_tensor("wktp", [D, D], F32, kind="ExternalInput").ap()
    wvt_d = nc.dram_tensor("wvt", [D, D], F32, kind="ExternalInput").ap()
    cos_d = nc.dram_tensor("ropec", [512, n_tok], F32, kind="ExternalInput").ap()
    sin_d = nc.dram_tensor("ropes", [512, n_tok], F32, kind="ExternalInput").ap()
    y_d = nc.dram_tensor("y", [2, n_tok, D], F32, kind="ExternalOutput").ap()

    with contextlib.ExitStack() as stack:
        p = P(nc, stack)
        ident_dram = nc.inline_tensor(np.eye(128, dtype=ml_dtypes.bfloat16), "identc").ap()
        ident = sbuf(p, "ident", [128, 128], BF16)
        eps_t = sbuf(p, "epsc", [128, 1], F32)
        x_sb = sbuf(p, "xres", [128, NT * 1024], F32)
        masksA_np = []
        masksA_d = []
        masksA = []
        for dd in range(4):
            m = np.zeros((128, 512), dtype=ml_dtypes.bfloat16)
            for pp_ in range(128):
                j0 = dd * 128 + pp_
                if j0 + 1 < 512:
                    m[pp_, j0 + 1:] = -1e30
            masksA_d.append(nc.inline_tensor(m, f"maskA{dd}_c").ap())
            masksA.append(sbuf(p, f"maskA{dd}", [128, 512], BF16))
        consts = {"ident": ident, "eps": eps_t[:, 0:1],
                  "masksA": [m[:] for m in masksA]}

        for s in range(2):
            with nc.Block() as block:
                p.block = block
                if s == 0:
                    @block.vector
                    def _(ve):
                        ve.memset(eps_t[:], LN_EPS)
                        ve.drain()
                    @block.sync
                    def _(sy):
                        for m_sb, m_d in zip(masksA, masksA_d):
                            p.inc(sy.dma_start(out=m_sb[:], in_=m_d), "dsync", 16)
                _seq_io(p, block, x_sb, x_d, y_d, s, NT, True, consts, ident_dram)
                emit_attn1(p, x_sb, wqtp_d, wktp_d, wvt_d, cos_d, sin_d, consts,
                           n_tok, f"s{s}")
                _seq_io(p, block, x_sb, x_d, y_d, s, NT, False)
    return nc




def build_full(n_tok):
    nc = bass.Bass("TRN2", target_bir_lowering=False)
    NT = n_tok // 128
    x_d = nc.dram_tensor("x", [2, n_tok, D], F32, kind="ExternalInput").ap()
    wrt_d = nc.dram_tensor("wrt", [D, D], F32, kind="ExternalInput").ap()
    wkt_d = nc.dram_tensor("wkt", [D, D], F32, kind="ExternalInput").ap()
    wvt_d = nc.dram_tensor("wvt", [D, D], F32, kind="ExternalInput").ap()
    mur_d = nc.dram_tensor("mur", [D], F32, kind="ExternalInput").ap()
    muk_d = nc.dram_tensor("muk", [D], F32, kind="ExternalInput").ap()
    wattn_d = nc.dram_tensor("wattn", [D, 3 * D], F32, kind="ExternalInput").ap()
    wproj_d = nc.dram_tensor("wproj", [D, D], F32, kind="ExternalInput").ap()
    wg_d = nc.dram_tensor("wg", [D, 8], F32, kind="ExternalInput").ap()
    w1_d = nc.dram_tensor("w1", [8, D, 4096], F32, kind="ExternalInput").ap()
    b1r_d = nc.dram_tensor("b1r", [128, 256], F32, kind="ExternalInput").ap()
    w2_d = nc.dram_tensor("w2", [8, 4096, D], F32, kind="ExternalInput").ap()
    b2_d = nc.dram_tensor("b2", [8, D], F32, kind="ExternalInput").ap()
    wqtp_d = nc.dram_tensor("wqtp", [D, D], F32, kind="ExternalInput").ap()
    wktp_d = nc.dram_tensor("wktp", [D, D], F32, kind="ExternalInput").ap()
    wvt1_d = nc.dram_tensor("wvt1", [D, D], F32, kind="ExternalInput").ap()
    cos_d = nc.dram_tensor("ropec", [512, n_tok], F32, kind="ExternalInput").ap()
    sin_d = nc.dram_tensor("ropes", [512, n_tok], F32, kind="ExternalInput").ap()
    y_d = nc.dram_tensor("y", [2, n_tok, D], F32, kind="ExternalOutput").ap()

    with contextlib.ExitStack() as stack:
        p = P(nc, stack)
        ident_dram = nc.inline_tensor(np.eye(128, dtype=ml_dtypes.bfloat16), "identc").ap()
        identf_dram = nc.inline_tensor(np.eye(128, dtype=np.float32), "identfc").ap()
        ident = sbuf(p, "ident", [128, 128], BF16)
        identf = sbuf(p, "identf", [128, 128], F32)
        eps_t = sbuf(p, "epsc", [128, 1], F32)
        x_sb = sbuf(p, "xres", [128, NT * 1024], F32)
        op_d, onespair, masksT_d, masksT, onesrow = make_attn0_consts(nc, p)
        masksA_d = []
        masksA = []
        for dd in range(4):
            m = np.zeros((128, 512), dtype=ml_dtypes.bfloat16)
            for pp_ in range(128):
                j0 = dd * 128 + pp_
                if j0 + 1 < 512:
                    m[pp_, j0 + 1:] = -1e30
            masksA_d.append(nc.inline_tensor(m, f"maskA{dd}_c").ap())
            masksA.append(sbuf(p, f"maskA{dd}", [128, 512], BF16))
        consts = {"ident": ident, "identf": identf, "eps": eps_t[:, 0:1],
                  "onespair": onespair[:], "onesrow": onesrow[:],
                  "masksT": [m[:] for m in masksT],
                  "masksA": [m[:] for m in masksA]}

        for s in range(2):
            with nc.Block() as block:
                p.block = block
                if s == 0:
                    @block.vector
                    def _(ve):
                        ve.memset(eps_t[:], LN_EPS)
                        ve.drain()
                        ve.memset(onesrow[:], 1.0)
                        ve.drain()

                    @block.sync
                    def _(sy):
                        p.inc(sy.dma_start(out=identf[:], in_=identf_dram), "dsync", 16)
                        p.inc(sy.dma_start(out=onespair[:], in_=op_d), "dsync", 16)
                        for m_sb, m_d in zip(masksT, masksT_d):
                            p.inc(sy.dma_start(out=m_sb[:], in_=m_d), "dsync", 16)
                        for m_sb, m_d in zip(masksA, masksA_d):
                            p.inc(sy.dma_start(out=m_sb[:], in_=m_d), "dsync", 16)
                _seq_io(p, block, x_sb, x_d, y_d, s, NT, True, consts, ident_dram)
                emit_ffn0(p, x_sb, wrt_d, wkt_d, wvt_d, mur_d, muk_d, consts,
                          n_tok, f"s{s}a")
            with nc.Block() as block:
                p.block = block
                emit_attn0(p, x_sb, wattn_d, wproj_d, consts, n_tok, f"s{s}")
            with nc.Block() as block:
                p.block = block
                emit_ffn0(p, x_sb, wrt_d, wkt_d, wvt_d, mur_d, muk_d, consts,
                          n_tok, f"s{s}b")
            with nc.Block() as block:
                p.block = block
                emit_moe(p, x_sb, wg_d, w1_d, b1r_d, w2_d, b2_d, consts,
                         n_tok, f"s{s}a", cast_w=True)
            with nc.Block() as block:
                p.block = block
                emit_attn1(p, x_sb, wqtp_d, wktp_d, wvt1_d, cos_d, sin_d, consts,
                           n_tok, f"s{s}")
            with nc.Block() as block:
                p.block = block
                emit_moe(p, x_sb, wg_d, w1_d, b1r_d, w2_d, b2_d, consts,
                         n_tok, f"s{s}b", cast_w=True)
                _seq_io(p, block, x_sb, x_d, y_d, s, NT, False)
    return nc


def _rope_tables_host(n_tok):
    pos = np.arange(n_tok, dtype=np.float64)[:, None]
    inv = 1.0 / 10000.0 ** (np.arange(0, D, 2, dtype=np.float64) / D)
    fr = pos * inv
    cos = np.ascontiguousarray(np.cos(fr).T.astype(np.float32))
    sin = np.ascontiguousarray(np.sin(fr).T.astype(np.float32))
    return cos, sin


def kernel(**inputs):
    n_tok = 2048
    n_cores = 8
    x = np.ascontiguousarray(np.asarray(inputs["L0_attn_x"], dtype=np.float32))

    wrt = np.ascontiguousarray(np.asarray(inputs["L0_ffn_W_r"]).T)
    wkt = np.ascontiguousarray(np.asarray(inputs["L0_ffn_W_k"]).T)
    wvt = np.ascontiguousarray(np.asarray(inputs["L0_ffn_W_v"]).T)
    mur = np.ascontiguousarray(np.asarray(inputs["L0_ffn_mu_r"]))
    muk = np.ascontiguousarray(np.asarray(inputs["L0_ffn_mu_k"]))
    wattn = np.ascontiguousarray(np.asarray(inputs["L0_attn_w_attn"]))
    wproj = np.ascontiguousarray(np.asarray(inputs["L0_attn_w_proj"]))
    wg = np.ascontiguousarray(np.asarray(inputs["L1_ffn_W_gate"]))
    w1 = np.ascontiguousarray(np.asarray(inputs["L1_ffn_W1"]))
    b1 = np.asarray(inputs["L1_ffn_b1"])
    b1r = np.ascontiguousarray(b1.reshape(8, 32, 128).transpose(2, 0, 1).reshape(128, 256))
    w2 = np.ascontiguousarray(np.asarray(inputs["L1_ffn_W2"]))
    b2 = np.ascontiguousarray(np.asarray(inputs["L1_ffn_b2"]))
    perm = np.concatenate([np.arange(0, D, 2), np.arange(1, D, 2)])
    wqtp = np.ascontiguousarray(np.asarray(inputs["L1_attn_wq"]).T[:, perm])
    wktp = np.ascontiguousarray(np.asarray(inputs["L1_attn_wk"]).T[:, perm])
    wvt1 = np.ascontiguousarray(np.asarray(inputs["L1_attn_wv"]).T)
    cosf, sinf = _rope_tables_host(n_tok)

    shards = [np.ascontiguousarray(x[2 * c: 2 * c + 2]) for c in range(n_cores)]

    nc = build_full(n_tok)
    weights = {"wrt": wrt, "wkt": wkt, "wvt": wvt, "mur": mur, "muk": muk,
               "wattn": wattn, "wproj": wproj,
               "wg": wg, "w1": w1, "b1r": b1r, "w2": w2, "b2": b2,
               "wqtp": wqtp, "wktp": wktp, "wvt1": wvt1,
               "ropec": cosf, "ropes": sinf}
    in_maps = [dict(weights, x=shards[c]) for c in range(n_cores)]
    res = run_bass_kernel_spmd(nc, in_maps, core_ids=list(range(n_cores)))
    return np.concatenate([np.asarray(r["y"]) for r in res.results], axis=0)



# version 06
# speedup vs baseline: 1.2549x; 1.0726x over previous
import contextlib
import numpy as np
import ml_dtypes
import concourse.bass as bass
import concourse.mybir as mybir
from concourse.bass_utils import run_bass_kernel_spmd

F32 = mybir.dt.float32
BF16 = mybir.dt.bfloat16
AF = mybir.ActivationFunctionType
ALU = mybir.AluOpType
AX = mybir.AxisListType

D = 1024
CD = D // 128
LN_EPS = 1e-5



class P:

    def __init__(self, nc, stack):
        self.nc = nc
        self.stack = stack
        self.root_stack = stack
        self.block = None
        self._sems = {}
        self.cnt = {}
        self.bku = {}
        self.uid = [0]

    def fresh(self, stack):
        q = P.__new__(P)
        q.__dict__.update(self.__dict__)
        q.stack = stack
        return q

    def sem(self, name):
        if name not in self._sems:
            self._sems[name] = self.root_stack.enter_context(self.nc.semaphore(name))
            self.cnt[name] = 0
        return self._sems[name]

    def inc(self, inst, name, n=1):
        inst.then_inc(self.sem(name), n)
        self.cnt[name] += n
        return self.cnt[name]

    def val(self, name):
        self.sem(name)
        return self.cnt[name]

    def wait(self, eng, name, v=None):
        eng.wait_ge(self.sem(name), self.cnt[name] if v is None else v)

    def fence(self, tag="ph"):
        name = "fence"
        self.sem(name)
        target = self.cnt[name] + 3

        @self.block.tensor
        def _(te):
            self.inc(te.drain(), name, 1)
            te.wait_ge(self._sems[name], target)

        @self.block.vector
        def _(ve):
            self.inc(ve.drain(), name, 1)
            ve.wait_ge(self._sems[name], target)

        @self.block.scalar
        def _(se):
            self.inc(se.drain(), name, 1)
            se.wait_ge(self._sems[name], target)
        assert self.cnt[name] == target


class Bank:

    def __init__(self, p, tensor, name):
        self.p = p
        self.t = tensor
        self.name = "bk_" + name
        p.sem(self.name)
        p.bku.setdefault(self.name, 0)

    @property
    def uses(self):
        return self.p.bku[self.name]

    def wait_free(self, eng):
        if self.uses > 0:
            eng.wait_ge(self.p.sem(self.name), self.uses)
        self.p.bku[self.name] += 1

    def release(self, inst):
        self.p.inc(inst, self.name, 1)

    def wait_released(self, eng):
        eng.wait_ge(self.p.sem(self.name), self.uses)


def sbuf(p, name, shape, dtype):
    p.uid[0] += 1
    return p.stack.enter_context(p.nc.sbuf_tensor(f"{name}_{p.uid[0]}", shape, dtype))


def psum_bank(p, name, shape, dtype=F32):
    p.uid[0] += 1
    t = p.stack.enter_context(p.nc.psum_tensor(f"{name}_{p.uid[0]}", shape, dtype))
    return Bank(p, t, name)



class LN:

    def __init__(self, p, tag, eps_ap):
        self.p = p
        self.eps = eps_ap
        self.stats = sbuf(p, f"lnst{tag}", [128, 12], F32)
        self.mv4 = sbuf(p, f"lnmv{tag}", [128, 8], F32)
        self.lnv = sbuf(p, f"lnlv{tag}", [128, 4], F32)
        self.rstd4 = sbuf(p, f"lnrs{tag}", [128, 4], F32)
        self.s_v2s = "lnv2s"
        self.s_s2v = "lns2v"
        self.rdy = "lnrdy"
        p.sem(self.s_v2s)
        p.sem(self.s_s2v)
        p.sem(self.rdy)

    def stats_emit(self, x_view, idx):
        p = self.p

        @p.block.vector
        def _(ve):
            if idx == 0 and p.val(self.s_s2v) > 0:
                p.wait(ve, self.s_s2v)
            ve.drain()
            ve.bn_stats(self.stats[:, 0:6], x_view[:, 0:512])
            ve.bn_stats(self.stats[:, 6:12], x_view[:, 512:1024])
            ve.drain()
            p.inc(ve.bn_aggr(self.mv4[:, 2 * idx:2 * idx + 2], self.stats[:]),
                  self.s_v2s, 1)

    def rstds_emit(self, n=4):
        p = self.p

        @p.block.scalar
        def _(se):
            p.wait(se, self.s_v2s)
            vars_view = self.mv4[:].rearrange("p (c two) -> p c two", two=2)[:, 0:n, 1:2]
            se.activation(self.lnv[:, 0:n], vars_view, AF.Ln, bias=self.eps)
            se.drain()
            se.activation(self.rstd4[:, 0:n], self.lnv[:, 0:n], AF.Exp, scale=-0.5)
            p.inc(se.drain(), self.s_s2v, 1)
        self._rn = p.val(self.s_s2v)

    def apply_emit(self, x_view, out_view, idx):
        p = self.p

        @p.block.vector
        def _(ve):
            p.wait(ve, self.s_s2v, self._rn)
            p.inc(ve.tensor_scalar(out_view, x_view, self.mv4[:, 2 * idx:2 * idx + 1],
                                   self.rstd4[:, idx:idx + 1],
                                   ALU.subtract, ALU.mult), self.rdy, 1)


def emit_ffn0(p, x_sb, wr_d, wk_d, wv_d, mur_d, muk_d, consts, n_tok, tag):
    NTILE = n_tok // 512
    ident = consts["ident"]

    with contextlib.ExitStack() as st:
        pp = p.fresh(st)

        wr = sbuf(pp, "fwr", [128, CD * 1024], BF16)
        wk = sbuf(pp, "fwk", [128, CD * 1024], BF16)
        wv = sbuf(pp, "fwv", [128, CD * 1024], BF16)
        mur = sbuf(pp, "fmur", [128, CD], F32)
        muk = sbuf(pp, "fmuk", [128, CD], F32)
        xh = sbuf(pp, "fxh", [128, CD * 512], BF16)
        xs = sbuf(pp, "fxs", [128, CD * 512], BF16)
        carry = sbuf(pp, "fcar", [128, CD], BF16)
        diff = sbuf(pp, "fdif", [128, CD * 512], BF16)
        xr = sbuf(pp, "fxr", [128, CD * 512], BF16)
        xk = sbuf(pp, "fxk", [128, CD * 512], BF16)
        kact = sbuf(pp, "fka", [128, CD * 512], BF16)
        krelu = [sbuf(pp, f"fkr{i}", [128, 512], F32) for i in range(2)]
        r_sb = sbuf(pp, "fr", [128, 4 * 1024], F32)
        t1 = sbuf(pp, "ft1", [128, 512], F32)
        xhat = [sbuf(pp, f"fxt{i}", [128, 1024], BF16) for i in range(2)]

        tr_banks = [psum_bank(pp, f"ftr{i}", [128, 128], BF16) for i in range(2)]
        g_banks = [psum_bank(pp, f"fg{i}", [128, 512], F32) for i in range(4)]

        ln = LN(pp, "f", consts["eps"])
        s_tr = "ftr"

        @pp.block.gpsimd
        def _(g):
            for w_sb, w_d in ((wr, wr_d), (wk, wk_d), (wv, wv_d)):
                o3 = w_sb[:].rearrange("p (c d) -> p c d", c=CD)
                i3 = w_d.rearrange("(c p) d -> p c d", p=128)
                pp.inc(g.dma_start(out=o3, in_=i3), "dpool", 16)
            pp.inc(g.dma_start(out=mur[:], in_=mur_d.rearrange("(c p) -> p c", p=128),
                               allow_slow_non_contiguous=True), "dpool", 16)
            pp.inc(g.dma_start(out=muk[:], in_=muk_d.rearrange("(c p) -> p c", p=128),
                               allow_slow_non_contiguous=True), "dpool", 16)
        w_ready = pp.val("dpool")

        for tile in range(NTILE):
            for j in range(4):
                ln.stats_emit(x_sb[:, (tile * 4 + j) * 1024:(tile * 4 + j + 1) * 1024], j)
            ln.rstds_emit()
            for j in range(4):
                tc = tile * 4 + j
                xht = xhat[j % 2]
                if j >= 2:

                    @pp.block.vector
                    def _(ve):
                        pp.wait(ve, s_tr, pp.cnt.get(s_tr, 0) - CD)
                ln.apply_emit(x_sb[:, tc * 1024:(tc + 1) * 1024], xht[:], j)
                ln_n = pp.val(ln.rdy)
                for c in range(CD):
                    bank = tr_banks[c % 2]

                    @pp.block.tensor
                    def _(te):
                        pp.wait(te, ln.rdy, ln_n)
                        bank.wait_free(te)
                        pp.inc(te.transpose(bank.t[:, :], xht[:, c * 128:(c + 1) * 128],
                                            ident[:]), s_tr, 1)

                    @pp.block.vector
                    def _(ve):
                        pp.wait(ve, s_tr)
                        col = j * 128
                        ve.tensor_copy(xh[:, c * 512 + col: c * 512 + col + 128], bank.t[:, :])
                        ve.drain()
                        ncols = 127 if j == 3 else 128
                        ev = ve.tensor_copy(xs[:, c * 512 + col + 1: c * 512 + col + 1 + ncols],
                                            bank.t[:, 0:ncols])
                        bank.release(ev)

            @pp.block.vector
            def _(ve):
                if tile == 0:
                    ve.memset(carry[:], 0.0)
                    ve.drain()
                for c in range(CD):
                    ve.tensor_copy(xs[:, c * 512: c * 512 + 1], carry[:, c:c + 1])
                ve.drain()
                for c in range(CD):
                    ve.tensor_copy(carry[:, c:c + 1], xh[:, c * 512 + 511: c * 512 + 512])
            pp.fence()

            @pp.block.vector
            def _(ve):
                if tile == 0:
                    pp.wait(ve, "dpool", w_ready)
                ve.tensor_sub(diff[:], xh[:], xs[:])
                ve.drain()
                for c in range(CD):
                    ve.scalar_tensor_tensor(xr[:, c * 512:(c + 1) * 512],
                                            diff[:, c * 512:(c + 1) * 512],
                                            mur[:, c:c + 1],
                                            xs[:, c * 512:(c + 1) * 512],
                                            ALU.mult, ALU.add)
                    ve.scalar_tensor_tensor(xk[:, c * 512:(c + 1) * 512],
                                            diff[:, c * 512:(c + 1) * 512],
                                            muk[:, c:c + 1],
                                            xs[:, c * 512:(c + 1) * 512],
                                            ALU.mult, ALU.add)
            pp.fence()

            frmm0 = pp.val("frmm")
            ubg = [pp.bku[b.name] for b in g_banks]
            gb_sem = [pp.sem(b.name) for b in g_banks]

            @pp.block.tensor
            def _(te):
                if tile == 0:
                    pp.wait(te, "dpool", w_ready)
                for i in range(8):
                    j, dh = i // 2, i % 2
                    b = i % 4
                    if ubg[b] + i // 4 > 0:
                        te.wait_ge(gb_sem[b], ubg[b] + i // 4)
                    for c in range(CD):
                        mm = te.matmul(g_banks[b].t[:, :],
                                       xr[:, c * 512 + j * 128: c * 512 + j * 128 + 128],
                                       wr[:, c * 1024 + dh * 512: c * 1024 + dh * 512 + 512],
                                       start=(c == 0), stop=(c == CD - 1))
                    mm.then_inc(pp.sem("frmm"), 1)

            @pp.block.scalar
            def _(se):
                for i in range(8):
                    j, dh = i // 2, i % 2
                    se.wait_ge(pp.sem("frmm"), frmm0 + i + 1)
                    se.activation(r_sb[:, j * 1024 + dh * 512: j * 1024 + dh * 512 + 512],
                                  g_banks[i % 4].t[:, :], AF.Sigmoid)
                    pp.inc(se.drain(), g_banks[i % 4].name, 1)
            pp.cnt["frmm"] = frmm0 + 8
            for b in range(4):
                pp.bku[g_banks[b].name] = ubg[b] + 2

            fkmm0 = pp.val("fkmm")
            fkrl0 = pp.val("fkrl")
            fkev0 = pp.val("fkev")
            ubg = [pp.bku[b.name] for b in g_banks]

            @pp.block.tensor
            def _(te):
                for dk in range(8):
                    b = dk % 4
                    if ubg[b] + dk // 4 > 0:
                        te.wait_ge(gb_sem[b], ubg[b] + dk // 4)
                    for c in range(CD):
                        mm = te.matmul(g_banks[b].t[:, :],
                                       wk[:, c * 1024 + dk * 128: c * 1024 + dk * 128 + 128],
                                       xk[:, c * 512:(c + 1) * 512],
                                       start=(c == 0), stop=(c == CD - 1))
                    mm.then_inc(pp.sem("fkmm"), 1)

            @pp.block.vector
            def _(ve):
                for dk in range(8):
                    ve.wait_ge(pp.sem("fkmm"), fkmm0 + dk + 1)
                    if dk >= 2:
                        ve.wait_ge(pp.sem("fkev"), fkev0 + dk - 1)
                    mx = ve.tensor_scalar(krelu[dk % 2][:, :], g_banks[dk % 4].t[:, :],
                                          0.0, None, ALU.max)
                    mx.then_inc(pp.sem(g_banks[dk % 4].name), 1)
                    ve.drain().then_inc(pp.sem("fkrl"), 1)

            @pp.block.scalar
            def _(se):
                for dk in range(8):
                    se.wait_ge(pp.sem("fkrl"), fkrl0 + dk + 1)
                    se.activation(kact[:, dk * 512:(dk + 1) * 512], krelu[dk % 2][:, :],
                                  AF.Square)
                    se.drain().then_inc(pp.sem("fkev"), 1)
            pp.cnt["fkmm"] = fkmm0 + 8
            pp.cnt["fkrl"] = fkrl0 + 8
            pp.cnt["fkev"] = fkev0 + 8
            for b in range(4):
                pp.cnt[g_banks[b].name] += 2
                pp.bku[g_banks[b].name] = ubg[b] + 2
            kact_n = pp.cnt["fkev"]

            fomm0 = pp.val("fomm")
            ubg = [pp.bku[b.name] for b in g_banks]

            @pp.block.tensor
            def _(te):
                pp.wait(te, "fkev", kact_n)
                for i in range(8):
                    j, dh = i // 2, i % 2
                    b = i % 4
                    if ubg[b] + i // 4 > 0:
                        te.wait_ge(gb_sem[b], ubg[b] + i // 4)
                    for dk in range(CD):
                        mm = te.matmul(g_banks[b].t[:, :],
                                       kact[:, dk * 512 + j * 128: dk * 512 + j * 128 + 128],
                                       wv[:, dk * 1024 + dh * 512: dk * 1024 + dh * 512 + 512],
                                       start=(dk == 0), stop=(dk == CD - 1))
                    mm.then_inc(pp.sem("fomm"), 1)

            @pp.block.vector
            def _(ve):
                for i in range(8):
                    j, dh = i // 2, i % 2
                    ve.wait_ge(pp.sem("fomm"), fomm0 + i + 1)
                    tc = tile * 4 + j
                    xv = x_sb[:, tc * 1024 + dh * 512: tc * 1024 + dh * 512 + 512]
                    stt = ve.scalar_tensor_tensor(t1[:, :], g_banks[i % 4].t[:, :], 0.5,
                                                  r_sb[:, j * 1024 + dh * 512: j * 1024 + dh * 512 + 512],
                                                  ALU.mult, ALU.mult)
                    stt.then_inc(pp.sem(g_banks[i % 4].name), 1)
                    ve.drain()
                    ve.tensor_add(xv, xv, t1[:, :])
                    ve.drain()
            pp.cnt["fomm"] = fomm0 + 8
            for b in range(4):
                pp.cnt[g_banks[b].name] += 2
                pp.bku[g_banks[b].name] = ubg[b] + 2

            pp.fence()

        p.cnt.update(pp.cnt)






def emit_xhat_fm(pp, ln, x_sb, tile, xhat_fm, xhat_bufs, tr_banks, ident, s_tr):
    for j in range(4):
        tc = tile * 4 + j
        ln.stats_emit(x_sb[:, tc * 1024:(tc + 1) * 1024], j)
    ln.rstds_emit()
    for j in range(4):
        tc = tile * 4 + j
        xht = xhat_bufs[j % 2]
        if j >= 2:
            @pp.block.vector
            def _(ve):
                pp.wait(ve, s_tr, pp.val(s_tr) - CD)
        ln.apply_emit(x_sb[:, tc * 1024:(tc + 1) * 1024], xht[:], j)
        ln_n = pp.val(ln.rdy)
        for c in range(CD):
            bank = tr_banks[c % len(tr_banks)]

            @pp.block.tensor
            def _(te):
                pp.wait(te, ln.rdy, ln_n)
                bank.wait_free(te)
                pp.inc(te.transpose(bank.t[:, :], xht[:, c * 128:(c + 1) * 128],
                                    ident[:]), s_tr, 1)

            @pp.block.vector
            def _(ve):
                pp.wait(ve, s_tr)
                ev = ve.tensor_copy(xhat_fm[:, c * 512 + j * 128: c * 512 + j * 128 + 128],
                                    bank.t[:, :])
                bank.release(ev)

    @pp.block.vector
    def _(ve):
        pp.inc(ve.drain(), "xfdone", 1)
    return pp.val("xfdone")



def emit_moe(p, x_sb, wg_d, w1_d, b1r_d, w2_d, b2_d, consts, n_tok, tag,
             cast_w=True):
    NTILE = n_tok // 512
    ident = consts["ident"]

    with contextlib.ExitStack() as st:
        pp = p.fresh(st)

        wg = sbuf(pp, "mwg", [128, CD * 8], F32)
        b1r = sbuf(pp, "mb1", [128, 256], F32)
        b2 = sbuf(pp, "mb2", [8, 1024], F32)
        xhat_fm = sbuf(pp, "mxf", [128, CD * 512], BF16)
        xhat = [sbuf(pp, f"mxt{i}", [128, 1024], BF16) for i in range(2)]
        xhf32 = sbuf(pp, "mxg", [128, 1024], F32)
        xf32c = sbuf(pp, "mxgc", [128, CD * 128], F32)
        w1 = sbuf(pp, "mw1", [128, CD * 4096], BF16)
        w2 = [sbuf(pp, f"mw2{i}", [128, 4 * 512], BF16) for i in range(2)]
        h_sb = sbuf(pp, "mh", [128, 32 * 512], BF16)
        e8 = sbuf(pp, "me8", [128, 8], F32)
        ge8 = sbuf(pp, "mge", [128, 8], F32)
        tmp8 = sbuf(pp, "mt8", [128, 8], F32)
        msk2 = sbuf(pp, "mm2", [128, 8], F32)
        m1n = sbuf(pp, "mm1n", [128, 1], F32)
        m2n = sbuf(pp, "mm2n", [128, 1], F32)
        lsum = sbuf(pp, "mls", [128, 1], F32)
        lrc = sbuf(pp, "mlr", [128, 1], F32)
        maskp = sbuf(pp, "mmp", [128, 4 * 8], F32)
        mtT = sbuf(pp, "mmt", [8, 4 * 128], F32)

        tr_banks = [psum_bank(pp, f"mtr{i}", [128, 128], BF16) for i in range(1)]
        h_banks = [psum_bank(pp, f"mhb{i}", [128, 512], F32) for i in range(2)]
        y_banks = [psum_bank(pp, f"myb{i}", [128, 512], F32) for i in range(4)]
        sm_bank = psum_bank(pp, "msm", [128, 512], F32)

        ln = LN(pp, "m", consts["eps"])
        s_tr = "mtr"
        dqeng = "gpsimd" if cast_w else "sync"

        @pp.block.gpsimd
        def _(g):
            pp.inc(g.dma_start(out=wg[:].rearrange("p (c e) -> p c e", c=CD),
                               in_=wg_d.rearrange("(c p) e -> p c e", p=128)), "dwg", 16)
        @pp.block.sync
        def _(sy):
            pp.inc(sy.dma_start(out=b1r[:], in_=b1r_d), "dsync2", 16)
            pp.inc(sy.dma_start(out=b2[:], in_=b2_d), "dsync2", 16)
        smalls_g = pp.val("dwg")
        smalls_s = pp.val("dsync2")

        def load_w1(e):
            eng = getattr(pp.block, dqeng)

            @eng
            def _(g):
                if pp.val("mw1f") > 0:
                    pp.wait(g, "mw1f")
                o3 = w1[:].rearrange("p (c h) -> p c h", c=CD)
                i3 = w1_d[e].rearrange("(c p) h -> p c h", p=128)
                pp.inc(g.dma_start(out=o3, in_=i3), "dw1", 16)
            return pp.val("dw1")

        def load_w2(e, dh, hq, buf):
            eng = getattr(pp.block, dqeng)

            @eng
            def _(g):
                if pp.val(f"mw2f{buf}") > 0:
                    pp.wait(g, f"mw2f{buf}")
                o3 = w2[buf][:].rearrange("p (q d) -> p q d", q=4)
                i3 = w2_d[e][hq * 512:(hq + 1) * 512,
                             dh * 512:(dh + 1) * 512].rearrange("(q p) d -> p q d", p=128)
                pp.inc(g.dma_start(out=o3, in_=i3), f"dw2{buf}", 16)
            return pp.val(f"dw2{buf}")

        w1_rdy = load_w1(0)

        for tile in range(NTILE):
            for j in range(4):
                ln.stats_emit(x_sb[:, (tile * 4 + j) * 1024:(tile * 4 + j + 1) * 1024], j)
            ln.rstds_emit()
            for j in range(4):
                tc = tile * 4 + j
                xht = xhat[j % 2]
                if j >= 2:
                    @pp.block.vector
                    def _(ve):
                        pp.wait(ve, s_tr, pp.val(s_tr) - 2 * CD)
                ln.apply_emit(x_sb[:, tc * 1024:(tc + 1) * 1024], xht[:], j)
                ln_n = pp.val(ln.rdy)
                @pp.block.vector
                def _(ve):
                    if pp.val(s_tr) > 0:
                        pp.wait(ve, s_tr)
                    pp.inc(ve.tensor_scalar(xhf32[:], x_sb[:, tc * 1024:(tc + 1) * 1024],
                                            ln.mv4[:, 2 * j:2 * j + 1],
                                            ln.rstd4[:, j:j + 1],
                                            ALU.subtract, ALU.mult), "mf32", 1)
                f32_n = pp.val("mf32")
                for c in range(CD):
                    bank = tr_banks[0]

                    @pp.block.tensor
                    def _(te):
                        pp.wait(te, ln.rdy, ln_n)
                        bank.wait_free(te)
                        pp.inc(te.transpose(bank.t[:, :], xht[:, c * 128:(c + 1) * 128],
                                            ident[:]), s_tr, 1)

                    @pp.block.vector
                    def _(ve):
                        pp.wait(ve, s_tr)
                        ev = ve.tensor_copy(xhat_fm[:, c * 512 + j * 128: c * 512 + j * 128 + 128],
                                            bank.t[:, :])
                        bank.release(ev)
                for c in range(CD):
                    @pp.block.tensor
                    def _(te):
                        pp.wait(te, "mf32", f32_n)
                        sm_bank.wait_free(te)
                        pp.inc(te.transpose(sm_bank.t[0:128, 0:128],
                                            xhf32[:, c * 128:(c + 1) * 128],
                                            consts["identf"][:]), s_tr, 1)

                    @pp.block.vector
                    def _(ve):
                        pp.wait(ve, s_tr)
                        ev = ve.tensor_copy(xf32c[:, c * 128:(c + 1) * 128], sm_bank.t[0:128, 0:128])
                        sm_bank.release(ev)

                @pp.block.tensor
                def _(te):
                    if tile == 0 and j == 0:
                        pp.wait(te, "dwg", smalls_g)
                        pp.wait(te, "dsync2", smalls_s)
                    sm_bank.wait_free(te)
                    for c in range(CD):
                        mm = te.matmul(sm_bank.t[:, 0:8],
                                       xf32c[:, c * 128:(c + 1) * 128],
                                       wg[:, c * 8:(c + 1) * 8],
                                       start=(c == 0), stop=(c == CD - 1))
                    pp.inc(mm, "mgmm", 1)

                @pp.block.vector
                def _(ve):
                    pp.wait(ve, "mgmm")
                    gps = sm_bank.t[:, 0:8]
                    ve.reduce_max(m1n[:], gps, axis=AX.X, negate=True)
                    ve.drain()
                    ve.tensor_scalar(ge8[:], gps, m1n[:, 0:1], 0.0, ALU.add, ALU.is_ge)
                    ve.drain()
                    ve.tensor_scalar(tmp8[:], ge8[:], -1e30, None, ALU.mult)
                    ve.drain()
                    ve.tensor_add(msk2[:], gps, tmp8[:])
                    ve.drain()
                    ve.reduce_max(m2n[:], msk2[:], axis=AX.X, negate=True)
                    ve.drain()
                    pp.inc(ve.tensor_scalar(ge8[:], gps, m2n[:, 0:1], 0.0, ALU.add, ALU.is_ge),
                           "mdv", 1)

                @pp.block.scalar
                def _(se):
                    pp.wait(se, "mdv")
                    se.activation(e8[:], sm_bank.t[:, 0:8], AF.Exp, bias=m1n[:, 0:1])
                    pp.inc(se.drain(), "mex", 1)

                @pp.block.vector
                def _(ve):
                    pp.wait(ve, "mex")
                    em = ve.tensor_mul(e8[:], e8[:], ge8[:])
                    sm_bank.release(em)
                    ve.drain()
                    ve.reduce_sum(lsum[:], e8[:], axis=AX.X)
                    ve.drain()
                    ve.reciprocal(lrc[:], lsum[:])
                    ve.drain()
                    pp.inc(ve.tensor_scalar(maskp[:, j * 8:(j + 1) * 8], e8[:],
                                            lrc[:, 0:1], 0.0625, ALU.mult, ALU.mult),
                           "mmp", 1)

            @pp.block.scalar
            def _(se):
                pp.inc(se.drain(), "xfdone", 1)
            xf_n = pp.val("xfdone")

            pp.fence()
            mp_n = pp.val("mmp")
            for j in range(4):
                @pp.block.tensor
                def _(te):
                    pp.wait(te, "mmp", mp_n)
                    sm_bank.wait_free(te)
                    pp.inc(te.transpose(sm_bank.t[0:8, 0:128],
                                        maskp[:, j * 8:(j + 1) * 8], consts["identf"][:]),
                           "mtp", 1)

                @pp.block.vector
                def _(ve):
                    pp.wait(ve, "mtp")
                    ev = ve.tensor_copy(mtT[0:8, j * 128:(j + 1) * 128], sm_bank.t[0:8, 0:128])
                    sm_bank.release(ev)
            for j in range(4):
                for dh in range(2):
                    @pp.block.tensor
                    def _(te):
                        sm_bank.wait_free(te)
                        pp.inc(te.matmul(sm_bank.t[:, 0:512],
                                         mtT[0:8, j * 128:(j + 1) * 128],
                                         b2[0:8, dh * 512:(dh + 1) * 512],
                                         start=True, stop=True), "mb2m", 1)

                    @pp.block.vector
                    def _(ve):
                        pp.wait(ve, "mb2m")
                        tc = tile * 4 + j
                        xv = x_sb[:, tc * 1024 + dh * 512: tc * 1024 + dh * 512 + 512]
                        ad = ve.tensor_add(xv, xv, sm_bank.t[:, 0:512])
                        sm_bank.release(ad)
                        ve.drain()

            for e in range(8):
                for hc in range(32):
                    bank = h_banks[hc % 2]

                    @pp.block.tensor
                    def _(te):
                        if hc == 0:
                            pp.wait(te, "dw1", w1_rdy)
                            pp.wait(te, "xfdone", xf_n)
                        bank.wait_free(te)
                        for c in range(CD):
                            mm = te.matmul(bank.t[:, :],
                                           w1[:, c * 4096 + hc * 128: c * 4096 + hc * 128 + 128],
                                           xhat_fm[:, c * 512:(c + 1) * 512],
                                           start=(c == 0), stop=(c == CD - 1))
                        pp.inc(mm, "mh1", 1)

                    @pp.block.vector
                    def _(ve):
                        pp.wait(ve, "mh1")
                        ev = ve.tensor_scalar(h_sb[:, hc * 512:(hc + 1) * 512], bank.t[:, :],
                                              b1r[:, e * 32 + hc: e * 32 + hc + 1], 0.0,
                                              ALU.add, ALU.max)
                        bank.release(ev)

                h_rel = [(b, b.uses) for b in h_banks]

                @pp.block.tensor
                def _(te):
                    pp.inc(te.drain(), "mw1f", 1)
                if not (e == 7 and tile == NTILE - 1):
                    w1_rdy = load_w1((e + 1) % 8)

                seq = [(dh, hq) for dh in (0, 1) for hq in range(8)]
                rdy = {}
                rdy[(0, 0)] = load_w2(e, 0, 0, 0)
                rdy[(0, 1)] = load_w2(e, 0, 1, 1)
                for idx, (dh, hq) in enumerate(seq):
                    buf = idx % 2

                    @pp.block.tensor
                    def _(te):
                        pp.wait(te, f"dw2{idx % 2}", rdy[(dh, hq)])
                        if hq == 0:
                            for j in range(4):
                                y_banks[j].wait_free(te)
                            if idx == 0:
                                for b, u in h_rel:
                                    te.wait_ge(pp.sem(b.name), u)
                        for hc4 in range(4):
                            hc = hq * 4 + hc4
                            for j in range(4):
                                mm = te.matmul(y_banks[j].t[:, :],
                                               h_sb[:, hc * 512 + j * 128: hc * 512 + j * 128 + 128],
                                               w2[buf][:, hc4 * 512:(hc4 + 1) * 512],
                                               start=(hq == 0 and hc4 == 0),
                                               stop=(hq == 7 and hc4 == 3))
                        pp.inc(mm, f"mw2f{buf}", 1)
                        if hq == 7:
                            pp.inc(te.drain(), "my2", 1)

                    if idx + 2 < 16:
                        nd, nh = seq[idx + 2]
                        rdy[(nd, nh)] = load_w2(e, nd, nh, buf)

                    if hq == 7:
                        y_done = pp.val("my2")

                        @pp.block.vector
                        def _(ve):
                            pp.wait(ve, "my2", y_done)
                            for j in range(4):
                                tc = tile * 4 + j
                                xv = x_sb[:, tc * 1024 + dh * 512: tc * 1024 + dh * 512 + 512]
                                stt = ve.scalar_tensor_tensor(xv, y_banks[j].t[:, :],
                                                              maskp[:, j * 8 + e: j * 8 + e + 1],
                                                              xv, ALU.mult, ALU.add)
                                y_banks[j].release(stt)
                                ve.drain()

            pp.fence()

        p.cnt.update(pp.cnt)





def emit_attn0(p, x_sb, wattn_d, wproj_d, consts, n_tok, tag):
    NTILE = n_tok // 512
    NT = n_tok // 128
    ident = consts["ident"]
    identf = consts["identf"]

    with contextlib.ExitStack() as st:
        pp = p.fresh(st)

        K_sb = sbuf(pp, "aK", [128, CD * n_tok], BF16)
        V_sb = sbuf(pp, "aV", [128, NT * 1040], BF16)
        knm = sbuf(pp, "akm", [1, 16], F32)
        knt = sbuf(pp, "akt", [1, 1], F32)
        qn2m = sbuf(pp, "aqm", [1, 16], F32)
        mhn = sbuf(pp, "amh", [1, 16], F32)
        prod = sbuf(pp, "apr", [1, 16], F32)
        xhat_fm = sbuf(pp, "axf", [128, CD * 512], BF16)
        xhat = [sbuf(pp, f"axt{i}", [128, 1024], BF16) for i in range(2)]
        sq = sbuf(pp, "asq", [128, 512], BF16)
        mhb = sbuf(pp, "amb", [128, 1], F32)
        onesrow = consts["onesrow"]
        onespair = consts["onespair"]
        masksT = consts["masksT"]

        tr_banks = [psum_bank(pp, "atr", [128, 128], BF16)]
        s_banks = [psum_bank(pp, f"as{i}", [128, 512], F32) for i in range(2)]
        av_bank = psum_bank(pp, "aav", [128, 512], F32)
        sm_bank = psum_bank(pp, "asm", [128, 512], F32)

        ln = LN(pp, "a", consts["eps"])
        s_tr = "atr"

        with contextlib.ExitStack() as st1:
            p1 = pp.fresh(st1)
            wak = sbuf(p1, "awk", [128, CD * 1024], BF16)
            wav = sbuf(p1, "awv", [128, CD * 1024], BF16)

            @p1.block.gpsimd
            def _(g):
                for w_sb, lo in ((wak, 0), (wav, 2048)):
                    o3 = w_sb[:].rearrange("p (c d) -> p c d", c=CD)
                    i3 = wattn_d[:, lo:lo + 1024].rearrange("(c p) d -> p c d", p=128)
                    p1.inc(g.dma_start(out=o3, in_=i3), "adw1", 16)
            w1_rdy = p1.val("adw1")

            @p1.block.vector
            def _(ve):
                ve.memset(V_sb[:], 1.0)
                ve.drain()
                ve.memset(knm[:], 0.0)
                p1.inc(ve.drain(), "avini", 1)
            vini = p1.val("avini")

            for tile in range(NTILE):
                xf_n = emit_xhat_fm(p1, ln, x_sb, tile, xhat_fm, xhat, tr_banks, ident, s_tr)

                for ch in range(CD):
                    bank = s_banks[ch % 2]

                    @p1.block.tensor
                    def _(te):
                        if tile == 0 and ch == 0:
                            p1.wait(te, "adw1", w1_rdy)
                        p1.wait(te, "xfdone", xf_n)
                        bank.wait_free(te)
                        for c in range(CD):
                            mm = te.matmul(bank.t[:, :],
                                           wak[:, c * 1024 + ch * 128: c * 1024 + ch * 128 + 128],
                                           xhat_fm[:, c * 512:(c + 1) * 512],
                                           start=(c == 0), stop=(c == CD - 1))
                        p1.inc(mm, "akmm", 1)

                    @p1.block.vector
                    def _(ve):
                        p1.wait(ve, "akmm")
                        kv = K_sb[:, ch * n_tok + tile * 512: ch * n_tok + tile * 512 + 512]
                        ev = ve.tensor_copy(kv, bank.t[:, :])
                        bank.release(ev)
                        ve.drain()
                        p1.inc(ve.tensor_mul(sq[:, :], kv, kv), "aksq", 1)

                    for par in range(2):
                        @p1.block.tensor
                        def _(te):
                            p1.wait(te, "aksq")
                            sm_bank.wait_free(te)
                            p1.inc(te.matmul(sm_bank.t[0:1, 0:512],
                                             onespair[:, par:par + 1], sq[:, :],
                                             start=True, stop=True), "aknm", 1)

                        @p1.block.vector
                        def _(ve):
                            p1.wait(ve, "aknm")
                            h = 2 * ch + par
                            mx = ve.reduce_max(knt[0:1, 0:1], sm_bank.t[0:1, 0:512], axis=AX.X)
                            sm_bank.release(mx)
                            ve.drain()
                            ve.tensor_max(knm[0:1, h:h + 1], knm[0:1, h:h + 1], knt[0:1, 0:1])
                            ve.drain()

                for j in range(4):
                    tc = tile * 4 + j
                    for vh in range(2):
                        bank = s_banks[vh % 2]

                        @p1.block.tensor
                        def _(te):
                            bank.wait_free(te)
                            for c in range(CD):
                                mm = te.matmul(bank.t[:, :],
                                               xhat_fm[:, c * 512 + j * 128: c * 512 + j * 128 + 128],
                                               wav[:, c * 1024 + vh * 512: c * 1024 + vh * 512 + 512],
                                               start=(c == 0), stop=(c == CD - 1))
                            p1.inc(mm, "avmm", 1)

                        @p1.block.vector
                        def _(ve):
                            p1.wait(ve, "avmm")
                            if tile == 0 and j == 0 and vh == 0:
                                p1.wait(ve, "avini", vini)
                            ov = V_sb[:, tc * 1040 + vh * 520: tc * 1040 + vh * 520 + 520]
                            ov = ov.rearrange("p (h i) -> p h i", h=8)[:, :, 0:64]
                            iv = bank.t[:, :].rearrange("p (h i) -> p h i", h=8)
                            ev = ve.tensor_copy(ov, iv)
                            bank.release(ev)
                pp.fence()

            pp.fence()
            pp.cnt.update(p1.cnt)

        with contextlib.ExitStack() as st2:
            p2 = pp.fresh(st2)
            waq = sbuf(p2, "awq", [128, CD * 1024], BF16)
            wap = sbuf(p2, "awp", [128, CD * 512], BF16)
            Qh = sbuf(p2, "aQ", [128, 16 * 512], BF16)
            STx = [sbuf(p2, f"aST{i}", [128, 512], BF16) for i in range(2)]
            y_sb = sbuf(p2, "ay", [128, CD * 512], BF16)
            rl = sbuf(p2, "arl", [1, 512], F32)
            lbc = sbuf(p2, "albc", [64, 512], F32)

            @p2.block.gpsimd
            def _(g):
                p2.wait(g, "adw1")
                p2.wait(g, "fence")
                o3 = waq[:].rearrange("p (c d) -> p c d", c=CD)
                i3 = wattn_d[:, 1024:2048].rearrange("(c p) d -> p c d", p=128)
                p2.inc(g.dma_start(out=o3, in_=i3), "adw2", 16)
            w2_rdy = p2.val("adw2")

            def load_wap(dh):
                @p2.block.gpsimd
                def _(g):
                    if p2.val("awapf") > 0:
                        p2.wait(g, "awapf")
                    o3 = wap[:].rearrange("p (c d) -> p c d", c=CD)
                    i3 = wproj_d[:, dh * 512:(dh + 1) * 512].rearrange("(c p) d -> p c d", p=128)
                    p2.inc(g.dma_start(out=o3, in_=i3), "adwp", 16)
                return p2.val("adwp")

            for qt in range(NTILE):
                xf_n = emit_xhat_fm(p2, ln, x_sb, qt, xhat_fm, xhat, tr_banks, ident, s_tr)

                @p2.block.vector
                def _(ve):
                    ve.memset(Qh[:], 0.0)
                    p2.inc(ve.drain(), "aqz", 1)
                qz = p2.val("aqz")

                for qh in range(CD):
                    bank = s_banks[qh % 2]

                    @p2.block.tensor
                    def _(te):
                        if qt == 0 and qh == 0:
                            p2.wait(te, "adw2", w2_rdy)
                        p2.wait(te, "xfdone", xf_n)
                        bank.wait_free(te)
                        for c in range(CD):
                            mm = te.matmul(bank.t[:, :],
                                           waq[:, c * 1024 + qh * 128: c * 1024 + qh * 128 + 128],
                                           xhat_fm[:, c * 512:(c + 1) * 512],
                                           start=(c == 0), stop=(c == CD - 1))
                        p2.inc(mm, "aqmm", 1)

                    @p2.block.vector
                    def _(ve):
                        p2.wait(ve, "aqmm")
                        if qh == 0:
                            p2.wait(ve, "aqz", qz)
                        ve.tensor_scalar(Qh[0:64, (2 * qh) * 512:(2 * qh) * 512 + 512],
                                         bank.t[0:64, :], 0.125, None, ALU.mult)
                        ve.drain()
                        ev = ve.tensor_scalar(Qh[64:128, (2 * qh + 1) * 512:(2 * qh + 1) * 512 + 512],
                                              bank.t[64:128, :], 0.125, None, ALU.mult)
                        bank.release(ev)

                    @p2.block.vector
                    def _(ve):
                        qe = Qh[0:64, (2 * qh) * 512:(2 * qh) * 512 + 512]
                        qo = Qh[64:128, (2 * qh + 1) * 512:(2 * qh + 1) * 512 + 512]
                        ve.tensor_mul(sq[0:64, :], qe, qe)
                        ve.drain()
                        p2.inc(ve.tensor_mul(sq[64:128, :], qo, qo), "aqsq", 1)

                    for par in range(2):
                        @p2.block.tensor
                        def _(te):
                            p2.wait(te, "aqsq")
                            sm_bank.wait_free(te)
                            p2.inc(te.matmul(sm_bank.t[0:1, 0:512],
                                             onespair[:, par:par + 1], sq[:, :],
                                             start=True, stop=True), "aqnm", 1)

                        @p2.block.vector
                        def _(ve):
                            p2.wait(ve, "aqnm")
                            mx = ve.reduce_max(qn2m[0:1, 2 * qh + par:2 * qh + par + 1],
                                               sm_bank.t[0:1, 0:512], axis=AX.X)
                            sm_bank.release(mx)

                @p2.block.vector
                def _(ve):
                    ve.drain()
                    p2.inc(ve.tensor_mul(prod[0:1, :], qn2m[0:1, :], knm[0:1, :]),
                           "apro", 1)

                @p2.block.scalar
                def _(se):
                    p2.wait(se, "apro")
                    se.activation(prod[0:1, :], prod[0:1, :], AF.Ln)
                    se.drain()
                    se.activation(mhn[0:1, :], prod[0:1, :], AF.Exp, scale=0.5)
                    p2.inc(se.drain(), "amh1", 1)

                @p2.block.vector
                def _(ve):
                    p2.wait(ve, "amh1")
                    p2.inc(ve.tensor_scalar(mhn[0:1, :], mhn[0:1, :], -1.0, None, ALU.mult),
                           "amh2", 1)
                mh_n = p2.val("amh2")

                n_kc = 4 * (qt + 1)
                for h in range(16):
                    ch = h // 2
                    @p2.block.tensor
                    def _(te):
                        p2.wait(te, "amh2", mh_n)
                        sm_bank.wait_free(te)
                        p2.inc(te.matmul(sm_bank.t[0:128, 0:1], onesrow, mhn[0:1, h:h + 1],
                                         start=True, stop=True), "ambc", 1)

                    @p2.block.vector
                    def _(ve):
                        p2.wait(ve, "ambc")
                        ev = ve.tensor_copy(mhb[:, 0:1], sm_bank.t[0:128, 0:1])
                        sm_bank.release(ev)
                    mb_u = sm_bank.uses

                    asmm0 = p2.val("asmm")
                    astx0 = p2.val("astx")
                    asev0 = p2.val("asev")
                    ub = [p2.bku[s_banks[0].name], p2.bku[s_banks[1].name]]
                    sb_sem = [p2.sem(s_banks[0].name), p2.sem(s_banks[1].name)]

                    @p2.block.scalar
                    def _(se):
                        for kc in range(n_kc):
                            se.wait_ge(p2.sem("asmm"), asmm0 + kc + 1)
                            if kc == 0:
                                se.wait_ge(p2.sem(sm_bank.name), mb_u)
                            if kc >= 2:
                                se.wait_ge(p2.sem("astx"), astx0 + kc - 1)
                            se.activation(STx[kc % 2][:, :], s_banks[kc % 2].t[:, :],
                                          AF.Exp, bias=mhb[:, 0:1])
                            p2.inc(se.drain(), s_banks[kc % 2].name, 1)

                    @p2.block.vector
                    def _(ve):
                        for d in range(4):
                            kc = 4 * qt + d
                            ve.wait_ge(sb_sem[kc % 2], ub[kc % 2] + kc // 2 + 1)
                            p2.inc(ve.tensor_mul(STx[kc % 2][:, :], STx[kc % 2][:, :],
                                                 masksT[d][:]), "asev", 1)

                    @p2.block.tensor
                    def _(te):
                        p2.wait(te, "amh2", mh_n)

                        def emit_S(kc):
                            b = kc % 2
                            m = kc // 2
                            if ub[b] + m > 0:
                                te.wait_ge(sb_sem[b], ub[b] + m)
                            te.matmul(s_banks[b].t[:, :],
                                      K_sb[:, ch * n_tok + kc * 128: ch * n_tok + kc * 128 + 128],
                                      Qh[:, h * 512:(h + 1) * 512],
                                      start=True, stop=True).then_inc(p2.sem("asmm"), 1)

                        def emit_AV(kc):
                            b = kc % 2
                            if kc >= 4 * qt:
                                te.wait_ge(p2.sem("asev"), asev0 + (kc - 4 * qt) + 1)
                            else:
                                te.wait_ge(sb_sem[b], ub[b] + kc // 2 + 1)
                            if kc == 0:
                                av_bank.wait_free(te)
                            te.matmul(av_bank.t[0:65, :],
                                      V_sb[:, kc * 1040 + h * 65: kc * 1040 + h * 65 + 65],
                                      STx[b][:, :],
                                      start=(kc == 0),
                                      stop=(kc == n_kc - 1)).then_inc(p2.sem("astx"), 1)

                        for kc in range(n_kc):
                            emit_S(kc)
                            if kc >= 1:
                                emit_AV(kc - 1)
                        emit_AV(n_kc - 1)
                        p2.inc(te.drain(), "aavd", 1)

                    p2.cnt["asmm"] = asmm0 + n_kc
                    p2.cnt["astx"] = astx0 + n_kc
                    for b in range(2):
                        n_uses = (n_kc - b + 1) // 2
                        p2.bku[s_banks[b].name] = ub[b] + n_uses
                    av_n = p2.val("aavd")

                    @p2.block.vector
                    def _(ve):
                        p2.wait(ve, "aavd", av_n)
                        p2.inc(ve.reciprocal(rl[0:1, :], av_bank.t[64:65, 0:512]),
                               "arl", 1)

                    @p2.block.tensor
                    def _(te):
                        p2.wait(te, "arl")
                        sm_bank.wait_free(te)
                        p2.inc(te.matmul(sm_bank.t[0:64, 0:512], onesrow[0:1, 0:64], rl[0:1, :],
                                         start=True, stop=True), "albm", 1)

                    @p2.block.vector
                    def _(ve):
                        p2.wait(ve, "albm")
                        ev = ve.tensor_copy(lbc[:, :], sm_bank.t[0:64, 0:512])
                        sm_bank.release(ev)
                    lb_u = sm_bank.uses

                    @p2.block.vector
                    def _(ve):
                        ve.wait_ge(p2.sem(sm_bank.name), lb_u)
                        ev = ve.tensor_mul(y_sb[(h % 2) * 64:(h % 2) * 64 + 64, ch * 512:(ch + 1) * 512],
                                           av_bank.t[0:64, :], lbc[:, :])
                        av_bank.release(ev)

                p2.fence()

                for dh in range(2):
                    wap_rdy = load_wap(dh)
                    for j in range(4):
                        tc = qt * 4 + j
                        bank = s_banks[j % 2]

                        @p2.block.tensor
                        def _(te):
                            if j == 0:
                                p2.wait(te, "adwp", wap_rdy)
                            bank.wait_free(te)
                            for c in range(CD):
                                mm = te.matmul(bank.t[:, :],
                                               y_sb[:, c * 512 + j * 128: c * 512 + j * 128 + 128],
                                               wap[:, c * 512:(c + 1) * 512],
                                               start=(c == 0), stop=(c == CD - 1))
                            p2.inc(mm, "apmm", 1)
                            if j == 3:
                                p2.inc(te.drain(), "awapf", 1)

                        @p2.block.vector
                        def _(ve):
                            p2.wait(ve, "apmm")
                            xv = x_sb[:, tc * 1024 + dh * 512: tc * 1024 + dh * 512 + 512]
                            ad = ve.tensor_add(xv, xv, bank.t[:, :])
                            bank.release(ad)
                            ve.drain()

                p2.fence()

            pp.cnt.update(p2.cnt)

        p.cnt.update(pp.cnt)





def emit_attn1(p, x_sb, wqtp_d, wktp_d, wvt_d, cos_d, sin_d, consts, n_tok, tag):
    NTILE = n_tok // 512
    NT = n_tok // 128
    ident = consts["ident"]

    with contextlib.ExitStack() as st:
        pp = p.fresh(st)

        K_sb = sbuf(pp, "bK", [128, CD * n_tok], BF16)
        V_sb = sbuf(pp, "bV", [128, NT * 1024], BF16)
        xhat_fm = sbuf(pp, "bxf", [128, CD * 512], BF16)
        xhat = [sbuf(pp, f"bxt{i}", [128, 1024], BF16) for i in range(2)]
        raw = sbuf(pp, "braw", [128, 2 * 512], F32)
        tcos = sbuf(pp, "bco", [128, 512], F32)
        tsin = sbuf(pp, "bsi", [128, 512], F32)
        t1 = sbuf(pp, "bt1", [128, 512], F32)
        t2 = sbuf(pp, "bt2", [128, 512], F32)
        negm = sbuf(pp, "bnm", [128, 1], F32)
        lsum = sbuf(pp, "bl", [128, 1], F32)
        rl1 = sbuf(pp, "brl", [128, 1], F32)
        masksA = consts["masksA"]

        tr_banks = [psum_bank(pp, "btr", [128, 128], BF16)]
        s_banks = [psum_bank(pp, f"bs{i}", [128, 512], F32) for i in range(2)]
        av_bank = psum_bank(pp, "bav", [128, 512], F32)

        ln = LN(pp, "b", consts["eps"])
        s_tr = "btr"

        def load_tables(i, tile):
            @pp.block.sync
            def _(sy):
                if pp.val("btbf") > 0:
                    pp.wait(sy, "btbf")
                for dst, srcd in ((tcos, cos_d), (tsin, sin_d)):
                    i3 = srcd[i * 128:(i + 1) * 128, tile * 512:(tile + 1) * 512]
                    pp.inc(sy.dma_start(out=dst[:], in_=i3), "btb", 16)
            return pp.val("btb")

        def rope_pair(ppx, i, tile, w_sb, dst_sb, dst_col, tb_rdy, wname, w_rdy, xf_n):
            for half, ch in ((0, i), (1, i + 4)):
                bank = s_banks[half]

                @ppx.block.tensor
                def _(te):
                    if w_rdy is not None:
                        ppx.wait(te, wname, w_rdy)
                    ppx.wait(te, "xfdone", xf_n)
                    bank.wait_free(te)
                    for c in range(CD):
                        mm = te.matmul(bank.t[:, :],
                                       w_sb[:, c * 1024 + ch * 128: c * 1024 + ch * 128 + 128],
                                       xhat_fm[:, c * 512:(c + 1) * 512],
                                       start=(c == 0), stop=(c == CD - 1))
                    ppx.inc(mm, "brmm", 1)

                @ppx.block.vector
                def _(ve):
                    ppx.wait(ve, "brmm")
                    if half == 0 and ppx.val("btbf") > 0:
                        ppx.wait(ve, "btbf")
                    ev = ve.tensor_copy(raw[:, half * 512:(half + 1) * 512], bank.t[:, :])
                    bank.release(ev)
            r_us = [(s_banks[0], s_banks[0].uses), (s_banks[1], s_banks[1].uses)]

            @ppx.block.vector
            def _(ve):
                for b, u in r_us:
                    ve.wait_ge(ppx.sem(b.name), u)
                ppx.wait(ve, "btb", tb_rdy)
                r0 = raw[:, 0:512]
                r1 = raw[:, 512:1024]
                ve.tensor_mul(t1[:], r0, tcos[:])
                ve.drain()
                ve.tensor_mul(t2[:], r1, tsin[:])
                ve.drain()
                ve.tensor_sub(dst_sb[:, i * n_tok + dst_col: i * n_tok + dst_col + 512], t1[:], t2[:])
                ve.drain()
                ve.tensor_mul(t1[:], r1, tcos[:])
                ve.drain()
                ve.tensor_mul(t2[:], r0, tsin[:])
                ve.drain()
                ve.tensor_add(dst_sb[:, (i + 4) * n_tok + dst_col: (i + 4) * n_tok + dst_col + 512],
                              t1[:], t2[:])
                ppx.inc(ve.drain(), "btbf", 1)

        with contextlib.ExitStack() as st1:
            p1 = pp.fresh(st1)
            wk = sbuf(p1, "bwk", [128, CD * 1024], BF16)
            wv = sbuf(p1, "bwv", [128, CD * 1024], BF16)

            @p1.block.gpsimd
            def _(g):
                for w_sb, w_d in ((wk, wktp_d), (wv, wvt_d)):
                    o3 = w_sb[:].rearrange("p (c d) -> p c d", c=CD)
                    i3 = w_d.rearrange("(c p) d -> p c d", p=128)
                    p1.inc(g.dma_start(out=o3, in_=i3), "bdw1", 16)
            w1_rdy = p1.val("bdw1")

            for tile in range(NTILE):
                xf_n = emit_xhat_fm(p1, ln, x_sb, tile, xhat_fm, xhat, tr_banks, ident, s_tr)
                for i in range(4):
                    tb = load_tables(i, tile)
                    rope_pair(p1, i, tile, wk, K_sb, tile * 512, tb,
                              "bdw1", w1_rdy if (tile == 0 and i == 0) else None, xf_n)
                for j in range(4):
                    tc = tile * 4 + j
                    for dh in range(2):
                        bank = s_banks[dh]

                        @p1.block.tensor
                        def _(te):
                            bank.wait_free(te)
                            for c in range(CD):
                                mm = te.matmul(bank.t[:, :],
                                               xhat_fm[:, c * 512 + j * 128: c * 512 + j * 128 + 128],
                                               wv[:, c * 1024 + dh * 512: c * 1024 + dh * 512 + 512],
                                               start=(c == 0), stop=(c == CD - 1))
                            p1.inc(mm, "bvmm", 1)

                        @p1.block.vector
                        def _(ve):
                            p1.wait(ve, "bvmm")
                            ev = ve.tensor_copy(V_sb[:, tc * 1024 + dh * 512: tc * 1024 + dh * 512 + 512],
                                                bank.t[:, :])
                            bank.release(ev)
                pp.fence()
            pp.cnt.update(p1.cnt)

        with contextlib.ExitStack() as st2:
            p2 = pp.fresh(st2)
            wq = sbuf(p2, "bwq", [128, CD * 1024], BF16)
            Qrot = sbuf(p2, "bQ", [128, CD * 512], BF16)
            Sf = sbuf(p2, "bS", [128, n_tok], F32)
            STe = sbuf(p2, "bSe", [128, n_tok], BF16)
            STt = sbuf(p2, "bSt", [128, n_tok], BF16)

            @p2.block.gpsimd
            def _(g):
                p2.wait(g, "bdw1")
                p2.wait(g, "fence")
                o3 = wq[:].rearrange("p (c d) -> p c d", c=CD)
                i3 = wqtp_d.rearrange("(c p) d -> p c d", p=128)
                p2.inc(g.dma_start(out=o3, in_=i3), "bdw2", 16)
            w2_rdy = p2.val("bdw2")

            for qt in range(NTILE):
                xf_n = emit_xhat_fm(p2, ln, x_sb, qt, xhat_fm, xhat, tr_banks, ident, s_tr)
                for i in range(4):
                    tb = load_tables(i, qt)
                    for half, ch in ((0, i), (1, i + 4)):
                        bank = s_banks[half]

                        @p2.block.tensor
                        def _(te):
                            if qt == 0 and i == 0 and half == 0:
                                p2.wait(te, "bdw2", w2_rdy)
                            p2.wait(te, "xfdone", xf_n)
                            bank.wait_free(te)
                            for c in range(CD):
                                mm = te.matmul(bank.t[:, :],
                                               wq[:, c * 1024 + ch * 128: c * 1024 + ch * 128 + 128],
                                               xhat_fm[:, c * 512:(c + 1) * 512],
                                               start=(c == 0), stop=(c == CD - 1))
                            p2.inc(mm, "brmm", 1)

                        @p2.block.vector
                        def _(ve):
                            p2.wait(ve, "brmm")
                            if half == 0 and p2.val("btbf") > 0:
                                p2.wait(ve, "btbf")
                            ev = ve.tensor_copy(raw[:, half * 512:(half + 1) * 512], bank.t[:, :])
                            bank.release(ev)
                    r_us = [(s_banks[0], s_banks[0].uses), (s_banks[1], s_banks[1].uses)]

                    @p2.block.vector
                    def _(ve):
                        for b, u in r_us:
                            ve.wait_ge(p2.sem(b.name), u)
                        p2.wait(ve, "btb", tb)
                        r0 = raw[:, 0:512]
                        r1 = raw[:, 512:1024]
                        ve.tensor_mul(t1[:], r0, tcos[:])
                        ve.drain()
                        ve.tensor_mul(t2[:], r1, tsin[:])
                        ve.drain()
                        ve.tensor_sub(Qrot[:, i * 512:(i + 1) * 512], t1[:], t2[:])
                        ve.drain()
                        ve.tensor_mul(t1[:], r1, tcos[:])
                        ve.drain()
                        ve.tensor_mul(t2[:], r0, tsin[:])
                        ve.drain()
                        ve.tensor_add(Qrot[:, (i + 4) * 512:(i + 5) * 512], t1[:], t2[:])
                        p2.inc(ve.drain(), "btbf", 1)

                @p2.block.vector
                def _(ve):
                    p2.inc(ve.drain(), "bqr", 1)
                q_n = p2.val("bqr")

                for qq in range(4):
                    k_hi = qt * 512 + (qq + 1) * 128
                    n_k5 = (k_hi + 511) // 512
                    for kc in range(n_k5):
                        bank = s_banks[kc % 2]
                        diag = (kc == n_k5 - 1)

                        @p2.block.tensor
                        def _(te):
                            if kc == 0:
                                p2.wait(te, "bqr", q_n)
                            bank.wait_free(te)
                            for c in range(CD):
                                mm = te.matmul(bank.t[:, :],
                                               Qrot[:, c * 512 + qq * 128: c * 512 + qq * 128 + 128],
                                               K_sb[:, c * n_tok + kc * 512: c * n_tok + kc * 512 + 512],
                                               start=(c == 0), stop=(c == CD - 1))
                            p2.inc(mm, "bsmm", 1)

                        if diag:
                            dd = (qt * 512 + qq * 128 - kc * 512) // 128

                            @p2.block.vector
                            def _(ve):
                                p2.wait(ve, "bsmm")
                                if p2.val("bex") > 0:
                                    p2.wait(ve, "bex")
                                ad = ve.tensor_add(Sf[:, kc * 512:(kc + 1) * 512],
                                                   bank.t[:, :], masksA[dd][:])
                                bank.release(ad)
                        else:
                            @p2.block.vector
                            def _(ve):
                                p2.wait(ve, "bsmm")
                                ev = ve.tensor_copy(Sf[:, kc * 512:(kc + 1) * 512], bank.t[:, :])
                                bank.release(ev)
                    s_rel = [(b, b.uses) for b in s_banks]
                    k512 = n_k5 * 512

                    @p2.block.vector
                    def _(ve):
                        for b, u in s_rel:
                            ve.wait_ge(p2.sem(b.name), u)
                        ve.reduce_max(negm[:], Sf[:, 0:k512], axis=AX.X, negate=True)
                        ve.drain()
                        p2.inc(ve.tensor_scalar(negm[:], negm[:], 0.03125, None, ALU.mult),
                               "bnm", 1)

                    @p2.block.scalar
                    def _(se):
                        p2.wait(se, "bnm")
                        if p2.val("bstr") > 0:
                            p2.wait(se, "bstr")
                        se.activation(STe[:, 0:k512], Sf[:, 0:k512], AF.Exp,
                                      bias=negm[:, 0:1], scale=0.03125,
                                      accum_out=lsum[:, 0:1])
                        p2.inc(se.drain(), "bex", 1)
                    e_n = p2.val("bex")

                    for kc in range(k512 // 128):
                        bank = tr_banks[0]

                        @p2.block.tensor
                        def _(te):
                            p2.wait(te, "bex", e_n)
                            bank.wait_free(te)
                            p2.inc(te.transpose(bank.t[:, :], STe[:, kc * 128:(kc + 1) * 128],
                                                ident[:]), "bstr", 1)

                        @p2.block.vector
                        def _(ve):
                            p2.wait(ve, "bstr")
                            ev = ve.tensor_copy(STt[:, kc * 128:(kc + 1) * 128], bank.t[:, :])
                            bank.release(ev)
                    stt_u = (tr_banks[0], tr_banks[0].uses)

                    @p2.block.vector
                    def _(ve):
                        p2.wait(ve, "bex", e_n)
                        p2.inc(ve.reciprocal(rl1[:], lsum[:, 0:1]), "brc", 1)
                    rc_n = p2.val("brc")

                    for dh in range(2):
                        @p2.block.tensor
                        def _(te):
                            if dh == 0:
                                te.wait_ge(p2.sem(stt_u[0].name), stt_u[1])
                            av_bank.wait_free(te)
                            for kc in range(k512 // 128):
                                mm = te.matmul(av_bank.t[:, :],
                                               STt[:, kc * 128:(kc + 1) * 128],
                                               V_sb[:, kc * 1024 + dh * 512: kc * 1024 + dh * 512 + 512],
                                               start=(kc == 0), stop=(kc == k512 // 128 - 1))
                            p2.inc(mm, "bavm", 1)

                        @p2.block.vector
                        def _(ve):
                            p2.wait(ve, "bavm")
                            if dh == 0:
                                p2.wait(ve, "brc", rc_n)
                            tc = qt * 4 + qq
                            xv = x_sb[:, tc * 1024 + dh * 512: tc * 1024 + dh * 512 + 512]
                            stt = ve.scalar_tensor_tensor(xv, av_bank.t[:, :], rl1[:, 0:1],
                                                          xv, ALU.mult, ALU.add)
                            av_bank.release(stt)
                            ve.drain()

                p2.fence()

            pp.cnt.update(p2.cnt)

        p.cnt.update(pp.cnt)



def _seq_io(p, block, x_sb, x_d, y_d, s, NT, load, consts=None, ident_dram=None):
    if load:
        @block.sync
        def _(sy):
            if consts is not None and s == 0:
                p.inc(sy.dma_start(out=consts["ident"][:], in_=ident_dram), "dsync", 16)
            o3 = x_sb[:].rearrange("p (c d) -> p c d", c=NT)
            i3 = x_d[s].rearrange("(c p) d -> p c d", p=128)
            p.inc(sy.dma_start(out=o3, in_=i3), "dsync", 16)
        n = p.val("dsync")
        for engname in ("tensor", "vector", "scalar"):
            @getattr(block, engname)
            def _(eng):
                p.wait(eng, "dsync", n)
    else:
        @block.sync
        def _(sy):
            sy.wait_ge(p.sem("fence"), p.cnt["fence"])
            o3 = y_d[s].rearrange("(c p) d -> p c d", p=128)
            i3 = x_sb[:].rearrange("p (c d) -> p c d", c=NT)
            p.inc(sy.dma_start(out=o3, in_=i3), "dstore", 16)
            p.wait(sy, "dstore")


def build_ffn0_only(n_tok):
    nc = bass.Bass("TRN2", target_bir_lowering=False)
    NT = n_tok // 128
    x_d = nc.dram_tensor("x", [2, n_tok, D], F32, kind="ExternalInput").ap()
    wr_d = nc.dram_tensor("wrt", [D, D], F32, kind="ExternalInput").ap()
    wk_d = nc.dram_tensor("wkt", [D, D], F32, kind="ExternalInput").ap()
    wv_d = nc.dram_tensor("wvt", [D, D], F32, kind="ExternalInput").ap()
    mur_d = nc.dram_tensor("mur", [D], F32, kind="ExternalInput").ap()
    muk_d = nc.dram_tensor("muk", [D], F32, kind="ExternalInput").ap()
    y_d = nc.dram_tensor("y", [2, n_tok, D], F32, kind="ExternalOutput").ap()

    with contextlib.ExitStack() as stack:
        p = P(nc, stack)
        ident_dram = nc.inline_tensor(np.eye(128, dtype=ml_dtypes.bfloat16), "identc").ap()
        ident = sbuf(p, "ident", [128, 128], BF16)
        eps_t = sbuf(p, "epsc", [128, 1], F32)
        x_sb = sbuf(p, "xres", [128, NT * 1024], F32)
        consts = {"ident": ident, "eps": eps_t[:, 0:1]}

        for s in range(2):
            with nc.Block() as block:
                p.block = block
                if s == 0:
                    @block.vector
                    def _(ve):
                        ve.memset(eps_t[:], LN_EPS)
                        ve.drain()
                _seq_io(p, block, x_sb, x_d, y_d, s, NT, True, consts, ident_dram)
                emit_ffn0(p, x_sb, wr_d, wk_d, wv_d, mur_d, muk_d, consts,
                          n_tok, f"s{s}")
                _seq_io(p, block, x_sb, x_d, y_d, s, NT, False)
    return nc




def build_moe_only(n_tok):
    nc = bass.Bass("TRN2", target_bir_lowering=False)
    NT = n_tok // 128
    x_d = nc.dram_tensor("x", [2, n_tok, D], F32, kind="ExternalInput").ap()
    wg_d = nc.dram_tensor("wg", [D, 8], F32, kind="ExternalInput").ap()
    w1_d = nc.dram_tensor("w1", [8, D, 4096], F32, kind="ExternalInput").ap()
    b1r_d = nc.dram_tensor("b1r", [128, 256], F32, kind="ExternalInput").ap()
    w2_d = nc.dram_tensor("w2", [8, 4096, D], F32, kind="ExternalInput").ap()
    b2_d = nc.dram_tensor("b2", [8, D], F32, kind="ExternalInput").ap()
    y_d = nc.dram_tensor("y", [2, n_tok, D], F32, kind="ExternalOutput").ap()

    with contextlib.ExitStack() as stack:
        p = P(nc, stack)
        ident_dram = nc.inline_tensor(np.eye(128, dtype=ml_dtypes.bfloat16), "identc").ap()
        identf_dram = nc.inline_tensor(np.eye(128, dtype=np.float32), "identfc").ap()
        ident = sbuf(p, "ident", [128, 128], BF16)
        identf = sbuf(p, "identf", [128, 128], F32)
        eps_t = sbuf(p, "epsc", [128, 1], F32)
        x_sb = sbuf(p, "xres", [128, NT * 1024], F32)
        consts = {"ident": ident, "identf": identf, "eps": eps_t[:, 0:1]}

        for s in range(2):
            with nc.Block() as block:
                p.block = block
                if s == 0:
                    @block.vector
                    def _(ve):
                        ve.memset(eps_t[:], LN_EPS)
                        ve.drain()
                    @block.sync
                    def _(sy):
                        p.inc(sy.dma_start(out=identf[:], in_=identf_dram), "dsync", 16)
                _seq_io(p, block, x_sb, x_d, y_d, s, NT, True, consts, ident_dram)
                emit_moe(p, x_sb, wg_d, w1_d, b1r_d, w2_d, b2_d, consts,
                         n_tok, f"s{s}", cast_w=True)
                _seq_io(p, block, x_sb, x_d, y_d, s, NT, False)
    return nc




def make_attn0_consts(nc, p):
    op = np.zeros((128, 2), dtype=ml_dtypes.bfloat16)
    op[0:64, 0] = 1.0
    op[64:128, 1] = 1.0
    op_d = nc.inline_tensor(op, "onespair_c").ap()
    onespair = sbuf(p, "onespair", [128, 2], BF16)
    masks = []
    masks_d = []
    for d in range(4):
        m = np.zeros((128, 512), dtype=ml_dtypes.bfloat16)
        for pp_ in range(128):
            j0 = d * 128 + pp_
            if j0 < 512:
                m[pp_, j0:] = 1.0
        masks_d.append(nc.inline_tensor(m, f"maskT{d}_c").ap())
        masks.append(sbuf(p, f"maskT{d}", [128, 512], BF16))
    onesrow = sbuf(p, "onesrow", [1, 128], F32)
    return op_d, onespair, masks_d, masks, onesrow


def build_attn0_only(n_tok):
    nc = bass.Bass("TRN2", target_bir_lowering=False)
    NT = n_tok // 128
    x_d = nc.dram_tensor("x", [2, n_tok, D], F32, kind="ExternalInput").ap()
    wattn_d = nc.dram_tensor("wattn", [D, 3 * D], F32, kind="ExternalInput").ap()
    wproj_d = nc.dram_tensor("wproj", [D, D], F32, kind="ExternalInput").ap()
    y_d = nc.dram_tensor("y", [2, n_tok, D], F32, kind="ExternalOutput").ap()

    with contextlib.ExitStack() as stack:
        p = P(nc, stack)
        ident_dram = nc.inline_tensor(np.eye(128, dtype=ml_dtypes.bfloat16), "identc").ap()
        identf_dram = nc.inline_tensor(np.eye(128, dtype=np.float32), "identfc").ap()
        ident = sbuf(p, "ident", [128, 128], BF16)
        identf = sbuf(p, "identf", [128, 128], F32)
        eps_t = sbuf(p, "epsc", [128, 1], F32)
        x_sb = sbuf(p, "xres", [128, NT * 1024], F32)
        op_d, onespair, masks_d, masks, onesrow = make_attn0_consts(nc, p)
        consts = {"ident": ident, "identf": identf, "eps": eps_t[:, 0:1],
                  "onespair": onespair[:], "onesrow": onesrow[:],
                  "masksT": [m[:] for m in masks]}

        for s in range(2):
            with nc.Block() as block:
                p.block = block
                if s == 0:
                    @block.vector
                    def _(ve):
                        ve.memset(eps_t[:], LN_EPS)
                        ve.drain()
                        ve.memset(onesrow[:], 1.0)
                        ve.drain()
                    @block.sync
                    def _(sy):
                        p.inc(sy.dma_start(out=identf[:], in_=identf_dram), "dsync", 16)
                        p.inc(sy.dma_start(out=onespair[:], in_=op_d), "dsync", 16)
                        for m_sb, m_d in zip(masks, masks_d):
                            p.inc(sy.dma_start(out=m_sb[:], in_=m_d), "dsync", 16)
                _seq_io(p, block, x_sb, x_d, y_d, s, NT, True, consts, ident_dram)
                emit_attn0(p, x_sb, wattn_d, wproj_d, consts, n_tok, f"s{s}")
                _seq_io(p, block, x_sb, x_d, y_d, s, NT, False)
    return nc




def build_attn1_only(n_tok):
    nc = bass.Bass("TRN2", target_bir_lowering=False)
    NT = n_tok // 128
    x_d = nc.dram_tensor("x", [2, n_tok, D], F32, kind="ExternalInput").ap()
    wqtp_d = nc.dram_tensor("wqtp", [D, D], F32, kind="ExternalInput").ap()
    wktp_d = nc.dram_tensor("wktp", [D, D], F32, kind="ExternalInput").ap()
    wvt_d = nc.dram_tensor("wvt", [D, D], F32, kind="ExternalInput").ap()
    cos_d = nc.dram_tensor("ropec", [512, n_tok], F32, kind="ExternalInput").ap()
    sin_d = nc.dram_tensor("ropes", [512, n_tok], F32, kind="ExternalInput").ap()
    y_d = nc.dram_tensor("y", [2, n_tok, D], F32, kind="ExternalOutput").ap()

    with contextlib.ExitStack() as stack:
        p = P(nc, stack)
        ident_dram = nc.inline_tensor(np.eye(128, dtype=ml_dtypes.bfloat16), "identc").ap()
        ident = sbuf(p, "ident", [128, 128], BF16)
        eps_t = sbuf(p, "epsc", [128, 1], F32)
        x_sb = sbuf(p, "xres", [128, NT * 1024], F32)
        masksA_np = []
        masksA_d = []
        masksA = []
        for dd in range(4):
            m = np.zeros((128, 512), dtype=ml_dtypes.bfloat16)
            for pp_ in range(128):
                j0 = dd * 128 + pp_
                if j0 + 1 < 512:
                    m[pp_, j0 + 1:] = -1e30
            masksA_d.append(nc.inline_tensor(m, f"maskA{dd}_c").ap())
            masksA.append(sbuf(p, f"maskA{dd}", [128, 512], BF16))
        consts = {"ident": ident, "eps": eps_t[:, 0:1],
                  "masksA": [m[:] for m in masksA]}

        for s in range(2):
            with nc.Block() as block:
                p.block = block
                if s == 0:
                    @block.vector
                    def _(ve):
                        ve.memset(eps_t[:], LN_EPS)
                        ve.drain()
                    @block.sync
                    def _(sy):
                        for m_sb, m_d in zip(masksA, masksA_d):
                            p.inc(sy.dma_start(out=m_sb[:], in_=m_d), "dsync", 16)
                _seq_io(p, block, x_sb, x_d, y_d, s, NT, True, consts, ident_dram)
                emit_attn1(p, x_sb, wqtp_d, wktp_d, wvt_d, cos_d, sin_d, consts,
                           n_tok, f"s{s}")
                _seq_io(p, block, x_sb, x_d, y_d, s, NT, False)
    return nc




def build_full(n_tok):
    nc = bass.Bass("TRN2", target_bir_lowering=False)
    NT = n_tok // 128
    x_d = nc.dram_tensor("x", [2, n_tok, D], F32, kind="ExternalInput").ap()
    wrt_d = nc.dram_tensor("wrt", [D, D], F32, kind="ExternalInput").ap()
    wkt_d = nc.dram_tensor("wkt", [D, D], F32, kind="ExternalInput").ap()
    wvt_d = nc.dram_tensor("wvt", [D, D], F32, kind="ExternalInput").ap()
    mur_d = nc.dram_tensor("mur", [D], F32, kind="ExternalInput").ap()
    muk_d = nc.dram_tensor("muk", [D], F32, kind="ExternalInput").ap()
    wattn_d = nc.dram_tensor("wattn", [D, 3 * D], F32, kind="ExternalInput").ap()
    wproj_d = nc.dram_tensor("wproj", [D, D], F32, kind="ExternalInput").ap()
    wg_d = nc.dram_tensor("wg", [D, 8], F32, kind="ExternalInput").ap()
    w1_d = nc.dram_tensor("w1", [8, D, 4096], F32, kind="ExternalInput").ap()
    b1r_d = nc.dram_tensor("b1r", [128, 256], F32, kind="ExternalInput").ap()
    w2_d = nc.dram_tensor("w2", [8, 4096, D], F32, kind="ExternalInput").ap()
    b2_d = nc.dram_tensor("b2", [8, D], F32, kind="ExternalInput").ap()
    wqtp_d = nc.dram_tensor("wqtp", [D, D], F32, kind="ExternalInput").ap()
    wktp_d = nc.dram_tensor("wktp", [D, D], F32, kind="ExternalInput").ap()
    wvt1_d = nc.dram_tensor("wvt1", [D, D], F32, kind="ExternalInput").ap()
    cos_d = nc.dram_tensor("ropec", [512, n_tok], F32, kind="ExternalInput").ap()
    sin_d = nc.dram_tensor("ropes", [512, n_tok], F32, kind="ExternalInput").ap()
    y_d = nc.dram_tensor("y", [2, n_tok, D], F32, kind="ExternalOutput").ap()

    with contextlib.ExitStack() as stack:
        p = P(nc, stack)
        ident_dram = nc.inline_tensor(np.eye(128, dtype=ml_dtypes.bfloat16), "identc").ap()
        identf_dram = nc.inline_tensor(np.eye(128, dtype=np.float32), "identfc").ap()
        ident = sbuf(p, "ident", [128, 128], BF16)
        identf = sbuf(p, "identf", [128, 128], F32)
        eps_t = sbuf(p, "epsc", [128, 1], F32)
        x_sb = sbuf(p, "xres", [128, NT * 1024], F32)
        op_d, onespair, masksT_d, masksT, onesrow = make_attn0_consts(nc, p)
        masksA_d = []
        masksA = []
        for dd in range(4):
            m = np.zeros((128, 512), dtype=ml_dtypes.bfloat16)
            for pp_ in range(128):
                j0 = dd * 128 + pp_
                if j0 + 1 < 512:
                    m[pp_, j0 + 1:] = -1e30
            masksA_d.append(nc.inline_tensor(m, f"maskA{dd}_c").ap())
            masksA.append(sbuf(p, f"maskA{dd}", [128, 512], BF16))
        consts = {"ident": ident, "identf": identf, "eps": eps_t[:, 0:1],
                  "onespair": onespair[:], "onesrow": onesrow[:],
                  "masksT": [m[:] for m in masksT],
                  "masksA": [m[:] for m in masksA]}

        for s in range(2):
            with nc.Block() as block:
                p.block = block
                if s == 0:
                    @block.vector
                    def _(ve):
                        ve.memset(eps_t[:], LN_EPS)
                        ve.drain()
                        ve.memset(onesrow[:], 1.0)
                        ve.drain()

                    @block.sync
                    def _(sy):
                        p.inc(sy.dma_start(out=identf[:], in_=identf_dram), "dsync", 16)
                        p.inc(sy.dma_start(out=onespair[:], in_=op_d), "dsync", 16)
                        for m_sb, m_d in zip(masksT, masksT_d):
                            p.inc(sy.dma_start(out=m_sb[:], in_=m_d), "dsync", 16)
                        for m_sb, m_d in zip(masksA, masksA_d):
                            p.inc(sy.dma_start(out=m_sb[:], in_=m_d), "dsync", 16)
                _seq_io(p, block, x_sb, x_d, y_d, s, NT, True, consts, ident_dram)
                emit_ffn0(p, x_sb, wrt_d, wkt_d, wvt_d, mur_d, muk_d, consts,
                          n_tok, f"s{s}a")
            with nc.Block() as block:
                p.block = block
                emit_attn0(p, x_sb, wattn_d, wproj_d, consts, n_tok, f"s{s}")
            with nc.Block() as block:
                p.block = block
                emit_ffn0(p, x_sb, wrt_d, wkt_d, wvt_d, mur_d, muk_d, consts,
                          n_tok, f"s{s}b")
            with nc.Block() as block:
                p.block = block
                emit_moe(p, x_sb, wg_d, w1_d, b1r_d, w2_d, b2_d, consts,
                         n_tok, f"s{s}a", cast_w=True)
            with nc.Block() as block:
                p.block = block
                emit_attn1(p, x_sb, wqtp_d, wktp_d, wvt1_d, cos_d, sin_d, consts,
                           n_tok, f"s{s}")
            with nc.Block() as block:
                p.block = block
                emit_moe(p, x_sb, wg_d, w1_d, b1r_d, w2_d, b2_d, consts,
                         n_tok, f"s{s}b", cast_w=True)
                _seq_io(p, block, x_sb, x_d, y_d, s, NT, False)
    return nc


def _rope_tables_host(n_tok):
    pos = np.arange(n_tok, dtype=np.float64)[:, None]
    inv = 1.0 / 10000.0 ** (np.arange(0, D, 2, dtype=np.float64) / D)
    fr = pos * inv
    cos = np.ascontiguousarray(np.cos(fr).T.astype(np.float32))
    sin = np.ascontiguousarray(np.sin(fr).T.astype(np.float32))
    return cos, sin


def kernel(**inputs):
    n_tok = 2048
    n_cores = 8
    x = np.ascontiguousarray(np.asarray(inputs["L0_attn_x"], dtype=np.float32))

    wrt = np.ascontiguousarray(np.asarray(inputs["L0_ffn_W_r"]).T)
    wkt = np.ascontiguousarray(np.asarray(inputs["L0_ffn_W_k"]).T)
    wvt = np.ascontiguousarray(np.asarray(inputs["L0_ffn_W_v"]).T)
    mur = np.ascontiguousarray(np.asarray(inputs["L0_ffn_mu_r"]))
    muk = np.ascontiguousarray(np.asarray(inputs["L0_ffn_mu_k"]))
    wattn = np.ascontiguousarray(np.asarray(inputs["L0_attn_w_attn"]))
    wproj = np.ascontiguousarray(np.asarray(inputs["L0_attn_w_proj"]))
    wg = np.ascontiguousarray(np.asarray(inputs["L1_ffn_W_gate"]))
    w1 = np.ascontiguousarray(np.asarray(inputs["L1_ffn_W1"]))
    b1 = np.asarray(inputs["L1_ffn_b1"])
    b1r = np.ascontiguousarray(b1.reshape(8, 32, 128).transpose(2, 0, 1).reshape(128, 256))
    w2 = np.ascontiguousarray(np.asarray(inputs["L1_ffn_W2"]))
    b2 = np.ascontiguousarray(np.asarray(inputs["L1_ffn_b2"]))
    perm = np.concatenate([np.arange(0, D, 2), np.arange(1, D, 2)])
    wqtp = np.ascontiguousarray(np.asarray(inputs["L1_attn_wq"]).T[:, perm])
    wktp = np.ascontiguousarray(np.asarray(inputs["L1_attn_wk"]).T[:, perm])
    wvt1 = np.ascontiguousarray(np.asarray(inputs["L1_attn_wv"]).T)
    cosf, sinf = _rope_tables_host(n_tok)

    shards = [np.ascontiguousarray(x[2 * c: 2 * c + 2]) for c in range(n_cores)]

    nc = build_full(n_tok)
    weights = {"wrt": wrt, "wkt": wkt, "wvt": wvt, "mur": mur, "muk": muk,
               "wattn": wattn, "wproj": wproj,
               "wg": wg, "w1": w1, "b1r": b1r, "w2": w2, "b2": b2,
               "wqtp": wqtp, "wktp": wktp, "wvt1": wvt1,
               "ropec": cosf, "ropes": sinf}
    in_maps = [dict(weights, x=shards[c]) for c in range(n_cores)]
    res = run_bass_kernel_spmd(nc, in_maps, core_ids=list(range(n_cores)))
    return np.concatenate([np.asarray(r["y"]) for r in res.results], axis=0)

